# version 06
# baseline (speedup 1.0000x reference)
import contextlib
import math
import numpy as np
import ml_dtypes

import concourse.bass as bass
import concourse.mybir as mybir
from concourse.bass_utils import run_bass_kernel_spmd

F32 = mybir.dt.float32
BF16 = mybir.dt.bfloat16
AF = mybir.ActivationFunctionType
ALU = mybir.AluOpType
AX = mybir.AxisListType

NCORES = 8
S = 4096
D = 1024
DC = D // 128
ST = S // 128
NEG_MARGIN = 50.0


ENGINES = ("te", "act", "dve", "pool", "sp")
ENG_ATTR = {"te": "tensor", "act": "scalar", "dve": "vector", "pool": "gpsimd", "sp": "sync"}


class Region:
    __slots__ = ("name", "writer", "readers")

    def __init__(self, name):
        self.name = name
        self.writer = None
        self.readers = {}


class Prog:
    def __init__(self, nc, stack, same_engine_sync=True):
        self.nc = nc
        self.stack = stack
        self.same_engine_sync = same_engine_sync
        self.sems = {}
        self.count = {}
        self.known = {e: {} for e in ENGINES}
        self.items = {e: [] for e in ENGINES}
        self.regions = {}
        for e in ENGINES:
            self.sems[e] = stack.enter_context(nc.semaphore(f"sem_{e}"))
            self.count[e] = 0
        self.n_instr = 0

    def region(self, name):
        if name not in self.regions:
            self.regions[name] = Region(name)
        return self.regions[name]

    def dma_counter(self, name):
        cname = "dma_" + name
        if cname not in self.sems:
            self.sems[cname] = self.stack.enter_context(self.nc.semaphore(cname))
            self.count[cname] = 0
        return cname

    def _deps_for(self, engine, reads, writes, is_dma):
        deps = {}

        def add(cv):
            if cv is None:
                return
            c, v = cv
            if deps.get(c, 0) < v:
                deps[c] = v

        for r in reads:
            add(r.writer)
        for r in writes:
            add(r.writer)
            for c, v in r.readers.items():
                add((c, v))
        known = self.known[engine]
        out = {}
        for c, v in deps.items():
            if c == engine and not is_dma:
                if not self.same_engine_sync or engine == "te":
                    continue
            if c.startswith("dma_"):
                v = max(v, self.count[c])
            if known.get(c, 0) >= v:
                continue
            out[c] = v
        return out

    def _note(self, counter, value, reads, writes):
        for r in reads:
            if r.readers.get(counter, 0) < value:
                r.readers[counter] = value
        for r in writes:
            r.writer = (counter, value)
            r.readers = {}

    def _regs(self, lst):
        return [self.region(r) if isinstance(r, str) else r for r in lst]

    def emit(self, engine, fn, reads=(), writes=(), tag=""):
        reads = self._regs(reads)
        writes = self._regs(writes)
        deps = self._deps_for(engine, reads, writes, False)
        for c, v in deps.items():
            self.known[engine][c] = v
        self.count[engine] += 1
        val = self.count[engine]
        self._note(engine, val, reads, writes)
        self.items[engine].append(("op", fn, deps, None, tag))
        self.n_instr += 1
        return (engine, val)

    def dma(self, queue_engine, sem_name, fn, reads=(), writes=(), tag=""):
        reads = self._regs(reads)
        writes = self._regs(writes)
        key = writes[0].name if writes else sem_name
        cname = self.dma_counter(queue_engine + "_" + key)
        deps = self._deps_for(queue_engine, reads, writes, True)
        for c, v in deps.items():
            self.known[queue_engine][c] = v
        self.count[cname] += 16
        val = self.count[cname]
        self._note(cname, val, reads, writes)
        self.items[queue_engine].append(("dma", fn, deps, cname, tag))
        self.n_instr += 1
        return (cname, val)

    def barrier(self):
        snapshot = dict(self.count)
        for e in ENGINES:
            deps = {}
            for c, v in snapshot.items():
                if v > 0 and c != e and self.known[e].get(c, 0) < v:
                    deps[c] = v
                    self.known[e][c] = v
            if deps:
                self.items[e].append(("wait", None, deps, None, "barrier"))

    def wait_all_dmas(self, engine):
        for cname, v in self.count.items():
            if cname.startswith("dma_") and v > 0:
                if self.known[engine].get(cname, 0) < v:
                    self.known[engine][cname] = v
                    self.items[engine].append(("wait", None, {cname: v}, None, "final"))

    def finalize_into_block(self, block):
        sems = self.sems

        def make_replay(engine):
            items = self.items[engine]
            my_sem = sems[engine]

            def replay(eng_obj):
                for kind, fn, deps, cname, tag in items:
                    if kind == "wait":
                        for c, v in deps.items():
                            eng_obj.wait_ge(sems[c], v)
                        continue
                    for c, v in deps.items():
                        eng_obj.wait_ge(sems[c], v)
                    inst = fn(eng_obj)
                    if tag:
                        inst.annotate(tag)
                    if kind == "op":
                        inst.then_inc(my_sem, 1)
                    else:
                        inst.then_inc(sems[cname], 16)
            return replay

        for engine in ENGINES:
            if self.items[engine]:
                getattr(block, ENG_ATTR[engine])(make_replay(engine))



def to_bf16(x):
    return np.asarray(x).astype(ml_dtypes.bfloat16)



class MK:

    def __init__(self, nc, prog):
        self.nc = nc
        self.p = prog
        self.hbm = {}
        self.sb = {}
        self.ps = {}
        self.host_inputs = {}

    def dram_in(self, name, shape, dtype=F32):
        t = self.nc.dram_tensor(name, list(shape), dtype, kind="ExternalInput")
        self.hbm[name] = t.ap()
        return self.hbm[name]

    def dram_out(self, name, shape, dtype=F32):
        t = self.nc.dram_tensor(name, list(shape), dtype, kind="ExternalOutput")
        self.hbm[name] = t.ap()
        return self.hbm[name]

    def dram_tmp(self, name, shape, dtype=F32):
        t = self.nc.dram_tensor(name, list(shape), dtype, kind="Internal")
        self.hbm[name] = t.ap()
        return self.hbm[name]

    def const_in(self, name, arr):
        dt = BF16 if arr.dtype == ml_dtypes.bfloat16 else F32
        self.dram_in(name, arr.shape, dt)
        self.host_inputs[name] = arr
        return self.hbm[name]

    def sbuf(self, name, shape, dtype):
        h = self.p.stack.enter_context(self.nc.sbuf_tensor(name, list(shape), dtype))
        self.sb[name] = h
        return h

    def psum(self, name, shape, dtype=F32):
        h = self.p.stack.enter_context(self.nc.psum_tensor(name, list(shape), dtype))
        self.ps[name] = h
        return h


def alloc_main(mk):
    mk.sbuf("XBF", [128, DC * S], BF16)
    mk.sbuf("XF", [128, 4 * 512], F32)
    mk.sbuf("BRS", [128, S], F32)
    mk.sbuf("ROW0", [1, S], BF16)
    mk.sbuf("ROWC", [1, S], BF16)
    mk.sbuf("ROWP", [1, 192], BF16)
    mk.sbuf("MISC", [1, 16], F32)
    mk.sbuf("RST", [128, ST], F32)
    mk.sbuf("ROPE", [64, 2 * S], BF16)
    mk.sbuf("CMASK", [128, 896], BF16)
    mk.sbuf("RMASK", [128, 896], BF16)
    mk.sbuf("CONSTS", [1, 16], F32)
    mk.sbuf("EPSC", [128, 1], F32)
    mk.sbuf("ONES1", [1, 512], BF16)
    mk.sbuf("ONESF", [1, 128], F32)
    mk.sbuf("IDENTF", [128, 128], F32)
    mk.sbuf("BIASC", [128, 40], F32)
    mk.sbuf("BROW", [1, 1024], BF16)
    mk.sbuf("BVROW", [1, 1024], BF16)
    mk.sbuf("ONESC", [128, 1], BF16)
    mk.sbuf("IDENT", [128, 128], BF16)
    ar = mk.sbuf("AR", [128, 35968], BF16)
    cols = {"QB": 4096, "KB": 4096, "SQ": 8192, "VB": ST * 65, "ATB": 1024,
            "OTB": 4096, "RB": 1024, "WQKV": 3072, "WOALL": 8192}
    off = 0
    for name, w in cols.items():
        mk.sb[name] = ar[:, off:off + w]
        off += w
    assert off <= 35968, off
    mk.ar_used = off
    mk.psum("PA", [128, 1024])
    mk.psum("PB", [128, 1024])
    mk.psum("PC", [128, 1024])
    mk.psum("PD", [128, 512])
    mk.psum("PT", [128, 512], BF16)


def load_constants(mk):
    p = mk.p
    consts = np.zeros((1, 16), np.float32)
    consts[0, 0] = 1e-5
    mk.const_in("c_consts", consts)
    p.dma("sp", "const", lambda e: e.dma_start(out=mk.sb["CONSTS"][:], in_=mk.hbm["c_consts"][:, :]),
          writes=["CONSTS"], tag="ld_consts")
    mk.const_in("c_epsc", np.full((128, 1), 1e-5, np.float32))
    p.dma("sp", "const2", lambda e: e.dma_start(out=mk.sb["EPSC"][:], in_=mk.hbm["c_epsc"][:, :]),
          writes=["EPSC"], tag="ld_epsc")
    mk.const_in("c_ident", np.eye(128, dtype=np.float32))
    mk.const_in("c_ones1", np.ones((1, 512), np.float32))
    mk.const_in("c_onesf", np.ones((1, 128), np.float32))
    mk.const_in("c_identf", np.eye(128, dtype=np.float32))
    p.dma("sp", "onesf", lambda e: e.dma_start(out=mk.sb["ONESF"][:], in_=mk.hbm["c_onesf"][:, :]),
          writes=["ONESF"], tag="ld_onesf")
    p.dma("sp", "identf", lambda e: e.dma_start(out=mk.sb["IDENTF"][:], in_=mk.hbm["c_identf"][:, :]),
          writes=["IDENTF"], tag="ld_identf")
    mk.const_in("c_onesc", np.ones((128, 1), np.float32))
    hd = 64
    inv = 1.0 / 10000.0 ** (np.arange(0, hd, 2, dtype=np.float64) / hd)
    t = np.arange(S, dtype=np.float64)
    ang = t[None, :] * inv[:, None]
    cos = np.cos(ang)
    sin = np.sin(ang)
    cost = np.repeat(cos, 2, axis=0).astype(np.float32)
    ssin = np.empty((hd, S), np.float64)
    ssin[0::2] = -sin
    ssin[1::2] = sin
    mk.const_in("c_rope_cos", cost)
    mk.const_in("c_rope_ssin", ssin.astype(np.float32))
    p.dma("pool", "const", lambda e: e.dma_start(out=mk.sb["IDENT"][:], in_=mk.hbm["c_ident"][:, :]),
          writes=["IDENT"], tag="ld_ident")
    p.dma("pool", "const", lambda e: e.dma_start(out=mk.sb["ONES1"][:], in_=mk.hbm["c_ones1"][:, :]),
          writes=["ONES1"], tag="ld_ones1")
    p.dma("pool", "const", lambda e: e.dma_start(out=mk.sb["ONESC"][:], in_=mk.hbm["c_onesc"][:, :]),
          writes=["ONESC"], tag="ld_onesc")
    p.dma("pool", "const", lambda e: e.dma_start(out=mk.sb["ROPE"][:, 0:S], in_=mk.hbm["c_rope_cos"][:, :]),
          writes=["ROPE"], tag="ld_cos")
    p.dma("pool", "const", lambda e: e.dma_start(out=mk.sb["ROPE"][:, S:2 * S], in_=mk.hbm["c_rope_ssin"][:, :]),
          writes=["ROPE"], tag="ld_ssin")
    k = np.arange(128)[:, None]
    u = np.arange(896)[None, :] - 384
    cm = (k <= u).astype(np.float32)
    mk.const_in("c_cmask", cm)
    p.dma("pool", "const", lambda e: e.dma_start(out=mk.sb["CMASK"][:], in_=mk.hbm["c_cmask"][:, :]),
          writes=["CMASK"], tag="ld_cmask")
    rm = (k >= u).astype(np.float32)
    mk.const_in("c_rmask", rm)
    p.dma("pool", "const", lambda e: e.dma_start(out=mk.sb["RMASK"][:], in_=mk.hbm["c_rmask"][:, :]),
          writes=["RMASK"], tag="ld_rmask")


def load_shadow(mk, master):
    p = mk.p
    xbf = mk.sb["XBF"]
    m = mk.hbm[master]
    for c in range(DC):
        p.dma("pool", "xbf", lambda e, c=c: e.dma_start(
            out=xbf[:, c * S:(c + 1) * S], in_=m[c * 128:(c + 1) * 128, :]),
            reads=[f"M:{master}"], writes=[f"xbf{c}"], tag=f"ld_shadow{c}")



def ln_stats(mk, tag, shifted=False, want_sd=None):
    p = mk.p
    xbf, sq, xf = mk.sb["XBF"], mk.sb["SQ"], mk.sb["XF"]
    row0 = mk.sb["ROW0"]
    brs, rst = mk.sb["BRS"], mk.sb["RST"]
    onesc, onesf, identf = mk.sb["ONESC"], mk.sb["ONESF"], mk.sb["IDENTF"]
    pa, pb, pc, pd = mk.ps["PA"], mk.ps["PB"], mk.ps["PC"], mk.ps["PD"]

    for gp in range(S // 1024):
        g0 = gp * 1024
        for c in range(DC):
            buf = (gp * DC + c) % 2
            if shifted:
                rhs0, ofs0 = _l0f_shift_rhs(xbf, c, g0, g0 + 1024)
            else:
                rhs0, ofs0 = xbf[:, c * S + g0: c * S + g0 + 1024], 0
            p.emit("act", lambda e, rhs0=rhs0, ofs0=ofs0, buf=buf: e.activation(
                out=sq[:, buf * S + ofs0: buf * S + 1024], in_=rhs0, func=AF.Square),
                reads=[f"xbf{c}"], writes=[f"sq{buf}"], tag=f"{tag}lnsq")
            if ofs0:
                p.emit("dve", lambda e, buf=buf: e.memset(sq[:, buf * S: buf * S + 1], 0.0),
                       reads=[], writes=[f"sq{buf}"], tag=f"{tag}lnz")
            for half in range(2):
                lo = g0 + half * 512
                if shifted:
                    rhs2, ofs2 = _l0f_shift_rhs(xbf, c, lo, lo + 512)
                else:
                    rhs2, ofs2 = xbf[:, c * S + lo: c * S + lo + 512], 0
                p.emit("te", lambda e, rhs2=rhs2, ofs2=ofs2, c=c, half=half: e.matmul(
                    pa[0:1, half * 512 + ofs2: half * 512 + 512], onesc[:, :], rhs2,
                    start=(c == 0), stop=(c == DC - 1)),
                    reads=[f"xbf{c}", "ONESC"], writes=[f"PA{half}"], tag=f"{tag}lnsum")
                p.emit("te", lambda e, c=c, buf=buf, half=half: e.matmul(
                    pb[0:1, half * 512:half * 512 + 512], onesc[:, :],
                    sq[:, buf * S + half * 512: buf * S + half * 512 + 512],
                    start=(c == 0), stop=(c == DC - 1)),
                    reads=[f"sq{buf}", "ONESC"], writes=[f"PB{half}"], tag=f"{tag}lnsumsq")
        p.emit("act", lambda e, g0=g0: e.activation(
            out=row0[0:1, g0:g0 + 1024], in_=pa[0:1, 0:1024], func=AF.Identity, scale=1.0 / D),
            reads=["PA0", "PA1"], writes=["row0"], tag=f"{tag}lnmu")
        p.emit("act", lambda e: e.activation(
            out=xf[0:1, 0:1024], in_=pa[0:1, 0:1024], func=AF.Square, scale=1.0 / D),
            reads=["PA0", "PA1"], writes=["xf0", "xf1"], tag=f"{tag}lnmusq")
        p.emit("dve", lambda e: e.scalar_tensor_tensor(
            xf[0:1, 0:1024], pb[0:1, 0:1024], 1.0 / D, xf[0:1, 0:1024],
            ALU.mult, ALU.subtract),
            reads=["PB0", "PB1", "xf0", "xf1"], writes=["xf0", "xf1"], tag=f"{tag}lnvar")
        p.emit("act", lambda e: e.activation(out=xf[0:1, 0:1024], in_=xf[0:1, 0:1024], func=AF.Sqrt,
                                             bias=mk.sb["CONSTS"][0:1, 0:1]),
               reads=["xf0", "xf1", "CONSTS"], writes=["xf0", "xf1"], tag=f"{tag}sd")
        if want_sd is not None:
            p.emit("dve", lambda e, g0=g0: e.tensor_copy(want_sd[0:1, g0:g0 + 1024], xf[0:1, 0:1024]),
                   reads=["xf0", "xf1"], writes=["l0sdr"], tag=f"{tag}sdrow")
        p.emit("dve", lambda e: e.reciprocal(xf[0:1, 0:1024], xf[0:1, 0:1024]),
               reads=["xf0", "xf1"], writes=["xf0", "xf1"], tag=f"{tag}rsig")
        for half in range(2):
            bank = half % 2
            p.emit("te", lambda e, g0=g0, half=half, bank=bank: e.matmul(
                pc[:, bank * 512:bank * 512 + 512], onesf[0:1, 0:128], xf[0:1, half * 512:half * 512 + 512],
                start=True, stop=True),
                reads=["xf0", "xf1", "ONESF"], writes=[f"PC{bank}"], tag=f"{tag}brsmm")
            p.emit("dve", lambda e, g0=g0, half=half, bank=bank: e.tensor_copy(
                brs[:, g0 + half * 512: g0 + half * 512 + 512], pc[:, bank * 512:bank * 512 + 512]),
                reads=[f"PC{bank}"], writes=["BRS"], tag=f"{tag}brsev")
    for tt in range(ST):
        p.emit("te", lambda e, tt=tt: e.transpose(
            pd[:, 0:128], brs[:, tt * 128:(tt + 1) * 128], identf[:, :]),
            reads=["BRS", "IDENTF"], writes=["PD0"], tag=f"{tag}rstT")
        p.emit("dve", lambda e, tt=tt: e.tensor_copy(rst[:, tt:tt + 1], pd[:, 0:1]),
            reads=["PD0"], writes=["RST"], tag=f"{tag}rstev")



def rope_apply(mk, buf_name, tag):
    p = mk.p
    buf = mk.sb[buf_name]
    rope = mk.sb["ROPE"]
    rsc = mk.sb["SQ"]
    mask = [i + 1 if i % 2 == 0 else i - 1 for i in range(32)]
    p.emit("dve", lambda e: e.stream_shuffle(rsc[0:64, 0:S], buf[0:64, :], mask),
           reads=[buf_name], writes=["sq0"], tag=f"{tag}shuf")
    p.emit("dve", lambda e: e.tensor_mul(rsc[0:64, 0:S], rsc[0:64, 0:S], rope[:, S:2 * S]),
           reads=["sq0", "ROPE"], writes=["sq0"], tag=f"{tag}ssin")
    p.emit("dve", lambda e: e.tensor_mul(rsc[0:64, S:2 * S], buf[0:64, :], rope[:, 0:S]),
           reads=[buf_name, "ROPE"], writes=["sq1"], tag=f"{tag}cos")
    p.emit("dve", lambda e: e.tensor_add(buf[0:64, :], rsc[0:64, 0:S], rsc[0:64, S:2 * S]),
           reads=["sq0", "sq1"], writes=[buf_name], tag=f"{tag}add")


def attn_std(mk, li, wqkv_name, wo_name, ot_name, nh, scale, rope, causal,
             master_in, master_out, tag, debug=None):
    p = mk.p
    sb, ps, hbm = mk.sb, mk.ps, mk.hbm
    xbf, sq = sb["XBF"], sb["SQ"]
    misc = sb["MISC"]
    row0, rowc, rowp = sb["ROW0"], sb["ROWC"], sb["ROWP"]
    onesf = sb["ONESF"]
    rbf = sb["RB"].bitcast(F32)
    qb, kb, vb, atb, otb, rb = sb["QB"], sb["KB"], sb["VB"], sb["ATB"], sb["OTB"], sb["RB"]
    xf = sb["XF"]
    wqkv, woall, ots = sb["WQKV"], sb["WOALL"], sb["OTB"]
    brs, rst = sb["BRS"], sb["RST"]
    onesc, ones1 = sb["ONESC"], sb["ONES1"]
    pa, pb, pc, pd = ps["PA"], ps["PB"], ps["PC"], ps["PD"]
    xf = sb["XF"]
    w_h = hbm[wqkv_name]
    wo_h = hbm[wo_name]
    ot = hbm[ot_name]
    NQG = S // 512

    ln_stats(mk, tag)

    p.emit("dve", lambda e: e.memset(kb[64:65, :], 1.0), writes=["kb_ones"], tag=f"{tag}kones")
    p.emit("dve", lambda e: e.memset(
        vb[:].rearrange("p (t c) -> p t c", c=65)[:, :, 64:65], 1.0),
        writes=["vb_ones"], tag=f"{tag}vones")

    for j in range(nh // 2):
        for r in range(2):
            p.dma("pool", "woall", lambda e, j=j, r=r: e.dma_start(
                out=woall[r * 64:(r + 1) * 64, j * 1024:(j + 1) * 1024],
                in_=wo_h[2 * j + r]),
                reads=[f"M:{wo_name}"], writes=["WOALL"], tag=f"{tag}ldwo{j}_{r}")

    nh_run = nh if debug in (None, "full16") else (2 if debug == "phaseB1" else 1)
    for h in range(nh_run):
        hb = h % 2
        wreg = f"wqkv{hb}"
        for c in range(DC):
            p.dma("pool", "wqkv", lambda e, h=h, hb=hb, c=c: e.dma_start(
                out=wqkv[:, hb * 1536 + c * 192: hb * 1536 + (c + 1) * 192],
                in_=w_h[h][c * 128:(c + 1) * 128, :]),
                reads=[f"M:{wqkv_name}"], writes=[wreg], tag=f"{tag}ldw{h}_{c}")
        for c in range(DC):
            p.emit("te", lambda e, c=c, hb=hb: e.matmul(
                pd[0:1, 0:192], onesc[:, :], wqkv[:, hb * 1536 + c * 192: hb * 1536 + (c + 1) * 192],
                start=(c == 0), stop=(c == DC - 1)),
                reads=[wreg, "ONESC"], writes=["PD0"], tag=f"{tag}cs{h}")
        p.emit("act", lambda e: e.activation(out=rowp[0:1, 0:192], in_=pd[0:1, 0:192],
                                             func=AF.Identity, scale=-1.0),
               reads=["PD0"], writes=["rowp"], tag=f"{tag}csev{h}")

        for g in range(NQG):
            gs = slice(g * 512, (g + 1) * 512)
            bank = g % 2
            pa_reg, pb_reg = f"PA{bank}", f"PB{bank}"
            p.emit("te", lambda e, gs=gs, bank=bank: e.matmul(
                pa[0:64, bank * 512:bank * 512 + 512], rowp[0:1, 0:64], row0[0:1, gs],
                start=True, stop=False),
                reads=["rowp", "row0"], writes=[pa_reg], tag=f"{tag}qseed{h}")
            p.emit("te", lambda e, gs=gs, bank=bank: e.matmul(
                pb[0:64, bank * 512:bank * 512 + 512], rowp[0:1, 64:128], row0[0:1, gs],
                start=True, stop=False),
                reads=["rowp", "row0"], writes=[pb_reg], tag=f"{tag}kseed{h}")
            for c in range(DC):
                p.emit("te", lambda e, c=c, g=g, hb=hb, bank=bank: e.matmul(
                    pa[0:64, bank * 512:bank * 512 + 512],
                    wqkv[:, hb * 1536 + c * 192: hb * 1536 + c * 192 + 64],
                    xbf[:, c * S + g * 512: c * S + g * 512 + 512],
                    start=False, stop=(c == DC - 1)),
                    reads=[wreg, f"xbf{c}"], writes=[pa_reg], tag=f"{tag}qmm{h}")
                p.emit("te", lambda e, c=c, g=g, hb=hb, bank=bank: e.matmul(
                    pb[0:64, bank * 512:bank * 512 + 512],
                    wqkv[:, hb * 1536 + c * 192 + 64: hb * 1536 + c * 192 + 128],
                    xbf[:, c * S + g * 512: c * S + g * 512 + 512],
                    start=False, stop=(c == DC - 1)),
                    reads=[wreg, f"xbf{c}"], writes=[pb_reg], tag=f"{tag}kmm{h}")
            p.emit("dve", lambda e, gs=gs, bank=bank: e.scalar_tensor_tensor(
                qb[0:64, gs], pa[0:64, bank * 512:bank * 512 + 512], scale, brs[0:64, gs],
                ALU.mult, ALU.mult),
                reads=[pa_reg, "BRS"], writes=["QB"], tag=f"{tag}qev{h}")
            p.emit("dve", lambda e, gs=gs, bank=bank: e.scalar_tensor_tensor(
                kb[0:64, gs], pb[0:64, bank * 512:bank * 512 + 512], 1.0, brs[0:64, gs],
                ALU.mult, ALU.mult),
                reads=[pb_reg, "BRS"], writes=["KB"], tag=f"{tag}kev{h}")

        for tt in range(ST):
            vps = pd if tt % 2 == 0 else pc
            vreg = "PD0" if tt % 2 == 0 else "PC0"
            p.emit("te", lambda e, tt=tt, vps=vps: e.matmul(
                vps[:, 0:64], row0[0:1, tt * 128:(tt + 1) * 128], rowp[0:1, 128:192],
                start=True, stop=False),
                reads=["rowp", "row0"], writes=[vreg], tag=f"{tag}vseed{h}")
            for c in range(DC):
                p.emit("te", lambda e, c=c, tt=tt, hb=hb, vps=vps: e.matmul(
                    vps[:, 0:64],
                    xbf[:, c * S + tt * 128: c * S + (tt + 1) * 128],
                    wqkv[:, hb * 1536 + c * 192 + 128: hb * 1536 + (c + 1) * 192],
                    start=False, stop=(c == DC - 1)),
                    reads=[wreg, f"xbf{c}"], writes=[vreg], tag=f"{tag}vmm{h}")
            p.emit("dve", lambda e, tt=tt, vps=vps: e.tensor_scalar(
                vb[:, tt * 65: tt * 65 + 64], vps[:, 0:64], rst[:, tt:tt + 1], None, ALU.mult),
                reads=[vreg, "RST"], writes=["VB"], tag=f"{tag}vev{h}")

        if debug == "proj":
            continue
        if rope:
            rope_apply(mk, "QB", f"{tag}rq{h}")
            rope_apply(mk, "KB", f"{tag}rk{h}")

        p.emit("act", lambda e: e.activation(out=sq[0:64, S:2 * S], in_=kb[0:64, :], func=AF.Square),
               reads=["KB"], writes=["sq1"], tag=f"{tag}k2{h}")
        for g in range(NQG):
            p.emit("te", lambda e, g=g: e.matmul(
                pd[0:1, 0:512], onesc[0:64, :], sq[0:64, S + g * 512: S + (g + 1) * 512],
                start=True, stop=True),
                reads=["sq1", "ONESC"], writes=["PD0"], tag=f"{tag}k2mm{h}")
            p.emit("dve", lambda e, g=g: e.tensor_reduce(
                misc[0:1, g:g + 1], pd[0:1, 0:512], AX.X, ALU.max),
                reads=["PD0"], writes=["misc"], tag=f"{tag}k2max{h}")
        p.emit("dve", lambda e: e.tensor_reduce(misc[0:1, 8:9], misc[0:1, 0:8], AX.X, ALU.max),
               reads=["misc"], writes=["misc"], tag=f"{tag}mk2{h}")
        p.emit("act", lambda e: e.activation(out=misc[0:1, 9:10], in_=misc[0:1, 8:9], func=AF.Sqrt),
               reads=["misc"], writes=["misc"], tag=f"{tag}mk{h}")
        p.emit("dve", lambda e: e.tensor_scalar(misc[0:1, 10:11], misc[0:1, 9:10], -1.0, None, ALU.mult),
               reads=["misc"], writes=["misc"], tag=f"{tag}mkn{h}")
        p.emit("act", lambda e: e.activation(out=sq[0:64, 0:S], in_=qb[0:64, :], func=AF.Square),
               reads=["QB"], writes=["sq0"], tag=f"{tag}q2{h}")
        for g in range(NQG):
            gs = slice(g * 512, (g + 1) * 512)
            p.emit("te", lambda e, gs=gs: e.matmul(
                pd[0:1, 0:512], onesc[0:64, :], sq[0:64, gs], start=True, stop=True),
                reads=["sq0", "ONESC"], writes=["PD0"], tag=f"{tag}q2mm{h}")
            p.emit("act", lambda e, gs=gs: e.activation(
                out=rowc[0:1, gs], in_=pd[0:1, 0:512], func=AF.Sqrt),
                reads=["PD0"], writes=["rowc"], tag=f"{tag}qn{h}")
        p.emit("dve", lambda e: e.tensor_scalar(
            qb[64:65, :], rowc[0:1, :], misc[0:1, 10:11], NEG_MARGIN, ALU.mult, ALU.add),
            reads=["rowc", "misc"], writes=["qb_c"], tag=f"{tag}negc{h}")

        if debug == "norms":
            continue
        for g in range(NQG):
            gs = slice(g * 512, (g + 1) * 512)
            oband = g % 2
            pc_reg = f"PC{oband}"
            kts = list(range(min(32, 4 * (g + 1)))) if causal else list(range(32))

            def emit_smm(i, kt):
                bank = i % 2
                pa_reg = f"PA{bank}"
                p.emit("te", lambda e, kt=kt, gs=gs, bank=bank: e.matmul(
                    pa[:, bank * 512:bank * 512 + 512], kb[0:65, kt * 128:(kt + 1) * 128],
                    qb[0:65, gs], start=True, stop=True),
                    reads=["KB", "kb_ones", "QB", "qb_c"], writes=[pa_reg], tag=f"{tag}smm{h}")
                p.emit("act", lambda e, bank=bank: e.activation(
                    out=atb[:, bank * 512:bank * 512 + 512], in_=pa[:, bank * 512:bank * 512 + 512],
                    func=AF.Exp),
                    reads=[pa_reg], writes=[f"atb{bank}"], tag=f"{tag}exp{h}")
                if causal and kt >= 4 * g:
                    j = kt - 4 * g
                    p.emit("dve", lambda e, bank=bank, j=j: e.tensor_mul(
                        atb[:, bank * 512:bank * 512 + 512], atb[:, bank * 512:bank * 512 + 512],
                        sb["CMASK"][:, 384 - 128 * j: 896 - 128 * j]),
                        reads=[f"atb{bank}", "CMASK"], writes=[f"atb{bank}"], tag=f"{tag}mask{h}")

            def emit_av(i, kt):
                bank = i % 2
                p.emit("te", lambda e, kt=kt, bank=bank, oband=oband, st=(i == 0), sp=(i == len(kts) - 1): e.matmul(
                    pc[0:65, oband * 512:oband * 512 + 512], vb[:, kt * 65:kt * 65 + 65],
                    atb[:, bank * 512:bank * 512 + 512], start=st, stop=sp),
                    reads=["VB", "vb_ones", f"atb{bank}"], writes=[pc_reg], tag=f"{tag}av{h}")

            for i, kt in enumerate(kts):
                emit_smm(i, kt)
                if i > 0:
                    emit_av(i - 1, kts[i - 1])
            emit_av(len(kts) - 1, kts[-1])
            p.emit("dve", lambda e, oband=oband: e.reciprocal(
                xf[0:1, 0:512], pc[64:65, oband * 512:oband * 512 + 512]),
                reads=[pc_reg], writes=["xf0"], tag=f"{tag}rrec{h}")
            p.emit("te", lambda e: e.matmul(
                pd[0:64, 0:512], onesf[0:1, 0:64], xf[0:1, 0:512], start=True, stop=True),
                reads=["xf0", "ONESF"], writes=["PD0"], tag=f"{tag}rbc{h}")
            p.emit("dve", lambda e: e.tensor_copy(rbf[0:64, 0:512], pd[0:64, 0:512]),
                reads=["PD0"], writes=["rbv"], tag=f"{tag}rbev{h}")
            p.emit("dve", lambda e, gs=gs, oband=oband: e.tensor_mul(
                otb[0:64, gs], pc[0:64, oband * 512:oband * 512 + 512], rbf[0:64, 0:512]),
                reads=[pc_reg, "rbv"], writes=["OTB"], tag=f"{tag}oev{h}")
        p.dma("sp", "ot", lambda e, h=h: e.dma_start(out=ot[h], in_=otb[0:64, :]),
              reads=["OTB"], writes=[f"M:ot{h}"], tag=f"{tag}otst{h}")

    if debug and not str(debug).startswith("phaseB"):
        passthrough(mk, master_in, master_out, tag + "dbg_")
        return
    pb_mode = None
    if str(debug).startswith("phaseB"):
        nh = 2
        pb_mode = debug[6:]
    for oc in range(DC):
        for g in range(NQG):
            gs = slice(g * 512, (g + 1) * 512)
            bank = g % 2
            pa_reg = f"PA{bank}"
            slot = (oc * NQG + g) % 4
            xf_reg = f"xf{slot}"
            p.dma("sp", "xf", lambda e, oc=oc, gs=gs, slot=slot: e.dma_start(
                out=xf[:, slot * 512:(slot + 1) * 512],
                in_=hbm[master_in][oc * 128:(oc + 1) * 128, gs.start:gs.stop]),
                reads=[f"M:{master_in}"], writes=[xf_reg], tag=f"{tag}ldxf{oc}_{g}")
            for hh in range(nh // 2):
                p.dma("sp", "ots", lambda e, hh=hh, g=g: e.dma_start(
                    out=ots[:, hh * 512:(hh + 1) * 512],
                    in_=ot[2 * hh:2 * hh + 2, :, g * 512:(g + 1) * 512]),
                    reads=[f"M:ot{2 * hh}", f"M:ot{2 * hh + 1}"], writes=[f"ots{hh}", "OTB"],
                    tag=f"{tag}ldots{oc}_{g}_{hh}")
            npair = nh // 2 if pb_mode in (None, "1") else (1 if pb_mode == "m0" else 0)
            for j in range(npair):
                p.emit("te", lambda e, j=j, oc=oc, bank=bank, npair=max(npair, 1): e.matmul(
                    pa[:, bank * 512:bank * 512 + 512],
                    woall[:, j * 1024 + oc * 128: j * 1024 + (oc + 1) * 128],
                    ots[:, j * 512:(j + 1) * 512],
                    start=(j == 0), stop=(j == npair - 1)),
                    reads=["WOALL", f"ots{j}"], writes=[pa_reg], tag=f"{tag}womm{oc}")
            if npair:
                p.emit("dve", lambda e, slot=slot, bank=bank: e.tensor_add(
                    xf[:, slot * 512:(slot + 1) * 512],
                    xf[:, slot * 512:(slot + 1) * 512],
                    pa[:, bank * 512:bank * 512 + 512]),
                    reads=[pa_reg, xf_reg], writes=[xf_reg], tag=f"{tag}resadd{oc}")
            p.dma("sp", "xfout", lambda e, oc=oc, gs=gs, slot=slot: e.dma_start(
                out=hbm[master_out][oc * 128:(oc + 1) * 128, gs.start:gs.stop],
                in_=xf[:, slot * 512:(slot + 1) * 512]),
                reads=[xf_reg], writes=[f"M:{master_out}"], tag=f"{tag}stxf{oc}_{g}")
            p.emit("dve", lambda e, oc=oc, gs=gs, slot=slot: e.tensor_copy(
                xbf[:, oc * S + gs.start: oc * S + gs.stop], xf[:, slot * 512:(slot + 1) * 512]),
                reads=[xf_reg], writes=[f"xbf{oc}"], tag=f"{tag}shad{oc}_{g}")



def repack_weights(inputs):
    w = {}
    have = lambda *names: all(n in inputs for n in names)

    def qkv_stack(wq, wk, wv, nh=16):
        def by_head(a):
            return np.ascontiguousarray(a.reshape(D, nh, 64).transpose(1, 0, 2))
        return np.ascontiguousarray(np.concatenate(
            [by_head(wq), by_head(wk), by_head(wv)], axis=2))

    if have("L0_attn_b_w_qkv"):
        wqkv = inputs["L0_attn_b_w_qkv"]
        w["w_l0b_qkv"] = np.ascontiguousarray(
            wqkv.reshape(3, 16, 64, D).transpose(1, 3, 0, 2).reshape(16, D, 192))
        w["w_l0b_o"] = np.ascontiguousarray(inputs["L0_attn_b_w_out"].T.reshape(16, 64, D))
    if have("L1_attn_a_wq"):
        w["w_l1a_qkv"] = qkv_stack(inputs["L1_attn_a_wq"], inputs["L1_attn_a_wk"],
                                   inputs["L1_attn_a_wv"])
        w["w_l1a_o"] = np.ascontiguousarray(inputs["L1_attn_a_wo"].reshape(16, 64, D))
    if have("L1_attn_b_wq"):
        w["w_l1b_qkv"] = qkv_stack(inputs["L1_attn_b_wq"].T, inputs["L1_attn_b_wk"].T,
                                   inputs["L1_attn_b_wv"].T)
        w["w_l1b_o"] = np.ascontiguousarray(inputs["L1_attn_b_wo"].T.reshape(16, 64, D))
    if "L0_ffn_wqkv" in inputs:
        wq = inputs["L0_ffn_wqkv"]
        w["w_l0f_qkv"] = np.ascontiguousarray(
            wq.reshape(2, D, 3, 8, 64).transpose(0, 3, 1, 2, 4).reshape(2, 8, D, 192))
        w1 = inputs["L0_ffn_w1"]
        w1p = np.zeros((2, 22, D, 256), np.float32)
        for j in range(22):
            a0, a1 = j * 128, min((j + 1) * 128, 2730)
            w1p[:, j, :, 0:a1 - a0] = w1[:, :, a0:a1]
            w1p[:, j, :, 128:128 + a1 - a0] = w1[:, :, 2730 + a0:2730 + a1]
        w["w1p"] = w1p
        b1 = inputs["L0_ffn_b1"]
        b1p = np.zeros((2, 22, 256), np.float32)
        for j in range(22):
            a0, a1 = j * 128, min((j + 1) * 128, 2730)
            b1p[:, j, 0:a1 - a0] = b1[:, a0:a1]
            b1p[:, j, 128:128 + a1 - a0] = b1[:, 2730 + a0:2730 + a1]
        w["b1p"] = b1p.reshape(2, 5632)
        w2 = inputs["L0_ffn_w2"]
        w2p = np.zeros((2, 22, 128, D), np.float32)
        for j in range(22):
            a0, a1 = j * 128, min((j + 1) * 128, 2730)
            w2p[:, j, 0:a1 - a0, :] = w2[:, a0:a1, :]
        w["w2p"] = w2p
        w["lg_c"] = np.ascontiguousarray(inputs["L0_ffn_ln_g"].reshape(2, 8, 128).transpose(0, 2, 1))
        w["lb_c"] = np.ascontiguousarray(inputs["L0_ffn_ln_b"].reshape(2, 8, 128).transpose(0, 2, 1))
        w["b2_c"] = np.ascontiguousarray(inputs["L0_ffn_b2"].reshape(2, 8, 128).transpose(0, 2, 1))
    if "L0_attn_a_b_UQ" in inputs:
        w["b_uq_c"] = np.ascontiguousarray(inputs["L0_attn_a_b_UQ"].reshape(8, 128).T)
        w["b_uk_c"] = np.ascontiguousarray(inputs["L0_attn_a_b_UK"].reshape(8, 128).T)
        w["b_qr_c"] = np.ascontiguousarray(inputs["L0_attn_a_b_QR"].reshape(8, 32).T)
    return w


def passthrough(mk, master_in, master_out, tag):
    p = mk.p
    xf = mk.sb["XF"]
    hbm = mk.hbm
    for oc in range(DC):
        for g in range(S // 512):
            slot = (oc * (S // 512) + g) % 4
            xf_reg = f"xf{slot}"
            p.dma("sp", "xf", lambda e, oc=oc, g=g, slot=slot: e.dma_start(
                out=xf[:, slot * 512:(slot + 1) * 512],
                in_=hbm[master_in][oc * 128:(oc + 1) * 128, g * 512:(g + 1) * 512]),
                reads=[f"M:{master_in}"], writes=[xf_reg], tag=f"{tag}ld{oc}_{g}")
            p.dma("sp", "xfout", lambda e, oc=oc, g=g, slot=slot: e.dma_start(
                out=hbm[master_out][oc * 128:(oc + 1) * 128, g * 512:(g + 1) * 512],
                in_=xf[:, slot * 512:(slot + 1) * 512]),
                reads=[xf_reg], writes=[f"M:{master_out}"], tag=f"{tag}st{oc}_{g}")


def build_sublayer(mk, idx, master_in, master_out):
    if idx == -1:
        passthrough(mk, master_in, master_out, "pt_")
    elif idx == -2:
        ln_stats(mk, "dbg_")
        passthrough(mk, master_in, master_out, "pt_")
    elif idx in (-3, -4, -5, -6, -7, -8, -9):
        dbg = {-3: "proj", -4: "norms", -5: "full1", -6: "full16", -7: "phaseB1",
               -8: "phaseBd", -9: "phaseBm0"}[idx]
        mk.dram_in("w_l0b_qkv", (16, D, 192))
        mk.dram_in("w_l0b_o", (16, 64, D))
        mk.dram_tmp("ot_l0b", (16, 64, S), BF16)
        attn_std(mk, idx, "w_l0b_qkv", "w_l0b_o", "ot_l0b", 16, 0.125, False, True,
                 master_in, master_out, "dbg_", debug=dbg)
    elif idx == 2:
        mk.dram_in("w_l0b_qkv", (16, D, 192))
        mk.dram_in("w_l0b_o", (16, 64, D))
        mk.dram_tmp("ot_l0b", (16, 64, S), BF16)
        attn_std(mk, idx, "w_l0b_qkv", "w_l0b_o", "ot_l0b", 16, 0.125, False, True,
                 master_in, master_out, "s2_")
    elif idx == 4:
        mk.dram_in("w_l1a_qkv", (16, D, 192))
        mk.dram_in("w_l1a_o", (16, 64, D))
        mk.dram_tmp("ot_l1a", (16, 64, S), BF16)
        attn_std(mk, idx, "w_l1a_qkv", "w_l1a_o", "ot_l1a", 16, 0.125, True, True,
                 master_in, master_out, "s4_")
    elif idx == 6:
        mk.dram_in("w_l1b_qkv", (16, D, 192))
        mk.dram_in("w_l1b_o", (16, 64, D))
        mk.dram_tmp("ot_l1b", (16, 64, S), BF16)
        attn_std(mk, idx, "w_l1b_qkv", "w_l1b_o", "ot_l1b", 16, 0.125, True, False,
                 master_in, master_out, "s6_")
    elif idx == 0:
        for nm in ("W_DQ", "b_DQ", "W_UQ", "b_UQ", "W_DKV", "b_DKV", "W_UK", "b_UK",
                   "W_UV", "b_UV", "W_QR", "b_QR", "W_KR", "b_KR", "W_O", "b_O"):
            shp = {"W_DQ": (D, 32), "b_DQ": (32,), "W_UQ": (32, D), "b_UQ": (D,),
                   "W_DKV": (D, 32), "b_DKV": (32,), "W_UK": (32, D), "b_UK": (D,),
                   "W_UV": (32, D), "b_UV": (D,), "W_QR": (32, 256), "b_QR": (256,),
                   "W_KR": (D, 32), "b_KR": (32,), "W_O": (D, D), "b_O": (D,)}[nm]
            mk.dram_in(f"L0_attn_a_{nm}", shp)
        mk.dram_in("b_uq_c", (128, 8))
        mk.dram_in("b_uk_c", (128, 8))
        mk.dram_in("b_qr_c", (32, 8))
        mk.dram_tmp("ot_l0a", (8, 128, S), BF16)
        mla(mk, master_in, master_out, "s0_")
    elif idx in (1, 3):
        first = not getattr(mk, "l0w_done", False)
        if first:
            mk.dram_in("L0_ffn_wqkv", (2, D, 1536))
            mk.dram_in("L0_ffn_q_scale", (2, 64))
            mk.dram_in("L0_ffn_k_scale", (2, 64))
            mk.dram_in("L0_ffn_wo", (2, 512, D))
            mk.dram_in("L0_ffn_b1", (2, 5460))
            mk.dram_in("L0_ffn_b2", (2, D))
            mk.dram_in("w_l0f_qkv", (2, 8, D, 192))
            mk.dram_in("w1p", (2, 22, D, 256))
            mk.dram_in("b1p", (2, 5632))
            mk.dram_in("w2p", (2, 22, 128, D))
            mk.dram_in("lg_c", (2, 128, 8))
            mk.dram_in("lb_c", (2, 128, 8))
            mk.dram_in("b2_c", (2, 128, 8))
            for nm in ("c_l0f_cos", "c_l0f_ssin", "c_l0f_xs", "c_l0f_xsi"):
                mk.dram_in(nm, (64, S))
            mk.host_inputs.update(l0f_tables())
            mk.dram_tmp("w1b", (2, 22, D, 256), BF16)
            mk.dram_tmp("l0rows", (2, 22, 2, 256))
            mk.dram_tmp("w2b", (2, 22, 128, D), BF16)
            mk.dram_tmp("ot_l0f", (8, 64, S), BF16)
            mk.dram_tmp("zm", (D, S))
            l0f_precast(mk)
            mk.l0w_done = True
        l0ffn(mk, master_in, master_out, f"s{idx}_")
    elif idx in (5, 7):
        first = not getattr(mk, "l1w_done", False)
        if first:
            mk.dram_in("L1_ffn_w_gate", (L1I, D))
            mk.dram_in("L1_ffn_w_up", (L1I, D))
            mk.dram_in("L1_ffn_w_down", (D, L1I))
            mk.l1w_done = True
        l1ffn(mk, master_in, master_out, f"s{idx}_", quant_weights=first)
    else:
        raise NotImplementedError(f"sublayer {idx}")


def build_program(sublayers, debug_masters=False, ses=True):
    stack = contextlib.ExitStack()
    nc = bass.Bass("TRN2", target_bir_lowering=False, debug=False)
    prog = Prog(nc, stack, same_engine_sync=ses)
    mk = MK(nc, prog)
    with stack:
        mk.dram_in("xT", (D, S))
        mk.dram_out("yT", (D, S))
        if len(sublayers) > 1:
            mk.dram_tmp("xm0", (D, S))
            mk.dram_tmp("xm1", (D, S))
        alloc_main(mk)
        load_constants(mk)
        load_shadow(mk, "xT")
        cur = "xT"
        for i, idx in enumerate(sublayers):
            last = (i == len(sublayers) - 1)
            nxt = "yT" if last else ("xm0" if i % 2 == 0 else "xm1")
            if i > 0:
                prog.barrier()
            build_sublayer(mk, idx, cur, nxt)
            cur = nxt
            if debug_masters:
                dbg = mk.dram_out(f"dbg{i}", (D, S))
                for c in range(DC):
                    prog.dma("sp", f"dbg{i}", lambda e, c=c, dbg=dbg, cur=cur: e.dma_start(
                        out=dbg[c * 128:(c + 1) * 128, :],
                        in_=mk.hbm[cur][c * 128:(c + 1) * 128, :]),
                        reads=[f"M:{cur}"], writes=[f"M:dbg{i}"], tag=f"dbg{i}_{c}")
        prog.wait_all_dmas("sp")
        with nc.Block() as block:
            prog.finalize_into_block(block)
    return mk


def build_in_map(mk, inputs, w, xT):
    m = {"xT": xT}
    m.update(mk.host_inputs)
    for name, arr in w.items():
        if name in mk.hbm and name not in m:
            m[name] = arr
    for name, arr in inputs.items():
        if name in mk.hbm and name not in m:
            m[name] = arr
    return m


def kernel(**inputs):
    x = inputs["L0_attn_a_h_t"]
    sublayers = list(range(8))
    mk = build_program(sublayers, ses=True)
    w = repack_weights(inputs)
    in_maps = [build_in_map(mk, inputs, w, np.ascontiguousarray(x[core].T))
               for core in range(NCORES)]
    res = run_bass_kernel_spmd(mk.nc, in_maps, core_ids=list(range(NCORES)))
    out = np.stack([np.ascontiguousarray(r["yT"].T) for r in res.results])
    return out



L1I = 2816

def _l1_views(mk):
    ar = mk.sb["AR"]
    v = {}
    v["XNAT"] = ar[:, 0:1024]
    v["XNF"] = ar[:, 1024:3072].bitcast(F32)
    v["XQ"] = ar[:, 0:1024]
    v["XQT"] = ar[:, 3072:7168]
    v["HQT"] = ar[:, 7168:18432]
    v["QL"] = ar[:, 18432:19456]
    v["T1"] = ar[:, 19456:20480]
    v["T2"] = ar[:, 20480:21504]
    v["SG"] = ar[:, 21504:22528]
    v["AQ"] = ar[:, 22528:24576].bitcast(F32)
    v["BM"] = ar[:, 24576:24704].bitcast(F32)
    v["BMH"] = ar[:, 24704:24832].bitcast(F32)
    v["BMR"] = ar[:, 24832:24960].bitcast(F32)
    v["GT"] = ar[:, 24960:25472]
    v["HF"] = ar[:, 25472:26496].bitcast(F32)
    v["HQ"] = ar[:, 26496:27008]
    v["COLS"] = ar[:, 27008:27136].bitcast(F32)
    v["WSL"] = ar[:, 27136:35328]
    return v


def quant_tile(mk, v, src_ap, out_ap, width, tag, reads, interm_pfx="l1q"):
    p = mk.p
    nb = width // 16
    ql, t1, t2, sg = v["QL"], v["T1"], v["T2"], v["SG"]
    aq, bm, bmh, bmr = v["AQ"], v["BM"], v["BMH"], v["BMR"]
    R = [interm_pfx]
    p.emit("dve", lambda e: e.tensor_reduce(
        bm[:, 0:nb], src_ap.rearrange("p (b k) -> p b k", k=16), AX.X, ALU.max,
        apply_absolute_value=True),
        reads=reads, writes=R, tag=f"{tag}bmax")
    p.emit("dve", lambda e: e.tensor_scalar(bm[:, 0:nb], bm[:, 0:nb], 1.0 / 6.0, 1e-12,
                                            ALU.mult, ALU.max),
           reads=R, writes=R, tag=f"{tag}scale")
    p.emit("dve", lambda e: e.tensor_scalar(bmh[:, 0:nb], bm[:, 0:nb], 0.5, None, ALU.mult),
           reads=R, writes=R, tag=f"{tag}hscale")
    p.emit("dve", lambda e: e.reciprocal(bmr[:, 0:nb], bm[:, 0:nb]),
           reads=R, writes=R, tag=f"{tag}rscale")
    p.emit("act", lambda e: e.activation(out=sg[:, 0:width], in_=src_ap, func=AF.Sign),
           reads=reads, writes=R, tag=f"{tag}sign")
    p.emit("act", lambda e: e.activation(out=aq[:, 0:width], in_=src_ap, func=AF.Abs),
           reads=reads, writes=R, tag=f"{tag}abs")
    p.emit("dve", lambda e: e.tensor_tensor(
        aq[:, 0:width].rearrange("p (b k) -> p b k", k=16),
        aq[:, 0:width].rearrange("p (b k) -> p b k", k=16),
        bmr[:, 0:nb, None].broadcast_to([128, nb, 16]), ALU.mult),
        reads=R, writes=R, tag=f"{tag}an")
    p.emit("dve", lambda e: e.tensor_scalar(ql[:, 0:width], aq[:, 0:width], 0.25, None, ALU.is_gt),
           reads=R, writes=R, tag=f"{tag}c0")
    for m in (0.75, 1.25, 1.75):
        p.emit("dve", lambda e, m=m: e.scalar_tensor_tensor(
            ql[:, 0:width], aq[:, 0:width], m, ql[:, 0:width], ALU.is_gt, ALU.add),
            reads=R, writes=R, tag=f"{tag}c{m}")
    p.emit("dve", lambda e: e.tensor_scalar(t1[:, 0:width], aq[:, 0:width], 2.5, None, ALU.is_gt),
           reads=R, writes=R, tag=f"{tag}c25")
    p.emit("dve", lambda e: e.scalar_tensor_tensor(
        t1[:, 0:width], aq[:, 0:width], 3.5, t1[:, 0:width], ALU.is_gt, ALU.add),
        reads=R, writes=R, tag=f"{tag}c35")
    p.emit("dve", lambda e: e.tensor_scalar(t2[:, 0:width], aq[:, 0:width], 5.0, None, ALU.is_gt),
           reads=R, writes=R, tag=f"{tag}c5")
    p.emit("dve", lambda e: e.scalar_tensor_tensor(
        t1[:, 0:width], t2[:, 0:width], 2.0, t1[:, 0:width], ALU.mult, ALU.add),
        reads=R, writes=R, tag=f"{tag}cmb1")
    p.emit("dve", lambda e: e.scalar_tensor_tensor(
        ql[:, 0:width], t1[:, 0:width], 2.0, ql[:, 0:width], ALU.mult, ALU.add),
        reads=R, writes=R, tag=f"{tag}cmb2")
    p.emit("dve", lambda e: e.tensor_tensor(
        ql[:, 0:width].rearrange("p (b k) -> p b k", k=16),
        ql[:, 0:width].rearrange("p (b k) -> p b k", k=16),
        bmh[:, 0:nb, None].broadcast_to([128, nb, 16]), ALU.mult),
        reads=R, writes=R, tag=f"{tag}dq1")
    p.emit("dve", lambda e: e.tensor_tensor(out_ap, ql[:, 0:width], sg[:, 0:width], ALU.mult),
           reads=R, writes=R + ["l1q_out"], tag=f"{tag}dq2")


def l1ffn_quant_weights(mk, v, tag):
    p = mk.p
    hbm = mk.hbm
    pt = mk.ps["PT"]
    ident = mk.sb["IDENT"]
    xnf, xq = v["XNF"], v["XQ"]
    jobs = [("L1_ffn_w_gate", "wgq", L1I, D), ("L1_ffn_w_up", "wuq", L1I, D),
            ("L1_ffn_w_down", "wdq", D, L1I)]
    for src_name, dst_name, nr, ncol in jobs:
        src = hbm[src_name]
        dst = hbm[dst_name]
        for rt in range(nr // 128):
            for c0 in range(0, ncol, 1024):
                w = min(1024, ncol - c0)
                p.dma("sp", "l1w", lambda e, src=src, rt=rt, c0=c0, w=w: e.dma_start(
                    out=xnf[:, 0:w], in_=src[rt * 128:(rt + 1) * 128, c0:c0 + w]),
                    reads=[f"M:{src_name}"], writes=["l1wf"], tag=f"{tag}ldw")
                quant_tile(mk, v, xnf[:, 0:w], xq[:, 0:w], w, f"{tag}q{dst_name}{rt}_{c0}",
                           ["l1wf"])
                for b in range(w // 128):
                    p.emit("te", lambda e, b=b: e.transpose(
                        pt[:, 0:128], xq[:, b * 128:(b + 1) * 128], ident[:, :]),
                        reads=["l1q_out", "IDENT"], writes=["PT"], tag=f"{tag}wT")
                    p.emit("dve", lambda e, b=b: e.tensor_copy(v["GT"][:, 0:128], pt[:, 0:128]),
                           reads=["PT"], writes=["l1gt"], tag=f"{tag}wTe")
                    p.dma("sp", "l1wq", lambda e, dst=dst, rt=rt, c0=c0, b=b: e.dma_start(
                        out=dst[c0 + b * 128: c0 + (b + 1) * 128, rt * 128:(rt + 1) * 128],
                        in_=v["GT"][:, 0:128]),
                        reads=["l1gt"], writes=[f"M:{dst_name}"], tag=f"{tag}stw")


def l1ffn(mk, master_in, master_out, tag, quant_weights):
    p = mk.p
    hbm = mk.hbm
    xbf, xf = mk.sb["XBF"], mk.sb["XF"]
    ident, consts = mk.sb["IDENT"], mk.sb["CONSTS"]
    pa, pb, pc, pd, pt = mk.ps["PA"], mk.ps["PB"], mk.ps["PC"], mk.ps["PD"], mk.ps["PT"]
    v = _l1_views(mk)
    if quant_weights:
        mk.dram_tmp("wgq", (D, L1I), BF16)
        mk.dram_tmp("wuq", (D, L1I), BF16)
        mk.dram_tmp("wdq", (L1I, D), BF16)
        l1ffn_quant_weights(mk, v, tag + "w_")
        p.barrier()
    wgq, wuq, wdq = hbm["wgq"], hbm["wuq"], hbm["wdq"]
    cols = v["COLS"]
    ORANGES = [(i * 512, 512) for i in range(5)] + [(2560, 256)]

    for grp in range(ST // 4):
        tts = [grp * 4 + i for i in range(4)]
        for i, tt in enumerate(tts):
            for c in range(DC):
                sl = c % 4
                p.dma("sp", "l1xm", lambda e, c=c, tt=tt, sl=sl: e.dma_start(
                    out=xf[:, sl * 512: sl * 512 + 128],
                    in_=hbm[master_in][c * 128:(c + 1) * 128, tt * 128:(tt + 1) * 128]),
                    reads=[f"M:{master_in}"], writes=[f"xf{sl}"], tag=f"{tag}xml")
                p.emit("te", lambda e, c=c, sl=sl: e.transpose(
                    pd[:, 0:128], xf[:, sl * 512: sl * 512 + 128], mk.sb["IDENTF"][:, :]),
                    reads=[f"xf{sl}", "IDENTF"], writes=["PD0"], tag=f"{tag}xT")
                p.emit("dve", lambda e, c=c: e.tensor_copy(
                    v["XNF"][:, c * 128:(c + 1) * 128], pd[:, 0:128]),
                    reads=["PD0"], writes=["l1xnf"], tag=f"{tag}xTe")
            p.emit("dve", lambda e: e.tensor_reduce(cols[:, 0:1], v["XNF"][:, :], AX.X, ALU.add),
                   reads=["l1xnf"], writes=["l1cols"], tag=f"{tag}sum")
            p.emit("act", lambda e: e.activation(out=v["AQ"][:, :], in_=v["XNF"][:, :], func=AF.Square),
                   reads=["l1xnf"], writes=["l1q"], tag=f"{tag}sq")
            p.emit("dve", lambda e: e.tensor_reduce(cols[:, 1:2], v["AQ"][:, :], AX.X, ALU.add),
                   reads=["l1q"], writes=["l1cols"], tag=f"{tag}sumsq")
            p.emit("dve", lambda e: e.tensor_scalar(cols[:, 0:1], cols[:, 0:1], 1.0 / D, None, ALU.mult),
                   reads=["l1cols"], writes=["l1cols"], tag=f"{tag}mu")
            p.emit("act", lambda e: e.activation(out=cols[:, 2:3], in_=cols[:, 0:1], func=AF.Square),
                   reads=["l1cols"], writes=["l1cols"], tag=f"{tag}mu2")
            p.emit("dve", lambda e: e.scalar_tensor_tensor(
                cols[:, 1:2], cols[:, 1:2], 1.0 / D, cols[:, 2:3], ALU.mult, ALU.subtract),
                reads=["l1cols"], writes=["l1cols"], tag=f"{tag}var")
            p.emit("act", lambda e: e.activation(out=cols[:, 1:2], in_=cols[:, 1:2], func=AF.Sqrt,
                                                 bias=mk.sb["EPSC"][:, 0:1]),
                   reads=["l1cols", "EPSC"], writes=["l1cols"], tag=f"{tag}sdv")
            p.emit("dve", lambda e: e.reciprocal(cols[:, 1:2], cols[:, 1:2]),
                   reads=["l1cols"], writes=["l1cols"], tag=f"{tag}rsig")
            p.emit("dve", lambda e: e.tensor_scalar(
                v["XNF"][:, :], v["XNF"][:, :], cols[:, 0:1], cols[:, 1:2],
                ALU.subtract, ALU.mult),
                reads=["l1xnf", "l1cols"], writes=["l1xnf"], tag=f"{tag}xn")
            quant_tile(mk, v, v["XNF"][:, :], v["XQ"][:, :], 1024, f"{tag}qx{tt}", ["l1xnf"])
            for b in range(8):
                p.emit("te", lambda e, b=b: e.transpose(
                    pt[:, 0:128], v["XQ"][:, b * 128:(b + 1) * 128], ident[:, :]),
                    reads=["l1q_out", "IDENT"], writes=["PT"], tag=f"{tag}xqT")
                p.emit("dve", lambda e, i=i, b=b: e.tensor_copy(
                    v["XQT"][:, i * 1024 + b * 128: i * 1024 + (b + 1) * 128], pt[:, 0:128]),
                    reads=["PT"], writes=["l1xqt"], tag=f"{tag}xqTe")

        for o0, ow in ORANGES:
            for c in range(DC):
                p.dma("pool", "l1wg", lambda e, c=c, o0=o0, ow=ow: e.dma_start(
                    out=v["WSL"][:, c * 512: c * 512 + ow], in_=wgq[c * 128:(c + 1) * 128, o0:o0 + ow]),
                    reads=["M:wgq"], writes=["l1wsl"], tag=f"{tag}ldg")
                p.dma("pool", "l1wu", lambda e, c=c, o0=o0, ow=ow: e.dma_start(
                    out=v["WSL"][:, 4096 + c * 512: 4096 + c * 512 + ow],
                    in_=wuq[c * 128:(c + 1) * 128, o0:o0 + ow]),
                    reads=["M:wuq"], writes=["l1wsl"], tag=f"{tag}ldu")
            for i, tt in enumerate(tts):
                bank = i % 2
                for c in range(DC):
                    p.emit("te", lambda e, i=i, c=c, ow=ow, bank=bank: e.matmul(
                        pa[:, bank * 512: bank * 512 + ow],
                        v["XQT"][:, i * 1024 + c * 128: i * 1024 + (c + 1) * 128],
                        v["WSL"][:, c * 512: c * 512 + ow],
                        start=(c == 0), stop=(c == DC - 1)),
                        reads=["l1xqt", "l1wsl"], writes=[f"PA{bank}"], tag=f"{tag}gmm")
                    p.emit("te", lambda e, i=i, c=c, ow=ow, bank=bank: e.matmul(
                        pb[:, bank * 512: bank * 512 + ow],
                        v["XQT"][:, i * 1024 + c * 128: i * 1024 + (c + 1) * 128],
                        v["WSL"][:, 4096 + c * 512: 4096 + c * 512 + ow],
                        start=(c == 0), stop=(c == DC - 1)),
                        reads=["l1xqt", "l1wsl"], writes=[f"PB{bank}"], tag=f"{tag}umm")
                p.emit("act", lambda e, ow=ow, bank=bank: e.activation(
                    out=v["XNF"][:, 0:ow], in_=pa[:, bank * 512: bank * 512 + ow], func=AF.Silu),
                    reads=[f"PA{bank}"], writes=["l1xnf"], tag=f"{tag}silu")
                p.emit("dve", lambda e, ow=ow, bank=bank: e.tensor_tensor(
                    v["HF"][:, 0:ow], v["XNF"][:, 0:ow], pb[:, bank * 512: bank * 512 + ow], ALU.mult),
                    reads=["l1xnf", f"PB{bank}"], writes=["l1hf"], tag=f"{tag}h")
                quant_tile(mk, v, v["HF"][:, 0:ow], v["HQ"][:, 0:ow], ow,
                           f"{tag}qh{tt}_{o0}", ["l1hf"])
                for b in range(ow // 128):
                    p.emit("te", lambda e, b=b: e.transpose(
                        pt[:, 0:128], v["HQ"][:, b * 128:(b + 1) * 128], ident[:, :]),
                        reads=["l1q_out", "IDENT"], writes=["PT"], tag=f"{tag}hT")
                    p.emit("dve", lambda e, i=i, o0=o0, b=b: e.tensor_copy(
                        v["HQT"][:, i * L1I + o0 + b * 128: i * L1I + o0 + (b + 1) * 128],
                        pt[:, 0:128]),
                        reads=["PT"], writes=["l1hqt"], tag=f"{tag}hTe")

        for pair in range(2):
            ptts = [(2 * pair, tts[2 * pair]), (2 * pair + 1, tts[2 * pair + 1])]
            banks = {(0, 0): ("PC", 0), (0, 1): ("PC", 1), (1, 0): ("PD", 0), (1, 1): ("PA", 0)}
            for (a, b) in ((0, 8), (8, 16), (16, 22)):
                for j in range(a, b):
                    p.dma("pool", "l1wd", lambda e, j=j, a=a: e.dma_start(
                        out=v["WSL"][:, (j - a) * 1024:(j - a) * 1024 + 1024],
                        in_=wdq[j * 128:(j + 1) * 128, 0:1024]),
                        reads=["M:wdq"], writes=["l1wsl"], tag=f"{tag}ldd")
                for li, tt in ptts:
                    for oc2 in range(2):
                        pn, pk = banks[(li % 2, oc2)]
                        psum_t = mk.ps[pn]
                        p_reg = f"{pn}{pk}"
                        for j in range(a, b):
                            p.emit("te", lambda e, li=li, j=j, a=a, oc2=oc2, psum_t=psum_t, pk=pk: e.matmul(
                                psum_t[:, pk * 512: pk * 512 + 512],
                                v["HQT"][:, li * L1I + j * 128: li * L1I + (j + 1) * 128],
                                v["WSL"][:, (j - a) * 1024 + oc2 * 512: (j - a) * 1024 + oc2 * 512 + 512],
                                start=(j == 0), stop=(j == 21)),
                                reads=["l1hqt", "l1wsl"], writes=[p_reg], tag=f"{tag}dmm")
            for li, tt in ptts:
                for oc2 in range(2):
                    pn, pk = banks[(li % 2, oc2)]
                    psum_t = mk.ps[pn]
                    p_reg = f"{pn}{pk}"
                    p.emit("dve", lambda e, oc2=oc2, psum_t=psum_t, pk=pk: e.tensor_copy(
                        v["T1"][:, 0:512] if oc2 == 0 else v["T2"][:, 0:512],
                        psum_t[:, pk * 512: pk * 512 + 512]),
                        reads=[p_reg], writes=["l1q"], tag=f"{tag}dev")
                for oc in range(DC):
                    srcv = v["T1"] if oc < 4 else v["T2"]
                    b = oc % 4
                    p.emit("te", lambda e, srcv=srcv, b=b: e.transpose(
                        pt[:, 0:128], srcv[:, b * 128:(b + 1) * 128], ident[:, :]),
                        reads=["l1q", "IDENT"], writes=["PT"], tag=f"{tag}oT")
                    slot = oc % 4
                    xf_reg = f"xf{slot}"
                    p.dma("sp", "xf", lambda e, oc=oc, tt=tt, slot=slot: e.dma_start(
                        out=xf[:, slot * 512: slot * 512 + 128],
                        in_=hbm[master_in][oc * 128:(oc + 1) * 128, tt * 128:(tt + 1) * 128]),
                        reads=[f"M:{master_in}"], writes=[xf_reg], tag=f"{tag}ldxf")
                    p.emit("dve", lambda e, slot=slot: e.tensor_add(
                        xf[:, slot * 512: slot * 512 + 128], xf[:, slot * 512: slot * 512 + 128],
                        pt[:, 0:128]),
                        reads=["PT", xf_reg], writes=[xf_reg], tag=f"{tag}radd")
                    p.dma("sp", "xfo", lambda e, oc=oc, tt=tt, slot=slot: e.dma_start(
                        out=hbm[master_out][oc * 128:(oc + 1) * 128, tt * 128:(tt + 1) * 128],
                        in_=xf[:, slot * 512: slot * 512 + 128]),
                        reads=[xf_reg], writes=[f"M:{master_out}"], tag=f"{tag}stxf")
                    p.emit("dve", lambda e, oc=oc, tt=tt, slot=slot: e.tensor_copy(
                        xbf[:, oc * S + tt * 128: oc * S + (tt + 1) * 128],
                        xf[:, slot * 512: slot * 512 + 128]),
                        reads=[xf_reg], writes=[f"xbf{oc}"], tag=f"{tag}shad")



def _mla_views(mk):
    ar = mk.sb["AR"]
    v = {}
    v["QB"] = ar[:, 0:4096]
    v["KB"] = ar[:, 4096:8192]
    v["CQ"] = ar[:, 8192:12288]
    v["CKV"] = ar[:, 12288:16384]
    v["QRB"] = ar[:, 16384:20480]
    v["KRB"] = ar[:, 20480:24576]
    v["VB"] = ar[:, 24576:28672]
    v["OTB"] = ar[:, 28672:32768]
    v["ATB"] = ar[:, 32768:33792]
    v["RB"] = ar[:, 33792:34816].bitcast(F32)
    v["W"] = ar[:, 34816:35232]
    v["SQN"] = ar[:, 35232:35744]
    return v


def mla(mk, master_in, master_out, tag):
    p = mk.p
    hbm = mk.hbm
    xbf, xf, brs = mk.sb["XBF"], mk.sb["XF"], mk.sb["BRS"]
    row0, rowc, rowp, miscb = mk.sb["ROW0"], mk.sb["ROWC"], mk.sb["ROWP"], mk.sb["MISC"]
    onesc, ones1, ident, onesf = mk.sb["ONESC"], mk.sb["ONES1"], mk.sb["IDENT"], mk.sb["ONESF"]
    biasc, brow, bvrow = mk.sb["BIASC"], mk.sb["BROW"], mk.sb["BVROW"]
    pa, pb, pc, pd, pt = mk.ps["PA"], mk.ps["PB"], mk.ps["PC"], mk.ps["PD"], mk.ps["PT"]
    v = _mla_views(mk)
    NQG = S // 512
    SC = 1.0 / math.sqrt(160.0)
    NH = 8
    ot = hbm["ot_l0a"]

    ln_stats(mk, tag)

    p.dma("pool", "b1", lambda e: e.dma_start(out=biasc[:, 0:8], in_=hbm["b_uq_c"][:, :]),
          reads=["M:b_uq_c"], writes=["BIASC"], tag=f"{tag}b1")
    p.dma("pool", "b2", lambda e: e.dma_start(out=biasc[:, 8:16], in_=hbm["b_uk_c"][:, :]),
          reads=["M:b_uk_c"], writes=["BIASC"], tag=f"{tag}b2")
    p.dma("pool", "b3", lambda e: e.dma_start(out=biasc[0:32, 24:32], in_=hbm["b_qr_c"][:, :]),
          reads=["M:b_qr_c"], writes=["BIASC"], tag=f"{tag}b3")
    p.dma("pool", "b4", lambda e: e.dma_start(out=biasc[0:32, 32:33], in_=hbm["L0_attn_a_b_DQ"].rearrange("(a b) -> a b", b=1)),
          reads=["M:b_dq"], writes=["BIASC"], tag=f"{tag}b4")
    p.dma("pool", "b5", lambda e: e.dma_start(out=biasc[0:32, 33:34], in_=hbm["L0_attn_a_b_DKV"].rearrange("(a b) -> a b", b=1)),
          reads=["M:b_dkv"], writes=["BIASC"], tag=f"{tag}b5")
    p.dma("pool", "b6", lambda e: e.dma_start(out=biasc[0:32, 34:35], in_=hbm["L0_attn_a_b_KR"].rearrange("(a b) -> a b", b=1)),
          reads=["M:b_kr"], writes=["BIASC"], tag=f"{tag}b6")
    p.dma("pool", "b7", lambda e: e.dma_start(out=brow[:, :], in_=hbm["L0_attn_a_b_O"].rearrange("(a b) -> a b", a=1)),
          reads=["M:b_o"], writes=["BROW"], tag=f"{tag}b7")
    p.dma("pool", "b8", lambda e: e.dma_start(out=bvrow[:, :], in_=hbm["L0_attn_a_b_UV"].rearrange("(a b) -> a b", a=1)),
          reads=["M:b_uv"], writes=["BVROW"], tag=f"{tag}b8")

    down_jobs = [("L0_attn_a_W_DQ", v["CQ"], "mlaCQ", 32), ("L0_attn_a_W_DKV", v["CKV"], "mlaCKV", 33),
                 ("L0_attn_a_W_KR", v["KRB"], "mlaKRB", 34)]
    for i, (nm, dst, dreg, bcol) in enumerate(down_jobs):
        for c in range(DC):
            p.dma("pool", "wd", lambda e, nm=nm, c=c: e.dma_start(
                out=v["W"][:, c * 32: (c + 1) * 32], in_=hbm[nm][c * 128:(c + 1) * 128, :]),
                reads=[f"M:{nm}"], writes=["mlaW0"], tag=f"{tag}lddw{i}")
        for c in range(DC):
            p.emit("te", lambda e, i=i, c=c: e.matmul(
                pd[0:1, i * 32:(i + 1) * 32], onesc[:, :], v["W"][:, c * 32:(c + 1) * 32],
                start=(c == 0), stop=(c == DC - 1)),
                reads=["mlaW0", "ONESC"], writes=["PD0"], tag=f"{tag}cs{i}")
        p.emit("act", lambda e, i=i: e.activation(out=rowp[0:1, i * 32:(i + 1) * 32],
                                                  in_=pd[0:1, i * 32:(i + 1) * 32],
                                                  func=AF.Identity, scale=-1.0),
               reads=["PD0"], writes=["rowp"], tag=f"{tag}csev{i}")
        for g in range(NQG):
            gs = slice(g * 512, (g + 1) * 512)
            bank = g % 2
            p.emit("te", lambda e, i=i, gs=gs, bank=bank: e.matmul(
                pa[0:32, bank * 512:bank * 512 + 512], rowp[0:1, i * 32:(i + 1) * 32], row0[0:1, gs],
                start=True, stop=False),
                reads=["rowp", "row0"], writes=[f"PA{bank}"], tag=f"{tag}dseed{i}")
            for c in range(DC):
                p.emit("te", lambda e, c=c, g=g, bank=bank: e.matmul(
                    pa[0:32, bank * 512:bank * 512 + 512],
                    v["W"][:, c * 32: (c + 1) * 32],
                    xbf[:, c * S + g * 512: c * S + g * 512 + 512],
                    start=False, stop=(c == DC - 1)),
                    reads=["mlaW0", f"xbf{c}"], writes=[f"PA{bank}"], tag=f"{tag}dmm{i}")
            p.emit("dve", lambda e, gs=gs, bank=bank, dst=dst: e.scalar_tensor_tensor(
                dst[0:32, gs], pa[0:32, bank * 512:bank * 512 + 512], 1.0, brs[0:32, gs],
                ALU.mult, ALU.mult),
                reads=[f"PA{bank}", "BRS"], writes=[dreg], tag=f"{tag}dev{i}")
            p.emit("dve", lambda e, gs=gs, dst=dst, bcol=bcol: e.tensor_scalar(
                dst[0:32, gs], dst[0:32, gs], biasc[0:32, bcol:bcol + 1], None, ALU.add),
                reads=[dreg, "BIASC"], writes=[dreg], tag=f"{tag}db{i}")
    p.emit("dve", lambda e: e.memset(v["KRB"][32:33, :], 1.0), writes=["mla_krones"], tag=f"{tag}krones")

    for h in range(NH):
        woff = 0
        wreg = "mlawh"
        for nm, o2, wdt in (("L0_attn_a_W_UQ", 0, 128), ("L0_attn_a_W_UK", 128, 128),
                            ("L0_attn_a_W_UV", 256, 128)):
            p.dma("pool", "wh", lambda e, nm=nm, o2=o2, h=h, woff=woff, wdt=wdt: e.dma_start(
                out=v["W"][0:32, woff + o2: woff + o2 + wdt], in_=hbm[nm][:, h * 128:(h + 1) * 128]),
                reads=[f"M:{nm}"], writes=[wreg, "mlaW0"], tag=f"{tag}ldwh{h}")
        p.dma("pool", "wh", lambda e, h=h, woff=woff: e.dma_start(
            out=v["W"][0:32, woff + 384: woff + 416], in_=hbm["L0_attn_a_W_QR"][:, h * 32:(h + 1) * 32]),
            reads=["M:L0_attn_a_W_QR"], writes=[wreg, "mlaW0"], tag=f"{tag}ldwq{h}")
        for g in range(NQG):
            gs = slice(g * 512, (g + 1) * 512)
            bank = g % 2
            p.emit("te", lambda e, gs=gs, bank=bank, woff=woff: e.matmul(
                pa[:, bank * 512:bank * 512 + 512], v["W"][0:32, woff:woff + 128], v["CQ"][0:32, gs],
                start=True, stop=True),
                reads=[wreg, "mlaCQ"], writes=[f"PA{bank}"], tag=f"{tag}qmm{h}")
            p.emit("dve", lambda e, gs=gs, bank=bank, h=h: e.tensor_scalar(
                v["QB"][:, gs], pa[:, bank * 512:bank * 512 + 512],
                biasc[:, h:h + 1], SC, ALU.add, ALU.mult),
                reads=[f"PA{bank}", "BIASC"], writes=["mlaQB"], tag=f"{tag}qev{h}")
            p.emit("te", lambda e, gs=gs, bank=bank, woff=woff: e.matmul(
                pb[:, bank * 512:bank * 512 + 512], v["W"][0:32, woff + 128:woff + 256], v["CKV"][0:32, gs],
                start=True, stop=True),
                reads=[wreg, "mlaCKV"], writes=[f"PB{bank}"], tag=f"{tag}kmm{h}")
            p.emit("dve", lambda e, gs=gs, bank=bank, h=h: e.tensor_scalar(
                v["KB"][:, gs], pb[:, bank * 512:bank * 512 + 512],
                biasc[:, 8 + h:9 + h], None, ALU.add),
                reads=[f"PB{bank}", "BIASC"], writes=["mlaKB"], tag=f"{tag}kev{h}")
            p.emit("te", lambda e, gs=gs, bank=bank, woff=woff: e.matmul(
                pc[0:32, bank * 512:bank * 512 + 512], v["W"][0:32, woff + 384:woff + 416], v["CQ"][0:32, gs],
                start=True, stop=True),
                reads=[wreg, "mlaCQ"], writes=[f"PC{bank}"], tag=f"{tag}rmm{h}")
            p.emit("dve", lambda e, gs=gs, bank=bank, h=h: e.tensor_scalar(
                v["QRB"][0:32, gs], pc[0:32, bank * 512:bank * 512 + 512],
                biasc[0:32, 24 + h:25 + h], SC, ALU.add, ALU.mult),
                reads=[f"PC{bank}", "BIASC"], writes=["mlaQRB"], tag=f"{tag}rev{h}")
        for tt in range(ST):
            p.emit("te", lambda e, tt=tt, h=h: e.matmul(
                pd[:, 0:128], ones1[0:1, 0:128], bvrow[0:1, h * 128:(h + 1) * 128],
                start=True, stop=False),
                reads=["ONES1", "BVROW"], writes=["PD0"], tag=f"{tag}vseed{h}")
            p.emit("te", lambda e, tt=tt, woff=woff: e.matmul(
                pd[:, 0:128], v["CKV"][0:32, tt * 128:(tt + 1) * 128], v["W"][0:32, woff + 256:woff + 384],
                start=False, stop=True),
                reads=[wreg, "mlaCKV"], writes=["PD0"], tag=f"{tag}vmm{h}")
            p.emit("dve", lambda e, tt=tt: e.tensor_copy(v["VB"][:, tt * 128:(tt + 1) * 128], pd[:, 0:128]),
                   reads=["PD0"], writes=["mlaVB"], tag=f"{tag}vev{h}")
        for g in range(NQG):
            gs = slice(g * 512, (g + 1) * 512)
            sb2 = 0
            p.emit("act", lambda e, gs=gs, sb2=sb2: e.activation(
                out=v["SQN"][:, sb2:sb2 + 512], in_=v["KB"][:, gs], func=AF.Square),
                reads=["mlaKB"], writes=[f"mlasq{sb2}", "mlaW0"], tag=f"{tag}k2{h}")
            p.emit("te", lambda e, sb2=sb2: e.matmul(
                pd[0:1, 0:512], onesc[:, :], v["SQN"][:, sb2:sb2 + 512], start=True, stop=False),
                reads=[f"mlasq{sb2}", "ONESC"], writes=["PD0"], tag=f"{tag}k2mm{h}")
            p.emit("act", lambda e, gs=gs, sb2=sb2: e.activation(
                out=v["SQN"][0:32, sb2:sb2 + 512], in_=v["KRB"][0:32, gs], func=AF.Square),
                reads=["mlaKRB"], writes=[f"mlasq{sb2}", "mlaW0"], tag=f"{tag}kr2{h}")
            p.emit("te", lambda e, sb2=sb2: e.matmul(
                pd[0:1, 0:512], onesc[0:32, :], v["SQN"][0:32, sb2:sb2 + 512], start=False, stop=True),
                reads=[f"mlasq{sb2}", "ONESC"], writes=["PD0"], tag=f"{tag}kr2mm{h}")
            p.emit("dve", lambda e, g=g: e.tensor_reduce(
                miscb[0:1, g:g + 1], pd[0:1, 0:512], AX.X, ALU.max),
                reads=["PD0"], writes=["misc"], tag=f"{tag}k2max{h}")
        p.emit("dve", lambda e: e.tensor_reduce(miscb[0:1, 8:9], miscb[0:1, 0:8], AX.X, ALU.max),
               reads=["misc"], writes=["misc"], tag=f"{tag}mk2{h}")
        p.emit("act", lambda e: e.activation(out=miscb[0:1, 9:10], in_=miscb[0:1, 8:9], func=AF.Sqrt),
               reads=["misc"], writes=["misc"], tag=f"{tag}mk{h}")
        p.emit("dve", lambda e: e.tensor_scalar(miscb[0:1, 10:11], miscb[0:1, 9:10], -1.0, None, ALU.mult),
               reads=["misc"], writes=["misc"], tag=f"{tag}mkn{h}")
        for g in range(NQG):
            gs = slice(g * 512, (g + 1) * 512)
            sb2 = 0
            p.emit("act", lambda e, gs=gs, sb2=sb2: e.activation(
                out=v["SQN"][:, sb2:sb2 + 512], in_=v["QB"][:, gs], func=AF.Square),
                reads=["mlaQB"], writes=[f"mlasq{sb2}", "mlaW0"], tag=f"{tag}q2{h}")
            p.emit("te", lambda e, sb2=sb2: e.matmul(
                pd[0:1, 0:512], onesc[:, :], v["SQN"][:, sb2:sb2 + 512], start=True, stop=False),
                reads=[f"mlasq{sb2}", "ONESC"], writes=["PD0"], tag=f"{tag}q2mm{h}")
            p.emit("act", lambda e, gs=gs, sb2=sb2: e.activation(
                out=v["SQN"][0:32, sb2:sb2 + 512], in_=v["QRB"][0:32, gs], func=AF.Square),
                reads=["mlaQRB"], writes=[f"mlasq{sb2}", "mlaW0"], tag=f"{tag}qr2{h}")
            p.emit("te", lambda e, sb2=sb2: e.matmul(
                pd[0:1, 0:512], onesc[0:32, :], v["SQN"][0:32, sb2:sb2 + 512], start=False, stop=True),
                reads=[f"mlasq{sb2}", "ONESC"], writes=["PD0"], tag=f"{tag}qr2mm{h}")
            p.emit("act", lambda e, gs=gs: e.activation(
                out=rowc[0:1, gs], in_=pd[0:1, 0:512], func=AF.Sqrt),
                reads=["PD0"], writes=["rowc"], tag=f"{tag}qn{h}")
        p.emit("dve", lambda e: e.tensor_scalar(
            v["QRB"][32:33, :], rowc[0:1, :], miscb[0:1, 10:11], NEG_MARGIN, ALU.mult, ALU.add),
            reads=["rowc", "misc"], writes=["mla_negc"], tag=f"{tag}negc{h}")
        for g in range(NQG):
            gs = slice(g * 512, (g + 1) * 512)
            oband = g % 2
            pc_reg = f"PC{oband}"
            def m_smm(kt):
                bank = kt % 2
                pa_reg = f"PA{bank}"
                p.emit("te", lambda e, kt=kt, gs=gs, bank=bank: e.matmul(
                    pa[:, bank * 512:bank * 512 + 512], v["KB"][:, kt * 128:(kt + 1) * 128],
                    v["QB"][:, gs], start=True, stop=False),
                    reads=["mlaKB", "mlaQB"], writes=[pa_reg], tag=f"{tag}smm{h}")
                p.emit("te", lambda e, kt=kt, gs=gs, bank=bank: e.matmul(
                    pa[:, bank * 512:bank * 512 + 512], v["KRB"][0:33, kt * 128:(kt + 1) * 128],
                    v["QRB"][0:33, gs], start=False, stop=True),
                    reads=["mlaKRB", "mla_krones", "mlaQRB", "mla_negc"], writes=[pa_reg],
                    tag=f"{tag}smm2{h}")
                p.emit("act", lambda e, bank=bank: e.activation(
                    out=v["ATB"][:, bank * 512:bank * 512 + 512],
                    in_=pa[:, bank * 512:bank * 512 + 512], func=AF.Exp),
                    reads=[pa_reg], writes=[f"mlaatb{bank}"], tag=f"{tag}exp{h}")

            def m_av(kt):
                bank = kt % 2
                p.emit("te", lambda e, kt=kt, bank=bank, oband=oband, st=(kt == 0), sp=(kt == 31): e.matmul(
                    pc[:, oband * 512:oband * 512 + 512], v["VB"][:, kt * 128:(kt + 1) * 128],
                    v["ATB"][:, bank * 512:bank * 512 + 512], start=st, stop=sp),
                    reads=["mlaVB", f"mlaatb{bank}"], writes=[pc_reg], tag=f"{tag}av{h}")
                p.emit("te", lambda e, kt=kt, bank=bank, st=(kt == 0), sp=(kt == 31): e.matmul(
                    pd[0:1, 0:512], onesc[:, 0:1], v["ATB"][:, bank * 512:bank * 512 + 512],
                    start=st, stop=sp),
                    reads=["ONESC", f"mlaatb{bank}"], writes=["PD0"], tag=f"{tag}rsum{h}")

            for kt in range(32):
                m_smm(kt)
                if kt > 0:
                    m_av(kt - 1)
            m_av(31)
            p.emit("dve", lambda e: e.reciprocal(xf[0:1, 0:512], pd[0:1, 0:512]),
                   reads=["PD0"], writes=["xf0"], tag=f"{tag}rrec{h}")
            p.emit("te", lambda e: e.matmul(
                pb[:, 0:512], onesf[0:1, 0:128], xf[0:1, 0:512], start=True, stop=True),
                reads=["xf0", "ONESF"], writes=["PB0"], tag=f"{tag}rbc{h}")
            p.emit("dve", lambda e: e.tensor_copy(v["RB"][:, 0:512], pb[:, 0:512]),
                reads=["PB0"], writes=["mlarb"], tag=f"{tag}rbev{h}")
            p.emit("dve", lambda e, gs=gs, oband=oband: e.tensor_mul(
                v["OTB"][:, gs], pc[:, oband * 512:oband * 512 + 512], v["RB"][:, 0:512]),
                reads=[pc_reg, "mlarb"], writes=["mlaOTB"], tag=f"{tag}oev{h}")
        p.dma("sp", "ot", lambda e, h=h: e.dma_start(out=ot[h], in_=v["OTB"][:, :]),
              reads=["mlaOTB"], writes=[f"M:ot{h}"], tag=f"{tag}otst{h}")

    for oc in range(DC):
        for hq in range(NH):
            p.dma("pool", "woc", lambda e, hq=hq, oc=oc: e.dma_start(
                out=v["VB"][:, hq * 128:(hq + 1) * 128],
                in_=hbm["L0_attn_a_W_O"][hq * 128:(hq + 1) * 128, oc * 128:(oc + 1) * 128]),
                reads=["M:L0_attn_a_W_O"], writes=["mlaWOC", "mlaVB"], tag=f"{tag}ldwo{oc}")
        for g in range(NQG):
            gs = slice(g * 512, (g + 1) * 512)
            bank = g % 2
            pa_reg = f"PA{bank}"
            slot = (oc * NQG + g) % 4
            xf_reg = f"xf{slot}"
            for hq in range(NH):
                p.dma("sp", "ots", lambda e, hq=hq, g=g: e.dma_start(
                    out=v["OTB"][:, hq * 512:(hq + 1) * 512], in_=ot[hq, :, g * 512:(g + 1) * 512]),
                    reads=[f"M:ot{hq}"], writes=["mlaOTB"], tag=f"{tag}ldots")
            p.dma("sp", "xf", lambda e, oc=oc, gs=gs, slot=slot: e.dma_start(
                out=xf[:, slot * 512:(slot + 1) * 512],
                in_=hbm[master_in][oc * 128:(oc + 1) * 128, gs.start:gs.stop]),
                reads=[f"M:{master_in}"], writes=[xf_reg], tag=f"{tag}ldxf")
            p.emit("te", lambda e, oc=oc, bank=bank: e.matmul(
                pa[:, bank * 512:bank * 512 + 512], brow[0:1, oc * 128:(oc + 1) * 128],
                ones1[0:1, 0:512], start=True, stop=False),
                reads=["BROW", "ONES1"], writes=[pa_reg], tag=f"{tag}boseed{oc}")
            for hq in range(NH):
                p.emit("te", lambda e, hq=hq, bank=bank, sp=(hq == NH - 1): e.matmul(
                    pa[:, bank * 512:bank * 512 + 512], v["VB"][:, hq * 128:(hq + 1) * 128],
                    v["OTB"][:, hq * 512:(hq + 1) * 512], start=False, stop=sp),
                    reads=["mlaWOC", "mlaOTB"], writes=[pa_reg], tag=f"{tag}womm{oc}")
            p.emit("dve", lambda e, slot=slot, bank=bank: e.tensor_add(
                xf[:, slot * 512:(slot + 1) * 512], xf[:, slot * 512:(slot + 1) * 512],
                pa[:, bank * 512:bank * 512 + 512]),
                reads=[pa_reg, xf_reg], writes=[xf_reg], tag=f"{tag}radd")
            p.dma("sp", "xfo", lambda e, oc=oc, gs=gs, slot=slot: e.dma_start(
                out=hbm[master_out][oc * 128:(oc + 1) * 128, gs.start:gs.stop],
                in_=xf[:, slot * 512:(slot + 1) * 512]),
                reads=[xf_reg], writes=[f"M:{master_out}"], tag=f"{tag}stxf")
            p.emit("dve", lambda e, oc=oc, gs=gs, slot=slot: e.tensor_copy(
                xbf[:, oc * S + gs.start: oc * S + gs.stop], xf[:, slot * 512:(slot + 1) * 512]),
                reads=[xf_reg], writes=[f"xbf{oc}"], tag=f"{tag}shad")



def _l0f_views(mk):
    ar = mk.sb["AR"]
    v = {}
    v["TBL"] = ar[:, 0:16384]
    v["QT"] = ar[:, 16384:20480]
    v["KT"] = ar[:, 20480:24576]
    v["VB"] = ar[:, 24576:26656]
    v["WQ"] = ar[:, 26656:28192]
    v["ATB"] = ar[:, 28192:29216]
    v["OTB"] = ar[:, 29216:33312]
    v["SQN"] = ar[:, 33312:33824]
    v["RB"] = ar[:, 33824:34848].bitcast(F32)
    v["SH"] = ar[:, 34848:35360]
    v["TMPA"] = ar[:, 35360:35872]
    v["H4G"] = ar[:, 0:11264]
    v["W1SL"] = ar[:, 11264:15360]
    v["W2SL"] = ar[:, 16384:19200]
    v["GB"] = ar[:, 19200:19712]
    v["AB"] = ar[:, 19712:20224]
    v["GELB"] = ar[:, 20224:20736]
    v["LGC"] = ar[:, 20736:20800].bitcast(F32)
    v["LBC"] = ar[:, 20800:20816]
    v["LBC2"] = ar[:, 20816:20832]
    v["B1R"] = ar[0:1, 20864:26496]
    v["CS1"] = ar[0:1, 26496:26752]
    v["QSC"] = ar[:, 35872:35904].bitcast(F32)
    v["NC1"] = ar[0:1, 27712:27968]
    v["ROWJ"] = ar[0:1, 32160:33184]
    v["B2C"] = ar[:, 27968:28032].bitcast(F32)
    v["SDR"] = ar[0:1, 28064:32160]
    return v


def _l0f_shift_rhs(xbf, c, lo, hi):
    if c < 4:
        return xbf[:, c * S + lo: c * S + hi], 0
    if lo == 0:
        return xbf[:, c * S + 0: c * S + hi - 1], 1
    return xbf[:, c * S + lo - 1: c * S + hi - 1], 0


def l0ffn(mk, master_in, master_out, tag):
    p = mk.p
    hbm = mk.hbm
    xbf, xf, brs, sq = mk.sb["XBF"], mk.sb["XF"], mk.sb["BRS"], mk.sb["SQ"]
    row0, rowc, rowp, miscb = mk.sb["ROW0"], mk.sb["ROWC"], mk.sb["ROWP"], mk.sb["MISC"]
    onesc, ones1, ident, cmask = mk.sb["ONESC"], mk.sb["ONES1"], mk.sb["IDENT"], mk.sb["CMASK"]
    onesf = mk.sb["ONESF"]
    rmask = mk.sb["RMASK"]
    pa, pb, pc, pd, pt = mk.ps["PA"], mk.ps["PB"], mk.ps["PC"], mk.ps["PD"], mk.ps["PT"]
    v = _l0f_views(mk)
    NQG = S // 512
    NH, HD = 8, 64
    ot = hbm["ot_l0f"]
    zm = hbm["zm"]

    ln_stats(mk, tag + "zi")
    bmu = v["TBL"][:, 0:4096]
    for g in range(NQG):
        bank = g % 2
        p.emit("te", lambda e, g=g, bank=bank: e.matmul(
            pc[:, bank * 512:bank * 512 + 512], ones1[0:1, 0:128], row0[0:1, g * 512:(g + 1) * 512],
            start=True, stop=True),
            reads=["row0", "ONES1"], writes=[f"PC{bank}"], tag=f"{tag}bmumm")
        p.emit("dve", lambda e, g=g, bank=bank: e.tensor_copy(
            bmu[:, g * 512:(g + 1) * 512], pc[:, bank * 512:bank * 512 + 512]),
            reads=[f"PC{bank}"], writes=["l0tbl"], tag=f"{tag}bmuev")
    for oc in range(DC):
        for g in range(NQG):
            gs = slice(g * 512, (g + 1) * 512)
            slot = (oc * NQG + g) % 4
            xf_reg = f"xf{slot}"
            p.dma("sp", "xf", lambda e, oc=oc, gs=gs, slot=slot: e.dma_start(
                out=xf[:, slot * 512:(slot + 1) * 512],
                in_=hbm[master_in][oc * 128:(oc + 1) * 128, gs.start:gs.stop]),
                reads=[f"M:{master_in}"], writes=[xf_reg], tag=f"{tag}zl")
            p.emit("dve", lambda e, gs=gs, slot=slot: e.tensor_sub(
                xf[:, slot * 512:(slot + 1) * 512], xf[:, slot * 512:(slot + 1) * 512], bmu[:, gs]),
                reads=["l0tbl", xf_reg], writes=[xf_reg], tag=f"{tag}zsub")
            p.emit("dve", lambda e, gs=gs, slot=slot: e.tensor_mul(
                xf[:, slot * 512:(slot + 1) * 512], xf[:, slot * 512:(slot + 1) * 512], brs[:, gs]),
                reads=["BRS", xf_reg], writes=[xf_reg], tag=f"{tag}zmul")
            p.dma("sp", "zm0", lambda e, oc=oc, gs=gs, slot=slot: e.dma_start(
                out=zm[oc * 128:(oc + 1) * 128, gs.start:gs.stop],
                in_=xf[:, slot * 512:(slot + 1) * 512]),
                reads=[xf_reg], writes=["M:zm"], tag=f"{tag}zs")
            p.emit("dve", lambda e, oc=oc, gs=gs, slot=slot: e.tensor_copy(
                xbf[:, oc * S + gs.start: oc * S + gs.stop], xf[:, slot * 512:(slot + 1) * 512]),
                reads=[xf_reg], writes=[f"xbf{oc}"], tag=f"{tag}zshad")

    for li in range(2):
        lt = f"{tag}l{li}_"
        for ti, nm in enumerate(("c_l0f_cos", "c_l0f_ssin", "c_l0f_xs", "c_l0f_xsi")):
            p.dma("pool", "tbl", lambda e, ti=ti, nm=nm: e.dma_start(
                out=v["TBL"][0:64, ti * 4096:(ti + 1) * 4096], in_=hbm[nm][:, :]),
                reads=[f"M:{nm}"], writes=["l0tbl"], tag=f"{lt}tbl")
        cosl = v["TBL"][:, 0:4096]
        sinl = v["TBL"][:, 4096:8192]
        xst = v["TBL"][:, 8192:12288]
        xsi = v["TBL"][:, 12288:16384]
        p.dma("pool", "qsc", lambda e, li=li: e.dma_start(
            out=v["QSC"][0:64, 0:1], in_=hbm["L0_ffn_q_scale"][li].rearrange("(a b) -> a b", b=1)),
            reads=["M:qs"], writes=["l0qsc"], tag=f"{lt}qs")
        p.dma("pool", "qsc", lambda e, li=li: e.dma_start(
            out=v["QSC"][0:64, 1:2], in_=hbm["L0_ffn_k_scale"][li].rearrange("(a b) -> a b", b=1)),
            reads=["M:ks"], writes=["l0qsc"], tag=f"{lt}ks")
        p.emit("dve", lambda e: e.tensor_scalar(v["QSC"][0:64, 2:3], v["QSC"][0:64, 0:1],
                                                8.0, None, ALU.mult),
               reads=["l0qsc"], writes=["l0qsc"], tag=f"{lt}qs8")

        for h in range(NH):
            for c in range(DC):
                p.dma("pool", "wq0", lambda e, li=li, h=h, c=c: e.dma_start(
                    out=v["WQ"][:, c * 192:(c + 1) * 192], in_=hbm["w_l0f_qkv"][li, h, c * 128:(c + 1) * 128, :]),
                    reads=["M:w_l0f_qkv"], writes=["l0wq"], tag=f"{lt}ldw{h}")
            for g in range(NQG):
                gs = slice(g * 512, (g + 1) * 512)
                bank = g % 2
                for c in range(DC):
                    rhs, ofs = _l0f_shift_rhs(xbf, c, g * 512, (g + 1) * 512)
                    for which, psum_t, col0 in (("q", pa, 0), ("k", pb, 64), ("v", pc, 128)):
                        p.emit("te", lambda e, rhs=rhs, ofs=ofs, c=c, bank=bank, psum_t=psum_t, col0=col0: e.matmul(
                            psum_t[0:64, bank * 512 + ofs: bank * 512 + 512],
                            v["WQ"][:, c * 192 + col0: c * 192 + col0 + 64], rhs,
                            start=(c == 0), stop=(c == DC - 1)),
                            reads=["l0wq", f"xbf{c}"],
                            writes=[f"{'PA' if psum_t is pa else ('PB' if psum_t is pb else 'PC')}{bank}"],
                            tag=f"{lt}{which}mm{h}")
                p.emit("dve", lambda e, gs=gs, bank=bank: e.tensor_copy(
                    v["QT"][0:64, gs], pa[0:64, bank * 512:bank * 512 + 512]),
                    reads=[f"PA{bank}"], writes=["l0QT"], tag=f"{lt}qev{h}")
                p.emit("dve", lambda e, gs=gs, bank=bank: e.tensor_copy(
                    v["KT"][0:64, gs], pb[0:64, bank * 512:bank * 512 + 512]),
                    reads=[f"PB{bank}"], writes=["l0KT"], tag=f"{lt}kev{h}")
                p.emit("dve", lambda e, bank=bank: e.tensor_copy(
                    v["SQN"][0:64, 0:512], pc[0:64, bank * 512:bank * 512 + 512]),
                    reads=[f"PC{bank}"], writes=["l0sq"], tag=f"{lt}vev{h}")
                for b in range(4):
                    p.emit("te", lambda e, b=b: e.transpose(
                        pt[:, 0:64], v["SQN"][0:64, b * 128:(b + 1) * 128],
                        ident[0:64, 0:64]),
                        reads=["l0sq", "IDENT"], writes=["PT"], tag=f"{lt}vT{h}")
                    p.emit("dve", lambda e, g=g, b=b: e.tensor_copy(
                        v["VB"][:, (g * 4 + b) * 65: (g * 4 + b) * 65 + 64], pt[:, 0:64]),
                        reads=["PT"], writes=["l0VB"], tag=f"{lt}vTe{h}")
            p.emit("dve", lambda e: e.memset(
                v["VB"][:].rearrange("p (t c) -> p t c", c=65)[:, :, 64:65], 1.0),
                writes=["l0VB"], tag=f"{lt}vones{h}")
            for (buf, sc_col, nrm) in ((v["QT"], 2, "q"), (v["KT"], 1, "k")):
                breg = "l0QT" if nrm == "q" else "l0KT"
                for g in range(NQG):
                    gs = slice(g * 512, (g + 1) * 512)
                    p.emit("act", lambda e, buf=buf, gs=gs: e.activation(
                        out=v["SQN"][0:64, 0:512], in_=buf[0:64, gs], func=AF.Square),
                        reads=[breg], writes=["l0sq"], tag=f"{lt}n2{nrm}{h}")
                    p.emit("te", lambda e: e.matmul(
                        pd[0:1, 0:512], onesc[0:64, :], v["SQN"][0:64, 0:512], start=True, stop=True),
                        reads=["l0sq", "ONESC"], writes=["PD0"], tag=f"{lt}n2mm{nrm}{h}")
                    p.emit("act", lambda e: e.activation(
                        out=xf[0:1, 0:512], in_=pd[0:1, 0:512], func=AF.Sqrt),
                        reads=["PD0"], writes=["xf0"], tag=f"{lt}nrm{nrm}{h}")
                    p.emit("dve", lambda e: e.tensor_scalar(xf[0:1, 0:512], xf[0:1, 0:512],
                                                            1e-12, None, ALU.max),
                           reads=["xf0"], writes=["xf0"], tag=f"{lt}clip{nrm}{h}")
                    p.emit("dve", lambda e: e.reciprocal(xf[0:1, 0:512], xf[0:1, 0:512]),
                           reads=["xf0"], writes=["xf0"], tag=f"{lt}rq{nrm}{h}")
                    p.emit("te", lambda e: e.matmul(
                        pd[0:64, 0:512], onesf[0:1, 0:64], xf[0:1, 0:512], start=True, stop=True),
                        reads=["xf0", "ONESF"], writes=["PD0"], tag=f"{lt}rqbc{nrm}{h}")
                    p.emit("dve", lambda e, buf=buf, gs=gs: e.tensor_mul(
                        buf[0:64, gs], buf[0:64, gs], pd[0:64, 0:512]),
                        reads=[breg, "PD0"], writes=[breg], tag=f"{lt}nmul{nrm}{h}")
                    p.emit("dve", lambda e, buf=buf, gs=gs, sc_col=sc_col: e.tensor_scalar(
                        buf[0:64, gs], buf[0:64, gs], v["QSC"][0:64, sc_col:sc_col + 1], None, ALU.mult),
                        reads=[breg, "l0qsc"], writes=[breg], tag=f"{lt}smul{nrm}{h}")
            for g in range(NQG):
                gs = slice(g * 512, (g + 1) * 512)
                p.emit("act", lambda e, gs=gs: e.activation(
                    out=v["SQN"][0:64, 0:512], in_=v["QT"][0:64, gs], func=AF.Square),
                    reads=["l0QT"], writes=["l0sq"], tag=f"{lt}u2{h}")
                p.emit("te", lambda e: e.matmul(
                    pd[0:1, 0:512], onesc[0:64, :], v["SQN"][0:64, 0:512], start=True, stop=True),
                    reads=["l0sq", "ONESC"], writes=["PD0"], tag=f"{lt}u2mm{h}")
                p.emit("act", lambda e, gs=gs: e.activation(
                    out=rowc[0:1, gs], in_=pd[0:1, 0:512], func=AF.Sqrt),
                    reads=["PD0"], writes=["rowc"], tag=f"{lt}un{h}")
                p.emit("act", lambda e, gs=gs: e.activation(
                    out=v["SQN"][0:64, 0:512], in_=v["KT"][0:64, gs], func=AF.Square),
                    reads=["l0KT"], writes=["l0sq"], tag=f"{lt}ku2{h}")
                p.emit("te", lambda e: e.matmul(
                    pb[0:1, 0:512], onesc[0:64, :], v["SQN"][0:64, 0:512], start=True, stop=True),
                    reads=["l0sq", "ONESC"], writes=["PB0"], tag=f"{lt}ku2mm{h}")
                p.emit("dve", lambda e, g=g: e.tensor_reduce(
                    miscb[0:1, g:g + 1], pb[0:1, 0:512], AX.X, ALU.max),
                    reads=["PB0"], writes=["misc"], tag=f"{lt}kumax{h}")
            for w in range(NQG):
                lo = max(0, w - 1)
                p.emit("dve", lambda e, w=w, lo=lo: e.tensor_reduce(
                    miscb[0:1, 8 + w:9 + w], miscb[0:1, lo:w + 1], AX.X, ALU.max),
                    reads=["misc"], writes=["misc"], tag=f"{lt}mkw{h}")
            p.emit("act", lambda e: e.activation(out=miscb[0:1, 8:16], in_=miscb[0:1, 8:16], func=AF.Sqrt),
                   reads=["misc"], writes=["misc"], tag=f"{lt}mks{h}")
            p.emit("dve", lambda e: e.tensor_scalar(miscb[0:1, 8:16], miscb[0:1, 8:16], -1.0, None, ALU.mult),
                   reads=["misc"], writes=["misc"], tag=f"{lt}mkn{h}")
            for w in range(NQG):
                gs = slice(w * 512, (w + 1) * 512)
                p.emit("dve", lambda e, gs=gs, w=w: e.tensor_scalar(
                    v["QT"][64:65, gs], rowc[0:1, gs], miscb[0:1, 8 + w:9 + w], NEG_MARGIN,
                    ALU.mult, ALU.add),
                    reads=["rowc", "misc"], writes=["l0negc"], tag=f"{lt}negc{h}")
            p.emit("dve", lambda e: e.memset(v["KT"][64:65, :], 1.0), writes=["l0kones"], tag=f"{lt}ko{h}")
            for (buf, breg, xtab) in ((v["QT"], "l0QT", xst), (v["KT"], "l0KT", xsi)):
                for g in range(NQG):
                    gs = slice(g * 512, (g + 1) * 512)
                    p.emit("dve", lambda e, buf=buf, gs=gs: e.tensor_copy(
                        v["SH"][0:32, :], buf[32:64, gs]),
                        reads=[breg], writes=["l0sh"], tag=f"{lt}r1{h}")
                    p.emit("dve", lambda e, buf=buf, gs=gs: e.tensor_copy(
                        v["SH"][32:64, :], buf[0:32, gs]),
                        reads=[breg], writes=["l0sh"], tag=f"{lt}r2{h}")
                    p.emit("dve", lambda e, buf=buf, gs=gs: e.tensor_mul(
                        v["TMPA"][0:64, :], buf[0:64, gs], cosl[0:64, gs]),
                        reads=[breg, "l0tbl"], writes=["l0tmpa"], tag=f"{lt}r3{h}")
                    p.emit("dve", lambda e, gs=gs: e.tensor_mul(
                        v["SH"][0:64, :], v["SH"][0:64, :], sinl[0:64, gs]),
                        reads=["l0sh", "l0tbl"], writes=["l0sh"], tag=f"{lt}r4{h}")
                    p.emit("dve", lambda e, buf=buf, gs=gs: e.tensor_add(
                        buf[0:64, gs], v["TMPA"][0:64, :], v["SH"][0:64, :]),
                        reads=["l0tmpa", "l0sh"], writes=[breg], tag=f"{lt}r5{h}")
                    p.emit("dve", lambda e, buf=buf, gs=gs, xtab=xtab: e.tensor_mul(
                        buf[0:64, gs], buf[0:64, gs], xtab[0:64, gs]),
                        reads=[breg, "l0tbl"], writes=[breg], tag=f"{lt}r6{h}")
            for w in range(NQG):
                gs = slice(w * 512, (w + 1) * 512)
                oband = w % 2
                pc_reg = f"PC{oband}"
                kts = []
                if w > 0:
                    kts += [("p", j, (w - 1) * 4 + j) for j in range(4)]
                kts += [("c", j, w * 4 + j) for j in range(4)]
                def f_smm(i, pk, j, kt):
                    bank = i % 2
                    pa_reg = f"PA{bank}"
                    p.emit("te", lambda e, kt=kt, gs=gs, bank=bank: e.matmul(
                        pa[:, bank * 512:bank * 512 + 512], v["KT"][0:65, kt * 128:(kt + 1) * 128],
                        v["QT"][0:65, gs], start=True, stop=True),
                        reads=["l0KT", "l0kones", "l0QT", "l0negc"], writes=[pa_reg], tag=f"{lt}smm{h}")
                    p.emit("act", lambda e, bank=bank: e.activation(
                        out=v["ATB"][:, bank * 512:bank * 512 + 512],
                        in_=pa[:, bank * 512:bank * 512 + 512], func=AF.Exp),
                        reads=[pa_reg], writes=[f"l0atb{bank}"], tag=f"{lt}exp{h}")
                    msk = rmask if pk == "p" else cmask
                    p.emit("dve", lambda e, bank=bank, j=j, msk=msk: e.tensor_mul(
                        v["ATB"][:, bank * 512:bank * 512 + 512],
                        v["ATB"][:, bank * 512:bank * 512 + 512],
                        msk[:, 384 - 128 * j: 896 - 128 * j]),
                        reads=[f"l0atb{bank}", "CMASK", "RMASK"], writes=[f"l0atb{bank}"],
                        tag=f"{lt}msk{h}")

                def f_av(i, kt):
                    bank = i % 2
                    p.emit("te", lambda e, kt=kt, bank=bank, oband=oband, st=(i == 0), sp=(i == len(kts) - 1): e.matmul(
                        pc[0:65, oband * 512:oband * 512 + 512], v["VB"][:, kt * 65:kt * 65 + 65],
                        v["ATB"][:, bank * 512:bank * 512 + 512], start=st, stop=sp),
                        reads=["l0VB", f"l0atb{bank}"], writes=[pc_reg], tag=f"{lt}av{h}")

                for i, (pk, j, kt) in enumerate(kts):
                    f_smm(i, pk, j, kt)
                    if i > 0:
                        f_av(i - 1, kts[i - 1][2])
                f_av(len(kts) - 1, kts[-1][2])
                p.emit("dve", lambda e, oband=oband: e.reciprocal(
                    xf[0:1, 0:512], pc[64:65, oband * 512:oband * 512 + 512]),
                    reads=[pc_reg], writes=["xf0"], tag=f"{lt}rrec{h}")
                p.emit("te", lambda e: e.matmul(
                    pd[0:64, 0:512], onesf[0:1, 0:64], xf[0:1, 0:512], start=True, stop=True),
                    reads=["xf0", "ONESF"], writes=["PD0"], tag=f"{lt}rbc{h}")
                p.emit("dve", lambda e: e.tensor_copy(
                    v["RB"][0:64, 0:512], pd[0:64, 0:512]),
                    reads=["PD0"], writes=["l0rb"], tag=f"{lt}rbev{h}")
                p.emit("dve", lambda e, gs=gs, oband=oband: e.tensor_mul(
                    v["OTB"][0:64, gs], pc[0:64, oband * 512:oband * 512 + 512],
                    v["RB"][0:64, 0:512]),
                    reads=[pc_reg, "l0rb"], writes=["l0OTB"], tag=f"{lt}oev{h}")
            p.dma("sp", "ot", lambda e, h=h: e.dma_start(out=ot[h], in_=v["OTB"][0:64, :]),
                  reads=["l0OTB"], writes=[f"M:l0ot{h}"], tag=f"{lt}otst{h}")

        for oc in range(DC):
            for pr in range(4):
                p.dma("pool", "woc", lambda e, li=li, pr=pr, oc=oc: e.dma_start(
                    out=v["WQ"][:, pr * 128:(pr + 1) * 128],
                    in_=hbm["L0_ffn_wo"][li, pr * 128:(pr + 1) * 128, oc * 128:(oc + 1) * 128]),
                    reads=["M:L0_ffn_wo"], writes=["l0wq"], tag=f"{lt}ldwo{oc}")
            for g in range(NQG):
                gs = slice(g * 512, (g + 1) * 512)
                bank = g % 2
                pa_reg = f"PA{bank}"
                slot = (oc * NQG + g) % 4
                xf_reg = f"xf{slot}"
                for hh in range(4):
                    p.dma("sp", "ots", lambda e, hh=hh, g=g: e.dma_start(
                        out=v["OTB"][:, hh * 512:(hh + 1) * 512],
                        in_=ot[2 * hh:2 * hh + 2, :, g * 512:(g + 1) * 512]),
                        reads=[f"M:l0ot{2 * hh}", f"M:l0ot{2 * hh + 1}"], writes=["l0OTB"],
                        tag=f"{lt}ldots")
                p.dma("sp", "xf", lambda e, oc=oc, gs=gs, slot=slot: e.dma_start(
                    out=xf[:, slot * 512:(slot + 1) * 512],
                    in_=zm[oc * 128:(oc + 1) * 128, gs.start:gs.stop]),
                    reads=["M:zm"], writes=[xf_reg], tag=f"{lt}ldxf")
                for hh in range(4):
                    p.emit("te", lambda e, hh=hh, bank=bank, st=(hh == 0), sp=(hh == 3): e.matmul(
                        pa[:, bank * 512:bank * 512 + 512], v["WQ"][:, hh * 128:(hh + 1) * 128],
                        v["OTB"][:, hh * 512:(hh + 1) * 512], start=st, stop=sp),
                        reads=["l0wq", "l0OTB"], writes=[pa_reg], tag=f"{lt}womm{oc}")
                p.emit("dve", lambda e, slot=slot, bank=bank: e.tensor_add(
                    xf[:, slot * 512:(slot + 1) * 512], xf[:, slot * 512:(slot + 1) * 512],
                    pa[:, bank * 512:bank * 512 + 512]),
                    reads=[pa_reg, xf_reg], writes=[xf_reg], tag=f"{lt}radd")
                p.dma("sp", "zmo", lambda e, oc=oc, gs=gs, slot=slot: e.dma_start(
                    out=zm[oc * 128:(oc + 1) * 128, gs.start:gs.stop],
                    in_=xf[:, slot * 512:(slot + 1) * 512]),
                    reads=[xf_reg], writes=["M:zm"], tag=f"{lt}stzm")
                p.emit("dve", lambda e, oc=oc, gs=gs, slot=slot: e.tensor_copy(
                    xbf[:, oc * S + gs.start: oc * S + gs.stop], xf[:, slot * 512:(slot + 1) * 512]),
                    reads=[xf_reg], writes=[f"xbf{oc}"], tag=f"{lt}shad")

        if getattr(mk, "l0f_debug", False):
            dbga = mk.dram_out(f"dbga{li}", (D, S))
            for c in range(DC):
                p.dma("sp", f"dbga{li}", lambda e, c=c, dbga=dbga: e.dma_start(
                    out=dbga[c * 128:(c + 1) * 128, :], in_=zm[c * 128:(c + 1) * 128, :]),
                    reads=["M:zm"], writes=[f"M:dbga{li}"], tag=f"dbga{li}")
        p.barrier()
        p.dma("sp", "lg", lambda e, li=li: e.dma_start(
            out=v["LGC"][:, 0:8], in_=hbm["lg_c"][li]),
            reads=["M:lg_c"], writes=["l0lg"], tag=f"{lt}lg")
        p.dma("pool", "lb", lambda e, li=li: e.dma_start(
            out=v["LBC"][:, 0:8], in_=hbm["lb_c"][li]),
            reads=["M:lb_c"], writes=["l0lb"], tag=f"{lt}lb")
        p.dma("pool", "b1r", lambda e, li=li: e.dma_start(
            out=v["B1R"][0:1, 0:5632], in_=hbm["b1p"][li].rearrange("(a b) -> a b", a=1)),
            reads=["M:b1p"], writes=["l0b1"], tag=f"{lt}b1")
        sdr = v["SDR"]
        p.dma("sp", "lbf", lambda e, li=li: e.dma_start(
            out=v["LGC"][:, 24:32], in_=hbm["lb_c"][li]),
            reads=["M:lb_c"], writes=["l0lbf"], tag=f"{lt}lbf")
        p.emit("dve", lambda e: e.reciprocal(v["LGC"][:, 16:24], v["LGC"][:, 0:8]),
               reads=["l0lg"], writes=["l0lbg"], tag=f"{lt}lgrec")
        p.emit("dve", lambda e: e.tensor_mul(v["LGC"][:, 16:24], v["LGC"][:, 16:24],
                                             v["LGC"][:, 24:32]),
               reads=["l0lbg", "l0lbf"], writes=["l0lbg"], tag=f"{lt}lbdiv")
        p.emit("dve", lambda e: e.tensor_copy(v["LBC2"][:, 0:8], v["LGC"][:, 16:24]),
               reads=["l0lbg"], writes=["l0lbg2"], tag=f"{lt}lbgb")
        for j in range(22):
            wbuf = j % 2
            wreg = f"l0w1_{wbuf}"
            for c in range(DC):
                p.dma("sp", "w1", lambda e, li=li, j=j, c=c, wbuf=wbuf: e.dma_start(
                    out=v["W1SL"][:, wbuf * 2048 + c * 256: wbuf * 2048 + (c + 1) * 256],
                    in_=hbm["w1b"][li, j, c * 128:(c + 1) * 128, :]),
                    reads=["M:w1b"], writes=[wreg], tag=f"{lt}pldw1")
            for c in range(DC):
                p.emit("te", lambda e, c=c, wbuf=wbuf: e.matmul(
                    pd[0:1, 0:256], v["LBC2"][:, c:c + 1],
                    v["W1SL"][:, wbuf * 2048 + c * 256: wbuf * 2048 + (c + 1) * 256],
                    start=(c == 0), stop=(c == DC - 1)),
                    reads=[wreg, "l0lbg2"], writes=["PD0"], tag=f"{lt}plbw1")
            p.emit("dve", lambda e, j=j: e.scalar_tensor_tensor(
                v["CS1"][0:1, 0:256], pd[0:1, 0:256], 1.0, v["B1R"][0:1, j * 256:(j + 1) * 256],
                ALU.mult, ALU.add),
                reads=["PD0", "l0b1"], writes=["l0cs1"], tag=f"{lt}pseedrow")
            for c in range(DC):
                p.emit("te", lambda e, c=c, wbuf=wbuf: e.matmul(
                    pd[0:1, 256:512], onesc[:, :],
                    v["W1SL"][:, wbuf * 2048 + c * 256: wbuf * 2048 + (c + 1) * 256],
                    start=(c == 0), stop=(c == DC - 1)),
                    reads=[wreg, "ONESC"], writes=["PD0"], tag=f"{lt}pcs1")
            p.emit("act", lambda e: e.activation(out=v["NC1"][0:1, 0:256], in_=pd[0:1, 256:512],
                                                 func=AF.Identity, scale=-1.0),
                   reads=["PD0"], writes=["l0nc1"], tag=f"{lt}pnegcs")
            p.dma("pool", "l0rows", lambda e, li=li, j=j: e.dma_start(
                out=hbm["l0rows"][li, j, 0:1, 0:256], in_=v["NC1"][0:1, 0:256]),
                reads=["l0nc1"], writes=["M:l0rows"], tag=f"{lt}prow1")
            p.dma("pool", "l0rows2", lambda e, li=li, j=j: e.dma_start(
                out=hbm["l0rows"][li, j, 1:2, 0:256], in_=v["CS1"][0:1, 0:256]),
                reads=["l0cs1"], writes=["M:l0rows"], tag=f"{lt}prow2")
        ln_stats(mk, lt + "f", shifted=True, want_sd=sdr)
        if getattr(mk, "l0f_debug", False) and li == 0:
            dbs = mk.dram_out("dbgs", (2, 1, S))
            p.dma("pool", "dbgs", lambda e, dbs=dbs: e.dma_start(
                out=dbs[0], in_=mk.sb["ROW0"][0:1, :]),
                reads=["row0"], writes=["M:dbgs"], tag="dbgs_mu")
            dbx = mk.dram_out("dbgx", (128, 16), BF16)
            p.dma("sp", "dbgx", lambda e, dbx=dbx: e.dma_start(
                out=dbx[:, :], in_=mk.sb["XBF"][:, 4 * S + 504: 4 * S + 520]),
                reads=["xbf4"], writes=["M:dbgx"], tag="dbgx")
            dbsd = mk.dram_out("dbgsd", (1, S), BF16)
            p.dma("sp", "dbgsd", lambda e, dbsd=dbsd: e.dma_start(
                out=dbsd[0:1, :], in_=v["SDR"][0:1, :]),
                reads=["l0sdr"], writes=["M:dbgsd"], tag="dbgsd")
            dbb = mk.dram_out("dbgb", (128, 8))
            p.dma("sp", "dbgb", lambda e, dbb=dbb: e.dma_start(
                out=dbb[:, :], in_=mk.sb["BRS"][:, 509:517]),
                reads=["BRS"], writes=["M:dbgb"], tag="dbgb")
            p.dma("sp", "dbgs", lambda e, dbs=dbs: e.dma_start(
                out=dbs[1], in_=mk.sb["BRS"][0:1, :]),
                reads=["BRS"], writes=["M:dbgs"], tag="dbgs_rs")
        p.barrier()
        p.dma("sp", "b2c", lambda e, li=li: e.dma_start(
            out=v["B2C"][:, 0:8], in_=hbm["b2_c"][li]),
            reads=["M:b2_c"], writes=["l0b2c"], tag=f"{lt}b2")
        NJ = 22
        for tg in range(NQG):
            gs = slice(tg * 512, (tg + 1) * 512)
            for j in range(NJ):
                wbuf = j % 2
                wreg = f"l0w1_{wbuf}"
                for c in range(DC):
                    p.dma("sp", "w1", lambda e, li=li, j=j, c=c, wbuf=wbuf: e.dma_start(
                        out=v["W1SL"][:, wbuf * 2048 + c * 256: wbuf * 2048 + (c + 1) * 256],
                        in_=hbm["w1b"][li, j, c * 128:(c + 1) * 128, :]),
                        reads=["M:w1b"], writes=[wreg], tag=f"{lt}ldw1")
                rbuf = j % 2
                p.dma("pool", "l0rowld", lambda e, li=li, j=j, rbuf=rbuf: e.dma_start(
                    out=v["ROWJ"][0:1, rbuf * 512:(rbuf + 1) * 512],
                    in_=hbm["l0rows"][li, j].rearrange("a b -> (a b)").rearrange("(x y) -> x y", x=1)),
                    reads=["M:l0rows"], writes=[f"l0rowj{rbuf}"], tag=f"{lt}rowld")
                bank = j % 2
                for half, psum_t, preg in ((0, pa, f"PA{bank}"), (1, pb, f"PB{bank}")):
                    p.emit("te", lambda e, half=half, gs=gs, bank=bank, psum_t=psum_t, rbuf=rbuf: e.matmul(
                        psum_t[:, bank * 512:bank * 512 + 512],
                        v["ROWJ"][0:1, rbuf * 512 + half * 128: rbuf * 512 + (half + 1) * 128],
                        row0[0:1, gs],
                        start=True, stop=False),
                        reads=[f"l0rowj{rbuf}", "row0"], writes=[preg], tag=f"{lt}seed1")
                    p.emit("te", lambda e, half=half, gs=gs, bank=bank, psum_t=psum_t, rbuf=rbuf: e.matmul(
                        psum_t[:, bank * 512:bank * 512 + 512],
                        v["ROWJ"][0:1, rbuf * 512 + 256 + half * 128: rbuf * 512 + 256 + (half + 1) * 128],
                        sdr[0:1, gs],
                        start=False, stop=False),
                        reads=[f"l0rowj{rbuf}", "l0sdr"], writes=[preg], tag=f"{lt}seed2")
                    for c in range(DC):
                        rhs, ofs = _l0f_shift_rhs(xbf, c, gs.start, gs.stop)
                        p.emit("te", lambda e, rhs=rhs, ofs=ofs, c=c, half=half, bank=bank, psum_t=psum_t, wbuf=wbuf: e.matmul(
                            psum_t[:, bank * 512 + ofs: bank * 512 + 512],
                            v["W1SL"][:, wbuf * 2048 + c * 256 + half * 128: wbuf * 2048 + c * 256 + half * 128 + 128],
                            rhs, start=False, stop=(c == DC - 1)),
                            reads=[wreg, f"xbf{c}"], writes=[preg], tag=f"{lt}h3mm")
                if getattr(mk, "l0f_debug", False) and li == 0 and tg == 1 and j == 0:
                    dbp = mk.dram_out("dbgp", (128, 512))
                    p.emit("dve", lambda e, bank=bank: e.tensor_copy(
                        v["XNAT" if False else "GELB"][:, 0:512], pa[:, bank * 512:bank * 512 + 512]),
                        reads=[f"PA{bank}"], writes=["l0gelb"], tag="dbgp_ev")
                    p.dma("pool", "dbgp", lambda e, dbp=dbp: e.dma_start(
                        out=dbp[:, :], in_=v["GELB"][:, 0:512].bitcast(F32)[:, 0:256] if False else v["GELB"][:, 0:512]),
                        reads=["l0gelb"], writes=["M:dbgp"], tag="dbgp")
                p.emit("dve", lambda e, gs=gs, bank=bank: e.tensor_mul(
                    v["AB"][:, 0:512], pa[:, bank * 512:bank * 512 + 512], brs[:, gs]),
                    reads=[f"PA{bank}", "BRS"], writes=["l0ab"], tag=f"{lt}aev")
                p.emit("dve", lambda e, gs=gs, bank=bank: e.tensor_mul(
                    v["GB"][:, 0:512], pb[:, bank * 512:bank * 512 + 512], brs[:, gs]),
                    reads=[f"PB{bank}", "BRS"], writes=["l0gb"], tag=f"{lt}gev")
                p.emit("act", lambda e: e.activation(out=v["GELB"][:, 0:512], in_=v["GB"][:, 0:512],
                                                     func=AF.Gelu),
                       reads=["l0gb"], writes=["l0gelb"], tag=f"{lt}gelu")
                p.emit("dve", lambda e, j=j: e.tensor_mul(
                    v["H4G"][:, j * 512:(j + 1) * 512], v["AB"][:, 0:512], v["GELB"][:, 0:512]),
                    reads=["l0ab", "l0gelb"], writes=["l0h4g"], tag=f"{lt}h4")
                if getattr(mk, "l0f_debug", False) and li == 0 and tg == 1 and j == 0:
                    dbh = mk.dram_out("dbgh", (2, 128, 512), BF16)
                    p.dma("sp", "dbgh", lambda e, dbh=dbh: e.dma_start(
                        out=dbh[0], in_=v["AB"][:, 0:512]),
                        reads=["l0ab"], writes=["M:dbgh"], tag="dbgh_a")
                    p.dma("sp", "dbgh", lambda e, dbh=dbh: e.dma_start(
                        out=dbh[1], in_=v["GB"][:, 0:512]),
                        reads=["l0gb"], writes=["M:dbgh"], tag="dbgh_g")
            for oh in range(2):
                pbanks = [("PC", 0), ("PC", 1), ("PD", 0), ("PA", 0)]
                for j in range(NJ):
                    wbuf = j % 2
                    w2reg = f"l0w2_{wbuf}"
                    p.dma("sp", "w2", lambda e, li=li, j=j, oh=oh, wbuf=wbuf: e.dma_start(
                        out=v["W2SL"][:, wbuf * 512:(wbuf + 1) * 512],
                        in_=hbm["w2b"][li, j, :, oh * 512:(oh + 1) * 512]),
                        reads=["M:w2b"], writes=[w2reg], tag=f"{lt}ldw2")
                    for q4 in range(4):
                        pn, pk = pbanks[q4]
                        psum_t = mk.ps[pn]
                        p.emit("te", lambda e, j=j, q4=q4, wbuf=wbuf, psum_t=psum_t, pk=pk: e.matmul(
                            psum_t[:, pk * 512:pk * 512 + 512],
                            v["W2SL"][:, wbuf * 512 + q4 * 128: wbuf * 512 + (q4 + 1) * 128],
                            v["H4G"][:, j * 512:(j + 1) * 512],
                            start=(j == 0), stop=(j == NJ - 1)),
                            reads=[w2reg, "l0h4g"], writes=[f"{pn}{pk}"], tag=f"{lt}w2mm")
                for q4 in range(4):
                    oc = oh * 4 + q4
                    pn, pk = pbanks[q4]
                    psum_t = mk.ps[pn]
                    slot = q4
                    xf_reg = f"xf{slot}"
                    p.dma("sp", "xf", lambda e, oc=oc, gs=gs, slot=slot: e.dma_start(
                        out=xf[:, slot * 512:(slot + 1) * 512],
                        in_=zm[oc * 128:(oc + 1) * 128, gs.start:gs.stop]),
                        reads=["M:zm"], writes=[xf_reg], tag=f"{lt}f_ldxf")
                    p.emit("dve", lambda e, slot=slot, psum_t=psum_t, pk=pk: e.tensor_add(
                        xf[:, slot * 512:(slot + 1) * 512], xf[:, slot * 512:(slot + 1) * 512],
                        psum_t[:, pk * 512:pk * 512 + 512]),
                        reads=[f"{pn}{pk}", xf_reg], writes=[xf_reg], tag=f"{lt}f_radd")
                    p.emit("dve", lambda e, slot=slot, oc=oc: e.tensor_scalar(
                        xf[:, slot * 512:(slot + 1) * 512], xf[:, slot * 512:(slot + 1) * 512],
                        v["B2C"][:, oc:oc + 1], None, ALU.add),
                        reads=[xf_reg, "l0b2c"], writes=[xf_reg], tag=f"{lt}f_b2")
                    p.dma("sp", "zmo", lambda e, oc=oc, gs=gs, slot=slot: e.dma_start(
                        out=zm[oc * 128:(oc + 1) * 128, gs.start:gs.stop],
                        in_=xf[:, slot * 512:(slot + 1) * 512]),
                        reads=[xf_reg], writes=["M:zm"], tag=f"{lt}f_stzm")
        for c in range(DC):
            p.dma("pool", "xbf", lambda e, c=c: e.dma_start(
                out=xbf[:, c * S:(c + 1) * S], in_=zm[c * 128:(c + 1) * 128, :]),
                reads=["M:zm"], writes=[f"xbf{c}"], tag=f"{lt}f_shadr{c}")
        if getattr(mk, "l0f_debug", False):
            dbgf = mk.dram_out(f"dbgf{li}", (D, S))
            for c in range(DC):
                p.dma("sp", f"dbgf{li}", lambda e, c=c, dbgf=dbgf: e.dma_start(
                    out=dbgf[c * 128:(c + 1) * 128, :], in_=zm[c * 128:(c + 1) * 128, :]),
                    reads=["M:zm"], writes=[f"M:dbgf{li}"], tag=f"dbgf{li}")
        p.barrier()

    for oc in range(DC):
        for g in range(NQG):
            gs = slice(g * 512, (g + 1) * 512)
            slot = (oc * NQG + g) % 2
            slot2 = 2 + slot
            xf_reg, xf2_reg = f"xf{slot}", f"xf{slot2}"
            p.dma("sp", "xf", lambda e, oc=oc, gs=gs, slot=slot: e.dma_start(
                out=xf[:, slot * 512:(slot + 1) * 512],
                in_=hbm[master_in][oc * 128:(oc + 1) * 128, gs.start:gs.stop]),
                reads=[f"M:{master_in}"], writes=[xf_reg], tag=f"{tag}fin_ldx")
            p.dma("sp", "xf2", lambda e, oc=oc, gs=gs, slot2=slot2: e.dma_start(
                out=xf[:, slot2 * 512:(slot2 + 1) * 512],
                in_=zm[oc * 128:(oc + 1) * 128, gs.start:gs.stop]),
                reads=["M:zm"], writes=[xf2_reg], tag=f"{tag}fin_ldz")
            p.emit("dve", lambda e, slot=slot, slot2=slot2: e.tensor_add(
                xf[:, slot * 512:(slot + 1) * 512], xf[:, slot * 512:(slot + 1) * 512],
                xf[:, slot2 * 512:(slot2 + 1) * 512]),
                reads=[xf_reg, xf2_reg], writes=[xf_reg], tag=f"{tag}fin_add")
            p.dma("sp", "xfo", lambda e, oc=oc, gs=gs, slot=slot: e.dma_start(
                out=hbm[master_out][oc * 128:(oc + 1) * 128, gs.start:gs.stop],
                in_=xf[:, slot * 512:(slot + 1) * 512]),
                reads=[xf_reg], writes=[f"M:{master_out}"], tag=f"{tag}fin_st")
            p.emit("dve", lambda e, oc=oc, gs=gs, slot=slot: e.tensor_copy(
                xbf[:, oc * S + gs.start: oc * S + gs.stop], xf[:, slot * 512:(slot + 1) * 512]),
                reads=[xf_reg], writes=[f"xbf{oc}"], tag=f"{tag}fin_shad")


def l0f_tables():
    hd = 64
    half = np.arange(0, hd, 2, dtype=np.float64)
    inv = 1.0 / 10000.0 ** (half / hd)
    t = np.arange(S, dtype=np.float64)
    freqs = t[None, :] * inv[:, None]
    cos = np.cos(freqs)
    sin = np.sin(freqs)
    base = (half + 0.4 * hd) / (1.4 * hd)
    power = (t - S // 2) / (512 // 2)
    scale = base[:, None] ** power[None, :]
    cosl = np.concatenate([cos, cos], 0).astype(np.float32)
    sinls = np.concatenate([-sin, sin], 0).astype(np.float32)
    xs = np.concatenate([scale, scale], 0).astype(np.float32)
    xsi = np.concatenate([1.0 / scale, 1.0 / scale], 0).astype(np.float32)
    return {"c_l0f_cos": cosl, "c_l0f_ssin": sinls, "c_l0f_xs": xs, "c_l0f_xsi": xsi}


def l0f_precast(mk):
    p = mk.p
    hbm = mk.hbm
    xf = mk.sb["XF"]
    ar = mk.sb["AR"]
    stage = ar[:, 0:2048]
    lgc = ar[:, 2048:2176].bitcast(F32)
    for li in range(2):
        p.dma("sp", "lgall", lambda e, li=li: e.dma_start(
            out=lgc[:, li * 8:(li + 1) * 8], in_=hbm["lg_c"][li]),
            reads=["M:lg_c"], writes=["l0lgall"], tag="pc_lg")
    w1p4 = hbm["w1p"].rearrange("l j (r p) o -> (l j r) p o", p=128)
    w1b4 = hbm["w1b"].rearrange("l j (r p) o -> (l j r) p o", p=128)
    for i in range(2 * 22 * 8):
        sl = i % 4
        li, r = i // 176, i % 8
        p.dma("sp", "pc1", lambda e, i=i, sl=sl: e.dma_start(
            out=xf[:, sl * 512:sl * 512 + 256], in_=w1p4[i]),
            reads=["M:w1p"], writes=[f"xf{sl}"], tag="pc1l")
        p.emit("dve", lambda e, i=i, sl=sl, li=li, r=r: e.tensor_scalar(
            stage[:, (i % 4) * 512: (i % 4) * 512 + 256], xf[:, sl * 512:sl * 512 + 256],
            lgc[:, li * 8 + r: li * 8 + r + 1], None, ALU.mult),
            reads=[f"xf{sl}", "l0lgall"], writes=[f"l0pc{i % 4}"], tag="pc1c")
        p.dma("sp", "pc1o", lambda e, i=i: e.dma_start(
            out=w1b4[i], in_=stage[:, (i % 4) * 512: (i % 4) * 512 + 256]),
            reads=[f"l0pc{i % 4}"], writes=["M:w1b"], tag="pc1s")
    w2p4 = hbm["w2p"].rearrange("l j p o -> (l j) p o")
    w2b4 = hbm["w2b"].rearrange("l j p o -> (l j) p o")
    for i in range(44):
        sl = i % 2
        p.dma("sp", "pc2", lambda e, i=i, sl=sl: e.dma_start(
            out=xf[:, sl * 1024:(sl + 1) * 1024], in_=w2p4[i]),
            reads=["M:w2p"], writes=[f"xf{2 * sl}", f"xf{2 * sl + 1}"], tag="pc2l")
        p.emit("dve", lambda e, i=i, sl=sl: e.tensor_copy(
            stage[:, sl * 1024:(sl + 1) * 1024],
            xf[:, sl * 1024:(sl + 1) * 1024]),
            reads=[f"xf{2 * sl}", f"xf{2 * sl + 1}"], writes=[f"l0pc{sl}"], tag="pc2c")
        p.dma("sp", "pc2o", lambda e, i=i, sl=sl: e.dma_start(
            out=w2b4[i], in_=stage[:, sl * 1024:(sl + 1) * 1024]),
            reads=[f"l0pc{sl}"], writes=["M:w2b"], tag="pc2s")
    p.barrier()


# version 07 (final)
# speedup vs baseline: 1.1768x; 1.1768x over previous
import contextlib
import math
import numpy as np
import ml_dtypes

import concourse.bass as bass
import concourse.mybir as mybir
from concourse.bass_utils import run_bass_kernel_spmd

F32 = mybir.dt.float32
BF16 = mybir.dt.bfloat16
AF = mybir.ActivationFunctionType
ALU = mybir.AluOpType
AX = mybir.AxisListType

NCORES = 8
S = 4096
D = 1024
DC = D // 128
ST = S // 128
NEG_MARGIN = 50.0


ENGINES = ("te", "act", "dve", "pool", "sp")
ENG_ATTR = {"te": "tensor", "act": "scalar", "dve": "vector", "pool": "gpsimd", "sp": "sync"}


class Region:
    __slots__ = ("name", "writer", "readers")

    def __init__(self, name):
        self.name = name
        self.writer = None
        self.readers = {}


class Prog:
    def __init__(self, nc, stack, same_engine_sync=True):
        self.nc = nc
        self.stack = stack
        self.same_engine_sync = same_engine_sync
        self.sems = {}
        self.count = {}
        self.known = {e: {} for e in ENGINES}
        self.items = {e: [] for e in ENGINES}
        self.regions = {}
        for e in ENGINES:
            self.sems[e] = stack.enter_context(nc.semaphore(f"sem_{e}"))
            self.count[e] = 0
        self.n_instr = 0

    def region(self, name):
        if name not in self.regions:
            self.regions[name] = Region(name)
        return self.regions[name]

    def dma_counter(self, name):
        cname = "dma_" + name
        if cname not in self.sems:
            self.sems[cname] = self.stack.enter_context(self.nc.semaphore(cname))
            self.count[cname] = 0
        return cname

    def _deps_for(self, engine, reads, writes, is_dma):
        deps = {}

        def add(cv):
            if cv is None:
                return
            c, v = cv
            if deps.get(c, 0) < v:
                deps[c] = v

        for r in reads:
            add(r.writer)
        for r in writes:
            add(r.writer)
            for c, v in r.readers.items():
                add((c, v))
        known = self.known[engine]
        out = {}
        for c, v in deps.items():
            if c == engine and not is_dma:
                if not self.same_engine_sync or engine == "te":
                    continue
            if c.startswith("dma_"):
                v = max(v, self.count[c])
            if known.get(c, 0) >= v:
                continue
            out[c] = v
        return out

    def _note(self, counter, value, reads, writes):
        for r in reads:
            if r.readers.get(counter, 0) < value:
                r.readers[counter] = value
        for r in writes:
            r.writer = (counter, value)
            r.readers = {}

    def _regs(self, lst):
        return [self.region(r) if isinstance(r, str) else r for r in lst]

    def emit(self, engine, fn, reads=(), writes=(), tag=""):
        reads = self._regs(reads)
        writes = self._regs(writes)
        deps = self._deps_for(engine, reads, writes, False)
        for c, v in deps.items():
            self.known[engine][c] = v
        self.count[engine] += 1
        val = self.count[engine]
        self._note(engine, val, reads, writes)
        self.items[engine].append(("op", fn, deps, None, tag))
        self.n_instr += 1
        return (engine, val)

    def dma(self, queue_engine, sem_name, fn, reads=(), writes=(), tag="", ckey=None):
        reads = self._regs(reads)
        writes = self._regs(writes)
        key = ckey if ckey is not None else (writes[0].name if writes else sem_name)
        cname = self.dma_counter(queue_engine + "_" + key)
        deps = self._deps_for(queue_engine, reads, writes, True)
        for c, v in deps.items():
            self.known[queue_engine][c] = v
        self.count[cname] += 16
        val = self.count[cname]
        self._note(cname, val, reads, writes)
        self.items[queue_engine].append(("dma", fn, deps, cname, tag))
        self.n_instr += 1
        return (cname, val)

    def barrier(self):
        snapshot = dict(self.count)
        for e in ENGINES:
            deps = {}
            for c, v in snapshot.items():
                if v > 0 and c != e and self.known[e].get(c, 0) < v:
                    deps[c] = v
                    self.known[e][c] = v
            if deps:
                self.items[e].append(("wait", None, deps, None, "barrier"))

    def wait_all_dmas(self, engine):
        for cname, v in self.count.items():
            if cname.startswith("dma_") and v > 0:
                if self.known[engine].get(cname, 0) < v:
                    self.known[engine][cname] = v
                    self.items[engine].append(("wait", None, {cname: v}, None, "final"))

    def finalize_into_block(self, block):
        sems = self.sems

        def make_replay(engine):
            items = self.items[engine]
            my_sem = sems[engine]

            def replay(eng_obj):
                for kind, fn, deps, cname, tag in items:
                    if kind == "wait":
                        for c, v in deps.items():
                            eng_obj.wait_ge(sems[c], v)
                        continue
                    for c, v in deps.items():
                        eng_obj.wait_ge(sems[c], v)
                    inst = fn(eng_obj)
                    if tag:
                        inst.annotate(tag)
                    if kind == "op":
                        inst.then_inc(my_sem, 1)
                    else:
                        inst.then_inc(sems[cname], 16)
            return replay

        for engine in ENGINES:
            if self.items[engine]:
                getattr(block, ENG_ATTR[engine])(make_replay(engine))



def to_bf16(x):
    return np.asarray(x).astype(ml_dtypes.bfloat16)



class MK:

    def __init__(self, nc, prog):
        self.nc = nc
        self.p = prog
        self.hbm = {}
        self.sb = {}
        self.ps = {}
        self.host_inputs = {}

    def dram_in(self, name, shape, dtype=F32):
        t = self.nc.dram_tensor(name, list(shape), dtype, kind="ExternalInput")
        self.hbm[name] = t.ap()
        return self.hbm[name]

    def dram_out(self, name, shape, dtype=F32):
        t = self.nc.dram_tensor(name, list(shape), dtype, kind="ExternalOutput")
        self.hbm[name] = t.ap()
        return self.hbm[name]

    def dram_tmp(self, name, shape, dtype=F32):
        t = self.nc.dram_tensor(name, list(shape), dtype, kind="Internal")
        self.hbm[name] = t.ap()
        return self.hbm[name]

    def const_in(self, name, arr):
        dt = BF16 if arr.dtype == ml_dtypes.bfloat16 else F32
        self.dram_in(name, arr.shape, dt)
        self.host_inputs[name] = arr
        return self.hbm[name]

    def sbuf(self, name, shape, dtype):
        h = self.p.stack.enter_context(self.nc.sbuf_tensor(name, list(shape), dtype))
        self.sb[name] = h
        return h

    def psum(self, name, shape, dtype=F32):
        h = self.p.stack.enter_context(self.nc.psum_tensor(name, list(shape), dtype))
        self.ps[name] = h
        return h


def alloc_main(mk):
    mk.sbuf("XBF", [128, DC * S], BF16)
    mk.sbuf("XF", [128, 4 * 512], F32)
    mk.sbuf("BRS", [128, S], F32)
    mk.sbuf("ROW0", [1, S], BF16)
    mk.sbuf("ROWC", [1, S], BF16)
    mk.sbuf("ROWP", [1, 192], BF16)
    mk.sbuf("MISC", [1, 16], F32)
    mk.sbuf("RST", [128, ST], F32)
    mk.sbuf("ROPE", [64, 2 * S], BF16)
    mk.sbuf("CMASK", [128, 896], BF16)
    mk.sbuf("RMASK", [128, 896], BF16)
    mk.sbuf("CONSTS", [1, 16], F32)
    mk.sbuf("EPSC", [128, 1], F32)
    mk.sbuf("ONES1", [1, 512], BF16)
    mk.sbuf("ONESF", [1, 128], F32)
    mk.sbuf("IDENTF", [128, 128], F32)
    mk.sbuf("BIASC", [128, 40], F32)
    mk.sbuf("BROW", [1, 1024], BF16)
    mk.sbuf("BVROW", [1, 1024], BF16)
    mk.sbuf("ONESC", [128, 1], BF16)
    mk.sbuf("IDENT", [128, 128], BF16)
    ar = mk.sbuf("AR", [128, 35968], BF16)
    cols = {"QB": 4096, "KB": 4096, "SQ": 8192, "VB": ST * 65, "ATB": 1024,
            "OTB": 4096, "RB": 1024, "WQKV": 3072, "WOALL": 8192}
    off = 0
    for name, w in cols.items():
        mk.sb[name] = ar[:, off:off + w]
        off += w
    assert off <= 35968, off
    mk.ar_used = off
    mk.psum("PA", [128, 1024])
    mk.psum("PB", [128, 1024])
    mk.psum("PC", [128, 1024])
    mk.psum("PD", [128, 512])
    mk.psum("PT", [128, 512], BF16)


def load_constants(mk):
    p = mk.p
    consts = np.zeros((1, 16), np.float32)
    consts[0, 0] = 1e-5
    mk.const_in("c_consts", consts)
    p.dma("sp", "const", lambda e: e.dma_start(out=mk.sb["CONSTS"][:], in_=mk.hbm["c_consts"][:, :]),
          writes=["CONSTS"], tag="ld_consts")
    mk.const_in("c_epsc", np.full((128, 1), 1e-5, np.float32))
    p.dma("sp", "const2", lambda e: e.dma_start(out=mk.sb["EPSC"][:], in_=mk.hbm["c_epsc"][:, :]),
          writes=["EPSC"], tag="ld_epsc")
    mk.const_in("c_ident", np.eye(128, dtype=np.float32))
    mk.const_in("c_ones1", np.ones((1, 512), np.float32))
    mk.const_in("c_onesf", np.ones((1, 128), np.float32))
    mk.const_in("c_identf", np.eye(128, dtype=np.float32))
    p.dma("sp", "onesf", lambda e: e.dma_start(out=mk.sb["ONESF"][:], in_=mk.hbm["c_onesf"][:, :]),
          writes=["ONESF"], tag="ld_onesf")
    p.dma("sp", "identf", lambda e: e.dma_start(out=mk.sb["IDENTF"][:], in_=mk.hbm["c_identf"][:, :]),
          writes=["IDENTF"], tag="ld_identf")
    mk.const_in("c_onesc", np.ones((128, 1), np.float32))
    hd = 64
    inv = 1.0 / 10000.0 ** (np.arange(0, hd, 2, dtype=np.float64) / hd)
    t = np.arange(S, dtype=np.float64)
    ang = t[None, :] * inv[:, None]
    cos = np.cos(ang)
    sin = np.sin(ang)
    cost = np.repeat(cos, 2, axis=0).astype(np.float32)
    ssin = np.empty((hd, S), np.float64)
    ssin[0::2] = -sin
    ssin[1::2] = sin
    mk.const_in("c_rope_cos", cost)
    mk.const_in("c_rope_ssin", ssin.astype(np.float32))
    p.dma("pool", "const", lambda e: e.dma_start(out=mk.sb["IDENT"][:], in_=mk.hbm["c_ident"][:, :]),
          writes=["IDENT"], tag="ld_ident")
    p.dma("pool", "const", lambda e: e.dma_start(out=mk.sb["ONES1"][:], in_=mk.hbm["c_ones1"][:, :]),
          writes=["ONES1"], tag="ld_ones1")
    p.dma("pool", "const", lambda e: e.dma_start(out=mk.sb["ONESC"][:], in_=mk.hbm["c_onesc"][:, :]),
          writes=["ONESC"], tag="ld_onesc")
    p.dma("pool", "const", lambda e: e.dma_start(out=mk.sb["ROPE"][:, 0:S], in_=mk.hbm["c_rope_cos"][:, :]),
          writes=["ROPE"], tag="ld_cos")
    p.dma("pool", "const", lambda e: e.dma_start(out=mk.sb["ROPE"][:, S:2 * S], in_=mk.hbm["c_rope_ssin"][:, :]),
          writes=["ROPE"], tag="ld_ssin")
    k = np.arange(128)[:, None]
    u = np.arange(896)[None, :] - 384
    cm = (k <= u).astype(np.float32)
    mk.const_in("c_cmask", cm)
    p.dma("pool", "const", lambda e: e.dma_start(out=mk.sb["CMASK"][:], in_=mk.hbm["c_cmask"][:, :]),
          writes=["CMASK"], tag="ld_cmask")
    rm = (k >= u).astype(np.float32)
    mk.const_in("c_rmask", rm)
    p.dma("pool", "const", lambda e: e.dma_start(out=mk.sb["RMASK"][:], in_=mk.hbm["c_rmask"][:, :]),
          writes=["RMASK"], tag="ld_rmask")


def load_shadow(mk, master):
    p = mk.p
    xbf = mk.sb["XBF"]
    m = mk.hbm[master]
    for c in range(DC):
        p.dma("pool", "xbf", lambda e, c=c: e.dma_start(
            out=xbf[:, c * S:(c + 1) * S], in_=m[c * 128:(c + 1) * 128, :]),
            reads=[f"M:{master}"], writes=[f"xbf{c}"], tag=f"ld_shadow{c}")



def ln_stats(mk, tag, shifted=False, want_sd=None):
    p = mk.p
    xbf, sq, xf = mk.sb["XBF"], mk.sb["SQ"], mk.sb["XF"]
    row0 = mk.sb["ROW0"]
    brs, rst = mk.sb["BRS"], mk.sb["RST"]
    onesc, onesf, identf = mk.sb["ONESC"], mk.sb["ONESF"], mk.sb["IDENTF"]
    pa, pb, pc, pd = mk.ps["PA"], mk.ps["PB"], mk.ps["PC"], mk.ps["PD"]

    for gp in range(S // 1024):
        g0 = gp * 1024
        for c in range(DC):
            buf = (gp * DC + c) % 2
            if shifted:
                rhs0, ofs0 = _l0f_shift_rhs(xbf, c, g0, g0 + 1024)
            else:
                rhs0, ofs0 = xbf[:, c * S + g0: c * S + g0 + 1024], 0
            p.emit("act", lambda e, rhs0=rhs0, ofs0=ofs0, buf=buf: e.activation(
                out=sq[:, buf * S + ofs0: buf * S + 1024], in_=rhs0, func=AF.Square),
                reads=[f"xbf{c}"], writes=[f"sq{buf}"], tag=f"{tag}lnsq")
            if ofs0:
                p.emit("dve", lambda e, buf=buf: e.memset(sq[:, buf * S: buf * S + 1], 0.0),
                       reads=[], writes=[f"sq{buf}"], tag=f"{tag}lnz")
            for half in range(2):
                lo = g0 + half * 512
                if shifted:
                    rhs2, ofs2 = _l0f_shift_rhs(xbf, c, lo, lo + 512)
                else:
                    rhs2, ofs2 = xbf[:, c * S + lo: c * S + lo + 512], 0
                p.emit("te", lambda e, rhs2=rhs2, ofs2=ofs2, c=c, half=half: e.matmul(
                    pa[0:1, half * 512 + ofs2: half * 512 + 512], onesc[:, :], rhs2,
                    start=(c == 0), stop=(c == DC - 1)),
                    reads=[f"xbf{c}", "ONESC"], writes=[f"PA{half}"], tag=f"{tag}lnsum")
                p.emit("te", lambda e, c=c, buf=buf, half=half: e.matmul(
                    pb[0:1, half * 512:half * 512 + 512], onesc[:, :],
                    sq[:, buf * S + half * 512: buf * S + half * 512 + 512],
                    start=(c == 0), stop=(c == DC - 1)),
                    reads=[f"sq{buf}", "ONESC"], writes=[f"PB{half}"], tag=f"{tag}lnsumsq")
        p.emit("act", lambda e, g0=g0: e.activation(
            out=row0[0:1, g0:g0 + 1024], in_=pa[0:1, 0:1024], func=AF.Identity, scale=1.0 / D),
            reads=["PA0", "PA1"], writes=["row0"], tag=f"{tag}lnmu")
        p.emit("act", lambda e: e.activation(
            out=xf[0:1, 0:1024], in_=pa[0:1, 0:1024], func=AF.Square, scale=1.0 / D),
            reads=["PA0", "PA1"], writes=["xf0", "xf1"], tag=f"{tag}lnmusq")
        p.emit("dve", lambda e: e.scalar_tensor_tensor(
            xf[0:1, 0:1024], pb[0:1, 0:1024], 1.0 / D, xf[0:1, 0:1024],
            ALU.mult, ALU.subtract),
            reads=["PB0", "PB1", "xf0", "xf1"], writes=["xf0", "xf1"], tag=f"{tag}lnvar")
        p.emit("act", lambda e: e.activation(out=xf[0:1, 0:1024], in_=xf[0:1, 0:1024], func=AF.Sqrt,
                                             bias=mk.sb["CONSTS"][0:1, 0:1]),
               reads=["xf0", "xf1", "CONSTS"], writes=["xf0", "xf1"], tag=f"{tag}sd")
        if want_sd is not None:
            p.emit("dve", lambda e, g0=g0: e.tensor_copy(want_sd[0:1, g0:g0 + 1024], xf[0:1, 0:1024]),
                   reads=["xf0", "xf1"], writes=["l0sdr"], tag=f"{tag}sdrow")
        p.emit("dve", lambda e: e.reciprocal(xf[0:1, 0:1024], xf[0:1, 0:1024]),
               reads=["xf0", "xf1"], writes=["xf0", "xf1"], tag=f"{tag}rsig")
        for half in range(2):
            bank = half % 2
            p.emit("te", lambda e, g0=g0, half=half, bank=bank: e.matmul(
                pc[:, bank * 512:bank * 512 + 512], onesf[0:1, 0:128], xf[0:1, half * 512:half * 512 + 512],
                start=True, stop=True),
                reads=["xf0", "xf1", "ONESF"], writes=[f"PC{bank}"], tag=f"{tag}brsmm")
            p.emit("dve", lambda e, g0=g0, half=half, bank=bank: e.tensor_copy(
                brs[:, g0 + half * 512: g0 + half * 512 + 512], pc[:, bank * 512:bank * 512 + 512]),
                reads=[f"PC{bank}"], writes=["BRS"], tag=f"{tag}brsev")
    for tt in range(ST):
        p.emit("te", lambda e, tt=tt: e.transpose(
            pd[:, 0:128], brs[:, tt * 128:(tt + 1) * 128], identf[:, :]),
            reads=["BRS", "IDENTF"], writes=["PD0"], tag=f"{tag}rstT")
        p.emit("dve", lambda e, tt=tt: e.tensor_copy(rst[:, tt:tt + 1], pd[:, 0:1]),
            reads=["PD0"], writes=["RST"], tag=f"{tag}rstev")



def rope_apply(mk, buf_name, tag):
    p = mk.p
    buf = mk.sb[buf_name]
    rope = mk.sb["ROPE"]
    rsc = mk.sb["SQ"]
    mask = [i + 1 if i % 2 == 0 else i - 1 for i in range(32)]
    p.emit("dve", lambda e: e.stream_shuffle(rsc[0:64, 0:S], buf[0:64, :], mask),
           reads=[buf_name], writes=["sq0"], tag=f"{tag}shuf")
    p.emit("dve", lambda e: e.tensor_mul(rsc[0:64, 0:S], rsc[0:64, 0:S], rope[:, S:2 * S]),
           reads=["sq0", "ROPE"], writes=["sq0"], tag=f"{tag}ssin")
    p.emit("dve", lambda e: e.tensor_mul(rsc[0:64, S:2 * S], buf[0:64, :], rope[:, 0:S]),
           reads=[buf_name, "ROPE"], writes=["sq1"], tag=f"{tag}cos")
    p.emit("dve", lambda e: e.tensor_add(buf[0:64, :], rsc[0:64, 0:S], rsc[0:64, S:2 * S]),
           reads=["sq0", "sq1"], writes=[buf_name], tag=f"{tag}add")


def attn_std(mk, li, wqkv_name, wo_name, ot_name, nh, scale, rope, causal,
             master_in, master_out, tag, debug=None):
    p = mk.p
    sb, ps, hbm = mk.sb, mk.ps, mk.hbm
    xbf, sq = sb["XBF"], sb["SQ"]
    misc = sb["MISC"]
    row0, rowc, rowp = sb["ROW0"], sb["ROWC"], sb["ROWP"]
    onesf = sb["ONESF"]
    rbf = sb["RB"].bitcast(F32)
    qb, kb, vb, atb, otb, rb = sb["QB"], sb["KB"], sb["VB"], sb["ATB"], sb["OTB"], sb["RB"]
    xf = sb["XF"]
    wqkv, woall, ots = sb["WQKV"], sb["WOALL"], sb["OTB"]
    brs, rst = sb["BRS"], sb["RST"]
    onesc, ones1 = sb["ONESC"], sb["ONES1"]
    pa, pb, pc, pd = ps["PA"], ps["PB"], ps["PC"], ps["PD"]
    xf = sb["XF"]
    w_h = hbm[wqkv_name]
    wo_h = hbm[wo_name]
    ot = hbm[ot_name]
    NQG = S // 512

    ln_stats(mk, tag)

    p.emit("dve", lambda e: e.memset(kb[64:65, :], 1.0), writes=["kb_ones"], tag=f"{tag}kones")
    p.emit("dve", lambda e: e.memset(
        vb[:].rearrange("p (t c) -> p t c", c=65)[:, :, 64:65], 1.0),
        writes=["vb_ones"], tag=f"{tag}vones")

    for j in range(nh // 2):
        for r in range(2):
            p.dma("pool", "woall", lambda e, j=j, r=r: e.dma_start(
                out=woall[r * 64:(r + 1) * 64, j * 1024:(j + 1) * 1024],
                in_=wo_h[2 * j + r]),
                reads=[f"M:{wo_name}"], writes=["WOALL"], tag=f"{tag}ldwo{j}_{r}")

    nh_run = nh if debug in (None, "full16") else (2 if debug == "phaseB1" else 1)
    for h in range(nh_run):
        hb = h % 2
        wreg = f"wqkv{hb}"
        for c in range(DC):
            p.dma("pool", "wqkv", lambda e, h=h, hb=hb, c=c: e.dma_start(
                out=wqkv[:, hb * 1536 + c * 192: hb * 1536 + (c + 1) * 192],
                in_=w_h[h][c * 128:(c + 1) * 128, :]),
                reads=[f"M:{wqkv_name}"], writes=[f"{wreg}_{c}"], ckey=wreg,
                tag=f"{tag}ldw{h}_{c}")
        for c in range(DC):
            p.emit("te", lambda e, c=c, hb=hb: e.matmul(
                pd[0:1, 0:192], onesc[:, :], wqkv[:, hb * 1536 + c * 192: hb * 1536 + (c + 1) * 192],
                start=(c == 0), stop=(c == DC - 1)),
                reads=[f"{wreg}_{c}", "ONESC"], writes=["PD0"], tag=f"{tag}cs{h}")
        p.emit("act", lambda e: e.activation(out=rowp[0:1, 0:192], in_=pd[0:1, 0:192],
                                             func=AF.Identity, scale=-1.0),
               reads=["PD0"], writes=["rowp"], tag=f"{tag}csev{h}")

        for g in range(NQG):
            gs = slice(g * 512, (g + 1) * 512)
            bank = g % 2
            pa_reg, pb_reg = f"PA{bank}", f"PB{bank}"
            p.emit("te", lambda e, gs=gs, bank=bank: e.matmul(
                pa[0:64, bank * 512:bank * 512 + 512], rowp[0:1, 0:64], row0[0:1, gs],
                start=True, stop=False),
                reads=["rowp", "row0"], writes=[pa_reg], tag=f"{tag}qseed{h}")
            p.emit("te", lambda e, gs=gs, bank=bank: e.matmul(
                pb[0:64, bank * 512:bank * 512 + 512], rowp[0:1, 64:128], row0[0:1, gs],
                start=True, stop=False),
                reads=["rowp", "row0"], writes=[pb_reg], tag=f"{tag}kseed{h}")
            for c in range(DC):
                p.emit("te", lambda e, c=c, g=g, hb=hb, bank=bank: e.matmul(
                    pa[0:64, bank * 512:bank * 512 + 512],
                    wqkv[:, hb * 1536 + c * 192: hb * 1536 + c * 192 + 64],
                    xbf[:, c * S + g * 512: c * S + g * 512 + 512],
                    start=False, stop=(c == DC - 1)),
                    reads=[f"{wreg}_{c}", f"xbf{c}"], writes=[pa_reg], tag=f"{tag}qmm{h}")
                p.emit("te", lambda e, c=c, g=g, hb=hb, bank=bank: e.matmul(
                    pb[0:64, bank * 512:bank * 512 + 512],
                    wqkv[:, hb * 1536 + c * 192 + 64: hb * 1536 + c * 192 + 128],
                    xbf[:, c * S + g * 512: c * S + g * 512 + 512],
                    start=False, stop=(c == DC - 1)),
                    reads=[f"{wreg}_{c}", f"xbf{c}"], writes=[pb_reg], tag=f"{tag}kmm{h}")
            p.emit("dve", lambda e, gs=gs, bank=bank: e.scalar_tensor_tensor(
                qb[0:64, gs], pa[0:64, bank * 512:bank * 512 + 512], scale, brs[0:64, gs],
                ALU.mult, ALU.mult),
                reads=[pa_reg, "BRS"], writes=["QB"], tag=f"{tag}qev{h}")
            p.emit("dve", lambda e, gs=gs, bank=bank: e.scalar_tensor_tensor(
                kb[0:64, gs], pb[0:64, bank * 512:bank * 512 + 512], 1.0, brs[0:64, gs],
                ALU.mult, ALU.mult),
                reads=[pb_reg, "BRS"], writes=["KB"], tag=f"{tag}kev{h}")

        for tt in range(ST):
            vps = pd if tt % 2 == 0 else pc
            vreg = "PD0" if tt % 2 == 0 else "PC0"
            p.emit("te", lambda e, tt=tt, vps=vps: e.matmul(
                vps[:, 0:64], row0[0:1, tt * 128:(tt + 1) * 128], rowp[0:1, 128:192],
                start=True, stop=False),
                reads=["rowp", "row0"], writes=[vreg], tag=f"{tag}vseed{h}")
            for c in range(DC):
                p.emit("te", lambda e, c=c, tt=tt, hb=hb, vps=vps: e.matmul(
                    vps[:, 0:64],
                    xbf[:, c * S + tt * 128: c * S + (tt + 1) * 128],
                    wqkv[:, hb * 1536 + c * 192 + 128: hb * 1536 + (c + 1) * 192],
                    start=False, stop=(c == DC - 1)),
                    reads=[f"{wreg}_{c}", f"xbf{c}"], writes=[vreg], tag=f"{tag}vmm{h}")
            p.emit("dve", lambda e, tt=tt, vps=vps: e.tensor_scalar(
                vb[:, tt * 65: tt * 65 + 64], vps[:, 0:64], rst[:, tt:tt + 1], None, ALU.mult),
                reads=[vreg, "RST"], writes=["VB"], tag=f"{tag}vev{h}")

        if debug == "proj":
            continue
        if rope:
            rope_apply(mk, "QB", f"{tag}rq{h}")
            rope_apply(mk, "KB", f"{tag}rk{h}")

        p.emit("act", lambda e: e.activation(out=sq[0:64, S:2 * S], in_=kb[0:64, :], func=AF.Square),
               reads=["KB"], writes=["sq1"], tag=f"{tag}k2{h}")
        for g in range(NQG):
            p.emit("te", lambda e, g=g: e.matmul(
                pd[0:1, 0:512], onesc[0:64, :], sq[0:64, S + g * 512: S + (g + 1) * 512],
                start=True, stop=True),
                reads=["sq1", "ONESC"], writes=["PD0"], tag=f"{tag}k2mm{h}")
            p.emit("dve", lambda e, g=g: e.tensor_reduce(
                misc[0:1, g:g + 1], pd[0:1, 0:512], AX.X, ALU.max),
                reads=["PD0"], writes=["misc"], tag=f"{tag}k2max{h}")
        p.emit("dve", lambda e: e.tensor_reduce(misc[0:1, 8:9], misc[0:1, 0:8], AX.X, ALU.max),
               reads=["misc"], writes=["misc"], tag=f"{tag}mk2{h}")
        p.emit("act", lambda e: e.activation(out=misc[0:1, 9:10], in_=misc[0:1, 8:9], func=AF.Sqrt),
               reads=["misc"], writes=["misc"], tag=f"{tag}mk{h}")
        p.emit("dve", lambda e: e.tensor_scalar(misc[0:1, 10:11], misc[0:1, 9:10], -1.0, None, ALU.mult),
               reads=["misc"], writes=["misc"], tag=f"{tag}mkn{h}")
        p.emit("act", lambda e: e.activation(out=sq[0:64, 0:S], in_=qb[0:64, :], func=AF.Square),
               reads=["QB"], writes=["sq0"], tag=f"{tag}q2{h}")
        for g in range(NQG):
            gs = slice(g * 512, (g + 1) * 512)
            p.emit("te", lambda e, gs=gs: e.matmul(
                pd[0:1, 0:512], onesc[0:64, :], sq[0:64, gs], start=True, stop=True),
                reads=["sq0", "ONESC"], writes=["PD0"], tag=f"{tag}q2mm{h}")
            p.emit("act", lambda e, gs=gs: e.activation(
                out=rowc[0:1, gs], in_=pd[0:1, 0:512], func=AF.Sqrt),
                reads=["PD0"], writes=["rowc"], tag=f"{tag}qn{h}")
        p.emit("dve", lambda e: e.tensor_scalar(
            qb[64:65, :], rowc[0:1, :], misc[0:1, 10:11], NEG_MARGIN, ALU.mult, ALU.add),
            reads=["rowc", "misc"], writes=["qb_c"], tag=f"{tag}negc{h}")

        if debug == "norms":
            continue
        for g in range(NQG):
            gs = slice(g * 512, (g + 1) * 512)
            oband = g % 2
            pc_reg = f"PC{oband}"
            kts = list(range(min(32, 4 * (g + 1)))) if causal else list(range(32))

            def emit_smm(i, kt):
                bank = i % 2
                pa_reg = f"PA{bank}"
                p.emit("te", lambda e, kt=kt, gs=gs, bank=bank: e.matmul(
                    pa[:, bank * 512:bank * 512 + 512], kb[0:65, kt * 128:(kt + 1) * 128],
                    qb[0:65, gs], start=True, stop=True),
                    reads=["KB", "kb_ones", "QB", "qb_c"], writes=[pa_reg], tag=f"{tag}smm{h}")
                p.emit("act", lambda e, bank=bank: e.activation(
                    out=atb[:, bank * 512:bank * 512 + 512], in_=pa[:, bank * 512:bank * 512 + 512],
                    func=AF.Exp),
                    reads=[pa_reg], writes=[f"atb{bank}"], tag=f"{tag}exp{h}")
                if causal and kt >= 4 * g:
                    j = kt - 4 * g
                    p.emit("dve", lambda e, bank=bank, j=j: e.tensor_mul(
                        atb[:, bank * 512:bank * 512 + 512], atb[:, bank * 512:bank * 512 + 512],
                        sb["CMASK"][:, 384 - 128 * j: 896 - 128 * j]),
                        reads=[f"atb{bank}", "CMASK"], writes=[f"atb{bank}"], tag=f"{tag}mask{h}")

            def emit_av(i, kt):
                bank = i % 2
                p.emit("te", lambda e, kt=kt, bank=bank, oband=oband, st=(i == 0), sp=(i == len(kts) - 1): e.matmul(
                    pc[0:65, oband * 512:oband * 512 + 512], vb[:, kt * 65:kt * 65 + 65],
                    atb[:, bank * 512:bank * 512 + 512], start=st, stop=sp),
                    reads=["VB", "vb_ones", f"atb{bank}"], writes=[pc_reg], tag=f"{tag}av{h}")

            for i, kt in enumerate(kts):
                emit_smm(i, kt)
                if i > 0:
                    emit_av(i - 1, kts[i - 1])
            emit_av(len(kts) - 1, kts[-1])
            p.emit("dve", lambda e, oband=oband: e.reciprocal(
                xf[0:1, 0:512], pc[64:65, oband * 512:oband * 512 + 512]),
                reads=[pc_reg], writes=["xf0"], tag=f"{tag}rrec{h}")
            p.emit("te", lambda e: e.matmul(
                pd[0:64, 0:512], onesf[0:1, 0:64], xf[0:1, 0:512], start=True, stop=True),
                reads=["xf0", "ONESF"], writes=["PD0"], tag=f"{tag}rbc{h}")
            p.emit("dve", lambda e: e.tensor_copy(rbf[0:64, 0:512], pd[0:64, 0:512]),
                reads=["PD0"], writes=["rbv"], tag=f"{tag}rbev{h}")
            p.emit("dve", lambda e, gs=gs, oband=oband: e.tensor_mul(
                otb[0:64, gs], pc[0:64, oband * 512:oband * 512 + 512], rbf[0:64, 0:512]),
                reads=[pc_reg, "rbv"], writes=["OTB"], tag=f"{tag}oev{h}")
        p.dma("sp", "ot", lambda e, h=h: e.dma_start(out=ot[h], in_=otb[0:64, :]),
              reads=["OTB"], writes=[f"M:ot{h}"], tag=f"{tag}otst{h}")

    if debug and not str(debug).startswith("phaseB"):
        passthrough(mk, master_in, master_out, tag + "dbg_")
        return
    pb_mode = None
    if str(debug).startswith("phaseB"):
        nh = 2
        pb_mode = debug[6:]
    for g in range(NQG):
        gs = slice(g * 512, (g + 1) * 512)
        for hh in range(nh // 2):
            p.dma("sp", "ots", lambda e, hh=hh, g=g: e.dma_start(
                out=ots[:, hh * 512:(hh + 1) * 512],
                in_=ot[2 * hh:2 * hh + 2, :, g * 512:(g + 1) * 512]),
                reads=[f"M:ot{k}" for k in range(nh)], writes=[f"ots{hh}"],
                tag=f"{tag}ldots{g}_{hh}")
        for oc in range(DC):
            bank = oc % 2
            pa_reg = f"PA{bank}"
            slot = (g * DC + oc) % 4
            xf_reg = f"xf{slot}"
            p.dma("sp", "xf", lambda e, oc=oc, gs=gs, slot=slot: e.dma_start(
                out=xf[:, slot * 512:(slot + 1) * 512],
                in_=hbm[master_in][oc * 128:(oc + 1) * 128, gs.start:gs.stop]),
                reads=[f"M:{master_in}"], writes=[xf_reg], tag=f"{tag}ldxf{oc}_{g}")
            npair = nh // 2 if pb_mode in (None, "1") else (1 if pb_mode == "m0" else 0)
            for j in range(npair):
                p.emit("te", lambda e, j=j, oc=oc, bank=bank, npair=max(npair, 1): e.matmul(
                    pa[:, bank * 512:bank * 512 + 512],
                    woall[:, j * 1024 + oc * 128: j * 1024 + (oc + 1) * 128],
                    ots[:, j * 512:(j + 1) * 512],
                    start=(j == 0), stop=(j == npair - 1)),
                    reads=["WOALL", f"ots{j}"], writes=[pa_reg], tag=f"{tag}womm{oc}")
            if npair:
                p.emit("dve", lambda e, slot=slot, bank=bank: e.tensor_add(
                    xf[:, slot * 512:(slot + 1) * 512],
                    xf[:, slot * 512:(slot + 1) * 512],
                    pa[:, bank * 512:bank * 512 + 512]),
                    reads=[pa_reg, xf_reg], writes=[xf_reg], tag=f"{tag}resadd{oc}")
            p.dma("sp", "xfout", lambda e, oc=oc, gs=gs, slot=slot: e.dma_start(
                out=hbm[master_out][oc * 128:(oc + 1) * 128, gs.start:gs.stop],
                in_=xf[:, slot * 512:(slot + 1) * 512]),
                reads=[xf_reg], writes=[f"M:{master_out}:{oc}_{g}"], ckey="xfout", tag=f"{tag}stxf{oc}_{g}")
            p.emit("dve", lambda e, oc=oc, gs=gs, slot=slot: e.tensor_copy(
                xbf[:, oc * S + gs.start: oc * S + gs.stop], xf[:, slot * 512:(slot + 1) * 512]),
                reads=[xf_reg], writes=[f"xbf{oc}"], tag=f"{tag}shad{oc}_{g}")



def repack_weights(inputs):
    w = {}
    have = lambda *names: all(n in inputs for n in names)

    def qkv_stack(wq, wk, wv, nh=16):
        def by_head(a):
            return np.ascontiguousarray(a.reshape(D, nh, 64).transpose(1, 0, 2))
        return np.ascontiguousarray(np.concatenate(
            [by_head(wq), by_head(wk), by_head(wv)], axis=2))

    if have("L0_attn_b_w_qkv"):
        wqkv = inputs["L0_attn_b_w_qkv"]
        w["w_l0b_qkv"] = np.ascontiguousarray(
            wqkv.reshape(3, 16, 64, D).transpose(1, 3, 0, 2).reshape(16, D, 192))
        w["w_l0b_o"] = np.ascontiguousarray(inputs["L0_attn_b_w_out"].T.reshape(16, 64, D))
    if have("L1_attn_a_wq"):
        w["w_l1a_qkv"] = qkv_stack(inputs["L1_attn_a_wq"], inputs["L1_attn_a_wk"],
                                   inputs["L1_attn_a_wv"])
        w["w_l1a_o"] = np.ascontiguousarray(inputs["L1_attn_a_wo"].reshape(16, 64, D))
    if have("L1_attn_b_wq"):
        w["w_l1b_qkv"] = qkv_stack(inputs["L1_attn_b_wq"].T, inputs["L1_attn_b_wk"].T,
                                   inputs["L1_attn_b_wv"].T)
        w["w_l1b_o"] = np.ascontiguousarray(inputs["L1_attn_b_wo"].T.reshape(16, 64, D))
    if "L0_ffn_wqkv" in inputs:
        wq = inputs["L0_ffn_wqkv"]
        w["w_l0f_qkv"] = np.ascontiguousarray(
            wq.reshape(2, D, 3, 8, 64).transpose(0, 3, 1, 2, 4).reshape(2, 8, D, 192))
        w1 = inputs["L0_ffn_w1"]
        w1p = np.zeros((2, 22, D, 256), np.float32)
        for j in range(22):
            a0, a1 = j * 128, min((j + 1) * 128, 2730)
            w1p[:, j, :, 0:a1 - a0] = w1[:, :, a0:a1]
            w1p[:, j, :, 128:128 + a1 - a0] = w1[:, :, 2730 + a0:2730 + a1]
        w["w1p"] = w1p
        b1 = inputs["L0_ffn_b1"]
        b1p = np.zeros((2, 22, 256), np.float32)
        for j in range(22):
            a0, a1 = j * 128, min((j + 1) * 128, 2730)
            b1p[:, j, 0:a1 - a0] = b1[:, a0:a1]
            b1p[:, j, 128:128 + a1 - a0] = b1[:, 2730 + a0:2730 + a1]
        w["b1p"] = b1p.reshape(2, 5632)
        w2 = inputs["L0_ffn_w2"]
        w2p = np.zeros((2, 22, 128, D), np.float32)
        for j in range(22):
            a0, a1 = j * 128, min((j + 1) * 128, 2730)
            w2p[:, j, 0:a1 - a0, :] = w2[:, a0:a1, :]
        w["w2p"] = w2p
        w["lg_c"] = np.ascontiguousarray(inputs["L0_ffn_ln_g"].reshape(2, 8, 128).transpose(0, 2, 1))
        w["lb_c"] = np.ascontiguousarray(inputs["L0_ffn_ln_b"].reshape(2, 8, 128).transpose(0, 2, 1))
        w["b2_c"] = np.ascontiguousarray(inputs["L0_ffn_b2"].reshape(2, 8, 128).transpose(0, 2, 1))
    if "L0_attn_a_b_UQ" in inputs:
        w["b_uq_c"] = np.ascontiguousarray(inputs["L0_attn_a_b_UQ"].reshape(8, 128).T)
        w["b_uk_c"] = np.ascontiguousarray(inputs["L0_attn_a_b_UK"].reshape(8, 128).T)
        w["b_qr_c"] = np.ascontiguousarray(inputs["L0_attn_a_b_QR"].reshape(8, 32).T)
    return w


def passthrough(mk, master_in, master_out, tag):
    p = mk.p
    xf = mk.sb["XF"]
    hbm = mk.hbm
    for oc in range(DC):
        for g in range(S // 512):
            slot = (oc * (S // 512) + g) % 4
            xf_reg = f"xf{slot}"
            p.dma("sp", "xf", lambda e, oc=oc, g=g, slot=slot: e.dma_start(
                out=xf[:, slot * 512:(slot + 1) * 512],
                in_=hbm[master_in][oc * 128:(oc + 1) * 128, g * 512:(g + 1) * 512]),
                reads=[f"M:{master_in}"], writes=[xf_reg], tag=f"{tag}ld{oc}_{g}")
            p.dma("sp", "xfout", lambda e, oc=oc, g=g, slot=slot: e.dma_start(
                out=hbm[master_out][oc * 128:(oc + 1) * 128, g * 512:(g + 1) * 512],
                in_=xf[:, slot * 512:(slot + 1) * 512]),
                reads=[xf_reg], writes=[f"M:{master_out}"], tag=f"{tag}st{oc}_{g}")


def build_sublayer(mk, idx, master_in, master_out):
    if idx == -1:
        passthrough(mk, master_in, master_out, "pt_")
    elif idx == -2:
        ln_stats(mk, "dbg_")
        passthrough(mk, master_in, master_out, "pt_")
    elif idx in (-3, -4, -5, -6, -7, -8, -9):
        dbg = {-3: "proj", -4: "norms", -5: "full1", -6: "full16", -7: "phaseB1",
               -8: "phaseBd", -9: "phaseBm0"}[idx]
        mk.dram_in("w_l0b_qkv", (16, D, 192))
        mk.dram_in("w_l0b_o", (16, 64, D))
        mk.dram_tmp("ot_l0b", (16, 64, S), BF16)
        attn_std(mk, idx, "w_l0b_qkv", "w_l0b_o", "ot_l0b", 16, 0.125, False, True,
                 master_in, master_out, "dbg_", debug=dbg)
    elif idx == 2:
        mk.dram_in("w_l0b_qkv", (16, D, 192))
        mk.dram_in("w_l0b_o", (16, 64, D))
        mk.dram_tmp("ot_l0b", (16, 64, S), BF16)
        attn_std(mk, idx, "w_l0b_qkv", "w_l0b_o", "ot_l0b", 16, 0.125, False, True,
                 master_in, master_out, "s2_")
    elif idx == 4:
        mk.dram_in("w_l1a_qkv", (16, D, 192))
        mk.dram_in("w_l1a_o", (16, 64, D))
        mk.dram_tmp("ot_l1a", (16, 64, S), BF16)
        attn_std(mk, idx, "w_l1a_qkv", "w_l1a_o", "ot_l1a", 16, 0.125, True, True,
                 master_in, master_out, "s4_")
    elif idx == 6:
        mk.dram_in("w_l1b_qkv", (16, D, 192))
        mk.dram_in("w_l1b_o", (16, 64, D))
        mk.dram_tmp("ot_l1b", (16, 64, S), BF16)
        attn_std(mk, idx, "w_l1b_qkv", "w_l1b_o", "ot_l1b", 16, 0.125, True, False,
                 master_in, master_out, "s6_")
    elif idx == 0:
        for nm in ("W_DQ", "b_DQ", "W_UQ", "b_UQ", "W_DKV", "b_DKV", "W_UK", "b_UK",
                   "W_UV", "b_UV", "W_QR", "b_QR", "W_KR", "b_KR", "W_O", "b_O"):
            shp = {"W_DQ": (D, 32), "b_DQ": (32,), "W_UQ": (32, D), "b_UQ": (D,),
                   "W_DKV": (D, 32), "b_DKV": (32,), "W_UK": (32, D), "b_UK": (D,),
                   "W_UV": (32, D), "b_UV": (D,), "W_QR": (32, 256), "b_QR": (256,),
                   "W_KR": (D, 32), "b_KR": (32,), "W_O": (D, D), "b_O": (D,)}[nm]
            mk.dram_in(f"L0_attn_a_{nm}", shp)
        mk.dram_in("b_uq_c", (128, 8))
        mk.dram_in("b_uk_c", (128, 8))
        mk.dram_in("b_qr_c", (32, 8))
        mk.dram_tmp("ot_l0a", (8, 128, S), BF16)
        mla(mk, master_in, master_out, "s0_")
    elif idx in (1, 3):
        first = not getattr(mk, "l0w_done", False)
        if first:
            mk.dram_in("L0_ffn_wqkv", (2, D, 1536))
            mk.dram_in("L0_ffn_q_scale", (2, 64))
            mk.dram_in("L0_ffn_k_scale", (2, 64))
            mk.dram_in("L0_ffn_wo", (2, 512, D))
            mk.dram_in("L0_ffn_b1", (2, 5460))
            mk.dram_in("L0_ffn_b2", (2, D))
            mk.dram_in("w_l0f_qkv", (2, 8, D, 192))
            mk.dram_in("w1p", (2, 22, D, 256))
            mk.dram_in("b1p", (2, 5632))
            mk.dram_in("w2p", (2, 22, 128, D))
            mk.dram_in("lg_c", (2, 128, 8))
            mk.dram_in("lb_c", (2, 128, 8))
            mk.dram_in("b2_c", (2, 128, 8))
            for nm in ("c_l0f_cos", "c_l0f_ssin", "c_l0f_xs", "c_l0f_xsi"):
                mk.dram_in(nm, (64, S))
            mk.host_inputs.update(l0f_tables())
            mk.dram_tmp("w1b", (2, 22, D, 256), BF16)
            mk.dram_tmp("w2b", (2, 22, 128, D), BF16)
            mk.dram_tmp("ot_l0f", (8, 64, S), BF16)
            mk.dram_tmp("zm", (D, S))
            l0f_precast(mk)
            mk.l0w_done = True
        l0ffn(mk, master_in, master_out, f"s{idx}_")
    elif idx in (5, 7):
        first = not getattr(mk, "l1w_done", False)
        if first:
            mk.dram_in("L1_ffn_w_gate", (L1I, D))
            mk.dram_in("L1_ffn_w_up", (L1I, D))
            mk.dram_in("L1_ffn_w_down", (D, L1I))
            mk.l1w_done = True
        l1ffn(mk, master_in, master_out, f"s{idx}_", quant_weights=first)
    else:
        raise NotImplementedError(f"sublayer {idx}")


def build_program(sublayers, debug_masters=False, ses=False):
    stack = contextlib.ExitStack()
    nc = bass.Bass("TRN2", target_bir_lowering=False, debug=False)
    prog = Prog(nc, stack, same_engine_sync=ses)
    mk = MK(nc, prog)
    with stack:
        mk.dram_in("xT", (D, S))
        mk.dram_out("yT", (D, S))
        if len(sublayers) > 1:
            mk.dram_tmp("xm0", (D, S))
            mk.dram_tmp("xm1", (D, S))
        alloc_main(mk)
        load_constants(mk)
        load_shadow(mk, "xT")
        cur = "xT"
        for i, idx in enumerate(sublayers):
            last = (i == len(sublayers) - 1)
            nxt = "yT" if last else ("xm0" if i % 2 == 0 else "xm1")
            if i > 0:
                prog.barrier()
            build_sublayer(mk, idx, cur, nxt)
            cur = nxt
            if debug_masters:
                dbg = mk.dram_out(f"dbg{i}", (D, S))
                for c in range(DC):
                    prog.dma("sp", f"dbg{i}", lambda e, c=c, dbg=dbg, cur=cur: e.dma_start(
                        out=dbg[c * 128:(c + 1) * 128, :],
                        in_=mk.hbm[cur][c * 128:(c + 1) * 128, :]),
                        reads=[f"M:{cur}"], writes=[f"M:dbg{i}"], tag=f"dbg{i}_{c}")
        prog.wait_all_dmas("sp")
        with nc.Block() as block:
            prog.finalize_into_block(block)
    return mk


def build_in_map(mk, inputs, w, xT):
    m = {"xT": xT}
    m.update(mk.host_inputs)
    for name, arr in w.items():
        if name in mk.hbm and name not in m:
            m[name] = arr
    for name, arr in inputs.items():
        if name in mk.hbm and name not in m:
            m[name] = arr
    return m


def kernel(**inputs):
    x = inputs["L0_attn_a_h_t"]
    sublayers = list(range(8))
    mk = build_program(sublayers, ses=True)
    w = repack_weights(inputs)
    in_maps = [build_in_map(mk, inputs, w, np.ascontiguousarray(x[core].T))
               for core in range(NCORES)]
    res = run_bass_kernel_spmd(mk.nc, in_maps, core_ids=list(range(NCORES)))
    out = np.stack([np.ascontiguousarray(r["yT"].T) for r in res.results])
    return out



L1I = 2816

def _l1_views(mk):
    ar = mk.sb["AR"]
    v = {}
    v["XNAT"] = ar[:, 0:1024]
    v["XNF"] = ar[:, 1024:3072].bitcast(F32)
    v["XQ"] = ar[:, 0:1024]
    v["XQT"] = ar[:, 3072:7168]
    v["HQT"] = ar[:, 7168:18432]
    v["QL"] = ar[:, 18432:19456]
    v["T1"] = ar[:, 19456:20480]
    v["T2"] = ar[:, 20480:21504]
    v["SG"] = ar[:, 21504:22528]
    v["AQ"] = ar[:, 22528:24576].bitcast(F32)
    v["BM"] = ar[:, 24576:24704].bitcast(F32)
    v["BMH"] = ar[:, 24704:24832].bitcast(F32)
    v["BMR"] = ar[:, 24832:24960].bitcast(F32)
    v["GT"] = ar[:, 24960:25472]
    v["HF"] = ar[:, 25472:26496].bitcast(F32)
    v["HQ"] = ar[:, 26496:27008]
    v["COLS"] = ar[:, 27008:27136].bitcast(F32)
    v["WSL"] = ar[:, 27136:35328]
    return v


def quant_tile(mk, v, src_ap, out_ap, width, tag, reads, interm_pfx="l1q"):
    p = mk.p
    nb = width // 16
    ql, t1, t2, sg = v["QL"], v["T1"], v["T2"], v["SG"]
    aq, bm, bmh, bmr = v["AQ"], v["BM"], v["BMH"], v["BMR"]
    R = [interm_pfx]
    p.emit("dve", lambda e: e.tensor_reduce(
        bm[:, 0:nb], src_ap.rearrange("p (b k) -> p b k", k=16), AX.X, ALU.max,
        apply_absolute_value=True),
        reads=reads, writes=R, tag=f"{tag}bmax")
    p.emit("dve", lambda e: e.tensor_scalar(bm[:, 0:nb], bm[:, 0:nb], 1.0 / 6.0, 1e-12,
                                            ALU.mult, ALU.max),
           reads=R, writes=R, tag=f"{tag}scale")
    p.emit("dve", lambda e: e.tensor_scalar(bmh[:, 0:nb], bm[:, 0:nb], 0.5, None, ALU.mult),
           reads=R, writes=R, tag=f"{tag}hscale")
    p.emit("dve", lambda e: e.reciprocal(bmr[:, 0:nb], bm[:, 0:nb]),
           reads=R, writes=R, tag=f"{tag}rscale")
    p.emit("act", lambda e: e.activation(out=sg[:, 0:width], in_=src_ap, func=AF.Sign),
           reads=reads, writes=R, tag=f"{tag}sign")
    p.emit("act", lambda e: e.activation(out=aq[:, 0:width], in_=src_ap, func=AF.Abs),
           reads=reads, writes=R, tag=f"{tag}abs")
    p.emit("dve", lambda e: e.tensor_tensor(
        aq[:, 0:width].rearrange("p (b k) -> p b k", k=16),
        aq[:, 0:width].rearrange("p (b k) -> p b k", k=16),
        bmr[:, 0:nb, None].broadcast_to([128, nb, 16]), ALU.mult),
        reads=R, writes=R, tag=f"{tag}an")
    p.emit("dve", lambda e: e.tensor_scalar(ql[:, 0:width], aq[:, 0:width], 0.25, None, ALU.is_gt),
           reads=R, writes=R, tag=f"{tag}c0")
    for m in (0.75, 1.25, 1.75):
        p.emit("dve", lambda e, m=m: e.scalar_tensor_tensor(
            ql[:, 0:width], aq[:, 0:width], m, ql[:, 0:width], ALU.is_gt, ALU.add),
            reads=R, writes=R, tag=f"{tag}c{m}")
    p.emit("dve", lambda e: e.tensor_scalar(t1[:, 0:width], aq[:, 0:width], 2.5, None, ALU.is_gt),
           reads=R, writes=R, tag=f"{tag}c25")
    p.emit("dve", lambda e: e.scalar_tensor_tensor(
        t1[:, 0:width], aq[:, 0:width], 3.5, t1[:, 0:width], ALU.is_gt, ALU.add),
        reads=R, writes=R, tag=f"{tag}c35")
    p.emit("dve", lambda e: e.tensor_scalar(t2[:, 0:width], aq[:, 0:width], 5.0, None, ALU.is_gt),
           reads=R, writes=R, tag=f"{tag}c5")
    p.emit("dve", lambda e: e.scalar_tensor_tensor(
        t1[:, 0:width], t2[:, 0:width], 2.0, t1[:, 0:width], ALU.mult, ALU.add),
        reads=R, writes=R, tag=f"{tag}cmb1")
    p.emit("dve", lambda e: e.scalar_tensor_tensor(
        ql[:, 0:width], t1[:, 0:width], 2.0, ql[:, 0:width], ALU.mult, ALU.add),
        reads=R, writes=R, tag=f"{tag}cmb2")
    p.emit("dve", lambda e: e.tensor_tensor(
        ql[:, 0:width].rearrange("p (b k) -> p b k", k=16),
        ql[:, 0:width].rearrange("p (b k) -> p b k", k=16),
        bmh[:, 0:nb, None].broadcast_to([128, nb, 16]), ALU.mult),
        reads=R, writes=R, tag=f"{tag}dq1")
    p.emit("dve", lambda e: e.tensor_tensor(out_ap, ql[:, 0:width], sg[:, 0:width], ALU.mult),
           reads=R, writes=R + ["l1q_out"], tag=f"{tag}dq2")


def l1ffn_quant_weights(mk, v, tag):
    p = mk.p
    hbm = mk.hbm
    pt = mk.ps["PT"]
    ident = mk.sb["IDENT"]
    xnf, xq = v["XNF"], v["XQ"]
    jobs = [("L1_ffn_w_gate", "wgq", L1I, D), ("L1_ffn_w_up", "wuq", L1I, D),
            ("L1_ffn_w_down", "wdq", D, L1I)]
    for src_name, dst_name, nr, ncol in jobs:
        src = hbm[src_name]
        dst = hbm[dst_name]
        for rt in range(nr // 128):
            for c0 in range(0, ncol, 1024):
                w = min(1024, ncol - c0)
                p.dma("sp", "l1w", lambda e, src=src, rt=rt, c0=c0, w=w: e.dma_start(
                    out=xnf[:, 0:w], in_=src[rt * 128:(rt + 1) * 128, c0:c0 + w]),
                    reads=[f"M:{src_name}"], writes=["l1wf"], tag=f"{tag}ldw")
                quant_tile(mk, v, xnf[:, 0:w], xq[:, 0:w], w, f"{tag}q{dst_name}{rt}_{c0}",
                           ["l1wf"])
                for b in range(w // 128):
                    p.emit("te", lambda e, b=b: e.transpose(
                        pt[:, 0:128], xq[:, b * 128:(b + 1) * 128], ident[:, :]),
                        reads=["l1q_out", "IDENT"], writes=["PT"], tag=f"{tag}wT")
                    p.emit("dve", lambda e, b=b: e.tensor_copy(v["GT"][:, 0:128], pt[:, 0:128]),
                           reads=["PT"], writes=["l1gt"], tag=f"{tag}wTe")
                    p.dma("sp", "l1wq", lambda e, dst=dst, rt=rt, c0=c0, b=b: e.dma_start(
                        out=dst[c0 + b * 128: c0 + (b + 1) * 128, rt * 128:(rt + 1) * 128],
                        in_=v["GT"][:, 0:128]),
                        reads=["l1gt"], writes=[f"M:{dst_name}:{rt}_{c0}_{b}"], ckey="l1wq", tag=f"{tag}stw")


def l1ffn(mk, master_in, master_out, tag, quant_weights):
    p = mk.p
    hbm = mk.hbm
    xbf, xf = mk.sb["XBF"], mk.sb["XF"]
    ident, consts = mk.sb["IDENT"], mk.sb["CONSTS"]
    pa, pb, pc, pd, pt = mk.ps["PA"], mk.ps["PB"], mk.ps["PC"], mk.ps["PD"], mk.ps["PT"]
    v = _l1_views(mk)
    if quant_weights:
        mk.dram_tmp("wgq", (D, L1I), BF16)
        mk.dram_tmp("wuq", (D, L1I), BF16)
        mk.dram_tmp("wdq", (L1I, D), BF16)
        l1ffn_quant_weights(mk, v, tag + "w_")
        p.barrier()
    wgq, wuq, wdq = hbm["wgq"], hbm["wuq"], hbm["wdq"]
    cols = v["COLS"]
    ORANGES = [(i * 512, 512) for i in range(5)] + [(2560, 256)]

    for grp in range(ST // 4):
        tts = [grp * 4 + i for i in range(4)]
        for i, tt in enumerate(tts):
            for c in range(DC):
                sl = c % 4
                p.dma("sp", "l1xm", lambda e, c=c, tt=tt, sl=sl: e.dma_start(
                    out=xf[:, sl * 512: sl * 512 + 128],
                    in_=hbm[master_in][c * 128:(c + 1) * 128, tt * 128:(tt + 1) * 128]),
                    reads=[f"M:{master_in}"], writes=[f"xf{sl}"], tag=f"{tag}xml")
                p.emit("te", lambda e, c=c, sl=sl: e.transpose(
                    pd[:, 0:128], xf[:, sl * 512: sl * 512 + 128], mk.sb["IDENTF"][:, :]),
                    reads=[f"xf{sl}", "IDENTF"], writes=["PD0"], tag=f"{tag}xT")
                p.emit("dve", lambda e, c=c: e.tensor_copy(
                    v["XNF"][:, c * 128:(c + 1) * 128], pd[:, 0:128]),
                    reads=["PD0"], writes=["l1xnf"], tag=f"{tag}xTe")
            p.emit("dve", lambda e: e.tensor_reduce(cols[:, 0:1], v["XNF"][:, :], AX.X, ALU.add),
                   reads=["l1xnf"], writes=["l1cols"], tag=f"{tag}sum")
            p.emit("act", lambda e: e.activation(out=v["AQ"][:, :], in_=v["XNF"][:, :], func=AF.Square),
                   reads=["l1xnf"], writes=["l1q"], tag=f"{tag}sq")
            p.emit("dve", lambda e: e.tensor_reduce(cols[:, 1:2], v["AQ"][:, :], AX.X, ALU.add),
                   reads=["l1q"], writes=["l1cols"], tag=f"{tag}sumsq")
            p.emit("dve", lambda e: e.tensor_scalar(cols[:, 0:1], cols[:, 0:1], 1.0 / D, None, ALU.mult),
                   reads=["l1cols"], writes=["l1cols"], tag=f"{tag}mu")
            p.emit("act", lambda e: e.activation(out=cols[:, 2:3], in_=cols[:, 0:1], func=AF.Square),
                   reads=["l1cols"], writes=["l1cols"], tag=f"{tag}mu2")
            p.emit("dve", lambda e: e.scalar_tensor_tensor(
                cols[:, 1:2], cols[:, 1:2], 1.0 / D, cols[:, 2:3], ALU.mult, ALU.subtract),
                reads=["l1cols"], writes=["l1cols"], tag=f"{tag}var")
            p.emit("act", lambda e: e.activation(out=cols[:, 1:2], in_=cols[:, 1:2], func=AF.Sqrt,
                                                 bias=mk.sb["EPSC"][:, 0:1]),
                   reads=["l1cols", "EPSC"], writes=["l1cols"], tag=f"{tag}sdv")
            p.emit("dve", lambda e: e.reciprocal(cols[:, 1:2], cols[:, 1:2]),
                   reads=["l1cols"], writes=["l1cols"], tag=f"{tag}rsig")
            p.emit("dve", lambda e: e.tensor_scalar(
                v["XNF"][:, :], v["XNF"][:, :], cols[:, 0:1], cols[:, 1:2],
                ALU.subtract, ALU.mult),
                reads=["l1xnf", "l1cols"], writes=["l1xnf"], tag=f"{tag}xn")
            quant_tile(mk, v, v["XNF"][:, :], v["XQ"][:, :], 1024, f"{tag}qx{tt}", ["l1xnf"])
            for b in range(8):
                p.emit("te", lambda e, b=b: e.transpose(
                    pt[:, 0:128], v["XQ"][:, b * 128:(b + 1) * 128], ident[:, :]),
                    reads=["l1q_out", "IDENT"], writes=["PT"], tag=f"{tag}xqT")
                p.emit("dve", lambda e, i=i, b=b: e.tensor_copy(
                    v["XQT"][:, i * 1024 + b * 128: i * 1024 + (b + 1) * 128], pt[:, 0:128]),
                    reads=["PT"], writes=["l1xqt"], tag=f"{tag}xqTe")

        for o0, ow in ORANGES:
            for c in range(DC):
                p.dma("pool", "l1wg", lambda e, c=c, o0=o0, ow=ow: e.dma_start(
                    out=v["WSL"][:, c * 512: c * 512 + ow], in_=wgq[c * 128:(c + 1) * 128, o0:o0 + ow]),
                    reads=["M:wgq"], writes=[f"l1g{c}", f"l1d{c // 2}"], ckey="l1wg", tag=f"{tag}ldg")
                p.dma("pool", "l1wu", lambda e, c=c, o0=o0, ow=ow: e.dma_start(
                    out=v["WSL"][:, 4096 + c * 512: 4096 + c * 512 + ow],
                    in_=wuq[c * 128:(c + 1) * 128, o0:o0 + ow]),
                    reads=["M:wuq"], writes=[f"l1u{c}", f"l1d{4 + c // 2}"], ckey="l1wu", tag=f"{tag}ldu")
            for i, tt in enumerate(tts):
                bank = i % 2
                for c in range(DC):
                    p.emit("te", lambda e, i=i, c=c, ow=ow, bank=bank: e.matmul(
                        pa[:, bank * 512: bank * 512 + ow],
                        v["XQT"][:, i * 1024 + c * 128: i * 1024 + (c + 1) * 128],
                        v["WSL"][:, c * 512: c * 512 + ow],
                        start=(c == 0), stop=(c == DC - 1)),
                        reads=["l1xqt", f"l1g{c}"], writes=[f"PA{bank}"], tag=f"{tag}gmm")
                    p.emit("te", lambda e, i=i, c=c, ow=ow, bank=bank: e.matmul(
                        pb[:, bank * 512: bank * 512 + ow],
                        v["XQT"][:, i * 1024 + c * 128: i * 1024 + (c + 1) * 128],
                        v["WSL"][:, 4096 + c * 512: 4096 + c * 512 + ow],
                        start=(c == 0), stop=(c == DC - 1)),
                        reads=["l1xqt", f"l1u{c}"], writes=[f"PB{bank}"], tag=f"{tag}umm")
                p.emit("act", lambda e, ow=ow, bank=bank: e.activation(
                    out=v["XNF"][:, 0:ow], in_=pa[:, bank * 512: bank * 512 + ow], func=AF.Silu),
                    reads=[f"PA{bank}"], writes=["l1xnf"], tag=f"{tag}silu")
                p.emit("dve", lambda e, ow=ow, bank=bank: e.tensor_tensor(
                    v["HF"][:, 0:ow], v["XNF"][:, 0:ow], pb[:, bank * 512: bank * 512 + ow], ALU.mult),
                    reads=["l1xnf", f"PB{bank}"], writes=["l1hf"], tag=f"{tag}h")
                quant_tile(mk, v, v["HF"][:, 0:ow], v["HQ"][:, 0:ow], ow,
                           f"{tag}qh{tt}_{o0}", ["l1hf"])
                for b in range(ow // 128):
                    p.emit("te", lambda e, b=b: e.transpose(
                        pt[:, 0:128], v["HQ"][:, b * 128:(b + 1) * 128], ident[:, :]),
                        reads=["l1q_out", "IDENT"], writes=["PT"], tag=f"{tag}hT")
                    p.emit("dve", lambda e, i=i, o0=o0, b=b: e.tensor_copy(
                        v["HQT"][:, i * L1I + o0 + b * 128: i * L1I + o0 + (b + 1) * 128],
                        pt[:, 0:128]),
                        reads=["PT"], writes=["l1hqt"], tag=f"{tag}hTe")

        for pair in range(2):
            ptts = [(2 * pair, tts[2 * pair]), (2 * pair + 1, tts[2 * pair + 1])]
            banks = {(0, 0): ("PC", 0), (0, 1): ("PC", 1), (1, 0): ("PD", 0), (1, 1): ("PA", 0)}
            for (a, b) in ((0, 8), (8, 16), (16, 22)):
                for j in range(a, b):
                    dj = j - a
                    dw = ([f"l1g{2 * dj}", f"l1g{2 * dj + 1}"] if dj < 4
                          else [f"l1u{2 * (dj - 4)}", f"l1u{2 * (dj - 4) + 1}"])
                    p.dma("pool", "l1wd", lambda e, j=j, a=a: e.dma_start(
                        out=v["WSL"][:, (j - a) * 1024:(j - a) * 1024 + 1024],
                        in_=wdq[j * 128:(j + 1) * 128, 0:1024]),
                        reads=["M:wdq"], writes=[f"l1d{dj}"] + dw, ckey="l1wd", tag=f"{tag}ldd")
                for li, tt in ptts:
                    for oc2 in range(2):
                        pn, pk = banks[(li % 2, oc2)]
                        psum_t = mk.ps[pn]
                        p_reg = f"{pn}{pk}"
                        for j in range(a, b):
                            p.emit("te", lambda e, li=li, j=j, a=a, oc2=oc2, psum_t=psum_t, pk=pk: e.matmul(
                                psum_t[:, pk * 512: pk * 512 + 512],
                                v["HQT"][:, li * L1I + j * 128: li * L1I + (j + 1) * 128],
                                v["WSL"][:, (j - a) * 1024 + oc2 * 512: (j - a) * 1024 + oc2 * 512 + 512],
                                start=(j == 0), stop=(j == 21)),
                                reads=["l1hqt", f"l1d{j - a}"], writes=[p_reg], tag=f"{tag}dmm")
            for li, tt in ptts:
                for oc2 in range(2):
                    pn, pk = banks[(li % 2, oc2)]
                    psum_t = mk.ps[pn]
                    p_reg = f"{pn}{pk}"
                    p.emit("dve", lambda e, oc2=oc2, psum_t=psum_t, pk=pk: e.tensor_copy(
                        v["T1"][:, 0:512] if oc2 == 0 else v["T2"][:, 0:512],
                        psum_t[:, pk * 512: pk * 512 + 512]),
                        reads=[p_reg], writes=["l1q"], tag=f"{tag}dev")
                for oc in range(DC):
                    srcv = v["T1"] if oc < 4 else v["T2"]
                    b = oc % 4
                    p.emit("te", lambda e, srcv=srcv, b=b: e.transpose(
                        pt[:, 0:128], srcv[:, b * 128:(b + 1) * 128], ident[:, :]),
                        reads=["l1q", "IDENT"], writes=["PT"], tag=f"{tag}oT")
                    slot = oc % 4
                    xf_reg = f"xf{slot}"
                    p.dma("sp", "xf", lambda e, oc=oc, tt=tt, slot=slot: e.dma_start(
                        out=xf[:, slot * 512: slot * 512 + 128],
                        in_=hbm[master_in][oc * 128:(oc + 1) * 128, tt * 128:(tt + 1) * 128]),
                        reads=[f"M:{master_in}"], writes=[xf_reg], tag=f"{tag}ldxf")
                    p.emit("dve", lambda e, slot=slot: e.tensor_add(
                        xf[:, slot * 512: slot * 512 + 128], xf[:, slot * 512: slot * 512 + 128],
                        pt[:, 0:128]),
                        reads=["PT", xf_reg], writes=[xf_reg], tag=f"{tag}radd")
                    p.dma("sp", "xfo", lambda e, oc=oc, tt=tt, slot=slot: e.dma_start(
                        out=hbm[master_out][oc * 128:(oc + 1) * 128, tt * 128:(tt + 1) * 128],
                        in_=xf[:, slot * 512: slot * 512 + 128]),
                        reads=[xf_reg], writes=[f"M:{master_out}:{oc}_{tt}"], ckey="xfo", tag=f"{tag}stxf")
                    p.emit("dve", lambda e, oc=oc, tt=tt, slot=slot: e.tensor_copy(
                        xbf[:, oc * S + tt * 128: oc * S + (tt + 1) * 128],
                        xf[:, slot * 512: slot * 512 + 128]),
                        reads=[xf_reg], writes=[f"xbf{oc}"], tag=f"{tag}shad")



def _mla_views(mk):
    ar = mk.sb["AR"]
    v = {}
    v["QB"] = ar[:, 0:4096]
    v["KB"] = ar[:, 4096:8192]
    v["CQ"] = ar[:, 8192:12288]
    v["CKV"] = ar[:, 12288:16384]
    v["QRB"] = ar[:, 16384:20480]
    v["KRB"] = ar[:, 20480:24576]
    v["VB"] = ar[:, 24576:28672]
    v["OTB"] = ar[:, 28672:32768]
    v["ATB"] = ar[:, 32768:33792]
    v["RB"] = ar[:, 33792:34816].bitcast(F32)
    v["W"] = ar[:, 34816:35232]
    v["SQN"] = ar[:, 35232:35744]
    return v


def mla(mk, master_in, master_out, tag):
    p = mk.p
    hbm = mk.hbm
    xbf, xf, brs = mk.sb["XBF"], mk.sb["XF"], mk.sb["BRS"]
    row0, rowc, rowp, miscb = mk.sb["ROW0"], mk.sb["ROWC"], mk.sb["ROWP"], mk.sb["MISC"]
    onesc, ones1, ident, onesf = mk.sb["ONESC"], mk.sb["ONES1"], mk.sb["IDENT"], mk.sb["ONESF"]
    biasc, brow, bvrow = mk.sb["BIASC"], mk.sb["BROW"], mk.sb["BVROW"]
    pa, pb, pc, pd, pt = mk.ps["PA"], mk.ps["PB"], mk.ps["PC"], mk.ps["PD"], mk.ps["PT"]
    v = _mla_views(mk)
    NQG = S // 512
    SC = 1.0 / math.sqrt(160.0)
    NH = 8
    ot = hbm["ot_l0a"]

    ln_stats(mk, tag)

    p.dma("pool", "b1", lambda e: e.dma_start(out=biasc[:, 0:8], in_=hbm["b_uq_c"][:, :]),
          reads=["M:b_uq_c"], writes=["BIASC"], tag=f"{tag}b1")
    p.dma("pool", "b2", lambda e: e.dma_start(out=biasc[:, 8:16], in_=hbm["b_uk_c"][:, :]),
          reads=["M:b_uk_c"], writes=["BIASC"], tag=f"{tag}b2")
    p.dma("pool", "b3", lambda e: e.dma_start(out=biasc[0:32, 24:32], in_=hbm["b_qr_c"][:, :]),
          reads=["M:b_qr_c"], writes=["BIASC"], tag=f"{tag}b3")
    p.dma("pool", "b4", lambda e: e.dma_start(out=biasc[0:32, 32:33], in_=hbm["L0_attn_a_b_DQ"].rearrange("(a b) -> a b", b=1)),
          reads=["M:b_dq"], writes=["BIASC"], tag=f"{tag}b4")
    p.dma("pool", "b5", lambda e: e.dma_start(out=biasc[0:32, 33:34], in_=hbm["L0_attn_a_b_DKV"].rearrange("(a b) -> a b", b=1)),
          reads=["M:b_dkv"], writes=["BIASC"], tag=f"{tag}b5")
    p.dma("pool", "b6", lambda e: e.dma_start(out=biasc[0:32, 34:35], in_=hbm["L0_attn_a_b_KR"].rearrange("(a b) -> a b", b=1)),
          reads=["M:b_kr"], writes=["BIASC"], tag=f"{tag}b6")
    p.dma("pool", "b7", lambda e: e.dma_start(out=brow[:, :], in_=hbm["L0_attn_a_b_O"].rearrange("(a b) -> a b", a=1)),
          reads=["M:b_o"], writes=["BROW"], tag=f"{tag}b7")
    p.dma("pool", "b8", lambda e: e.dma_start(out=bvrow[:, :], in_=hbm["L0_attn_a_b_UV"].rearrange("(a b) -> a b", a=1)),
          reads=["M:b_uv"], writes=["BVROW"], tag=f"{tag}b8")

    down_jobs = [("L0_attn_a_W_DQ", v["CQ"], "mlaCQ", 32), ("L0_attn_a_W_DKV", v["CKV"], "mlaCKV", 33),
                 ("L0_attn_a_W_KR", v["KRB"], "mlaKRB", 34)]
    for i, (nm, dst, dreg, bcol) in enumerate(down_jobs):
        for c in range(DC):
            p.dma("pool", "wd", lambda e, nm=nm, c=c: e.dma_start(
                out=v["W"][:, c * 32: (c + 1) * 32], in_=hbm[nm][c * 128:(c + 1) * 128, :]),
                reads=[f"M:{nm}"], writes=["mlaW0"], tag=f"{tag}lddw{i}")
        for c in range(DC):
            p.emit("te", lambda e, i=i, c=c: e.matmul(
                pd[0:1, i * 32:(i + 1) * 32], onesc[:, :], v["W"][:, c * 32:(c + 1) * 32],
                start=(c == 0), stop=(c == DC - 1)),
                reads=["mlaW0", "ONESC"], writes=["PD0"], tag=f"{tag}cs{i}")
        p.emit("act", lambda e, i=i: e.activation(out=rowp[0:1, i * 32:(i + 1) * 32],
                                                  in_=pd[0:1, i * 32:(i + 1) * 32],
                                                  func=AF.Identity, scale=-1.0),
               reads=["PD0"], writes=["rowp"], tag=f"{tag}csev{i}")
        for g in range(NQG):
            gs = slice(g * 512, (g + 1) * 512)
            bank = g % 2
            p.emit("te", lambda e, i=i, gs=gs, bank=bank: e.matmul(
                pa[0:32, bank * 512:bank * 512 + 512], rowp[0:1, i * 32:(i + 1) * 32], row0[0:1, gs],
                start=True, stop=False),
                reads=["rowp", "row0"], writes=[f"PA{bank}"], tag=f"{tag}dseed{i}")
            for c in range(DC):
                p.emit("te", lambda e, c=c, g=g, bank=bank: e.matmul(
                    pa[0:32, bank * 512:bank * 512 + 512],
                    v["W"][:, c * 32: (c + 1) * 32],
                    xbf[:, c * S + g * 512: c * S + g * 512 + 512],
                    start=False, stop=(c == DC - 1)),
                    reads=["mlaW0", f"xbf{c}"], writes=[f"PA{bank}"], tag=f"{tag}dmm{i}")
            p.emit("dve", lambda e, gs=gs, bank=bank, dst=dst: e.scalar_tensor_tensor(
                dst[0:32, gs], pa[0:32, bank * 512:bank * 512 + 512], 1.0, brs[0:32, gs],
                ALU.mult, ALU.mult),
                reads=[f"PA{bank}", "BRS"], writes=[dreg], tag=f"{tag}dev{i}")
            p.emit("dve", lambda e, gs=gs, dst=dst, bcol=bcol: e.tensor_scalar(
                dst[0:32, gs], dst[0:32, gs], biasc[0:32, bcol:bcol + 1], None, ALU.add),
                reads=[dreg, "BIASC"], writes=[dreg], tag=f"{tag}db{i}")
    p.emit("dve", lambda e: e.memset(v["KRB"][32:33, :], 1.0), writes=["mla_krones"], tag=f"{tag}krones")

    for h in range(NH):
        woff = 0
        wreg = "mlawh"
        for nm, o2, wdt in (("L0_attn_a_W_UQ", 0, 128), ("L0_attn_a_W_UK", 128, 128),
                            ("L0_attn_a_W_UV", 256, 128)):
            p.dma("pool", "wh", lambda e, nm=nm, o2=o2, h=h, woff=woff, wdt=wdt: e.dma_start(
                out=v["W"][0:32, woff + o2: woff + o2 + wdt], in_=hbm[nm][:, h * 128:(h + 1) * 128]),
                reads=[f"M:{nm}"], writes=[wreg, "mlaW0"], tag=f"{tag}ldwh{h}")
        p.dma("pool", "wh", lambda e, h=h, woff=woff: e.dma_start(
            out=v["W"][0:32, woff + 384: woff + 416], in_=hbm["L0_attn_a_W_QR"][:, h * 32:(h + 1) * 32]),
            reads=["M:L0_attn_a_W_QR"], writes=[wreg, "mlaW0"], tag=f"{tag}ldwq{h}")
        for g in range(NQG):
            gs = slice(g * 512, (g + 1) * 512)
            bank = g % 2
            p.emit("te", lambda e, gs=gs, bank=bank, woff=woff: e.matmul(
                pa[:, bank * 512:bank * 512 + 512], v["W"][0:32, woff:woff + 128], v["CQ"][0:32, gs],
                start=True, stop=True),
                reads=[wreg, "mlaCQ"], writes=[f"PA{bank}"], tag=f"{tag}qmm{h}")
            p.emit("dve", lambda e, gs=gs, bank=bank, h=h: e.tensor_scalar(
                v["QB"][:, gs], pa[:, bank * 512:bank * 512 + 512],
                biasc[:, h:h + 1], SC, ALU.add, ALU.mult),
                reads=[f"PA{bank}", "BIASC"], writes=["mlaQB"], tag=f"{tag}qev{h}")
            p.emit("te", lambda e, gs=gs, bank=bank, woff=woff: e.matmul(
                pb[:, bank * 512:bank * 512 + 512], v["W"][0:32, woff + 128:woff + 256], v["CKV"][0:32, gs],
                start=True, stop=True),
                reads=[wreg, "mlaCKV"], writes=[f"PB{bank}"], tag=f"{tag}kmm{h}")
            p.emit("dve", lambda e, gs=gs, bank=bank, h=h: e.tensor_scalar(
                v["KB"][:, gs], pb[:, bank * 512:bank * 512 + 512],
                biasc[:, 8 + h:9 + h], None, ALU.add),
                reads=[f"PB{bank}", "BIASC"], writes=["mlaKB"], tag=f"{tag}kev{h}")
            p.emit("te", lambda e, gs=gs, bank=bank, woff=woff: e.matmul(
                pc[0:32, bank * 512:bank * 512 + 512], v["W"][0:32, woff + 384:woff + 416], v["CQ"][0:32, gs],
                start=True, stop=True),
                reads=[wreg, "mlaCQ"], writes=[f"PC{bank}"], tag=f"{tag}rmm{h}")
            p.emit("dve", lambda e, gs=gs, bank=bank, h=h: e.tensor_scalar(
                v["QRB"][0:32, gs], pc[0:32, bank * 512:bank * 512 + 512],
                biasc[0:32, 24 + h:25 + h], SC, ALU.add, ALU.mult),
                reads=[f"PC{bank}", "BIASC"], writes=["mlaQRB"], tag=f"{tag}rev{h}")
        for tt in range(ST):
            p.emit("te", lambda e, tt=tt, h=h: e.matmul(
                pd[:, 0:128], ones1[0:1, 0:128], bvrow[0:1, h * 128:(h + 1) * 128],
                start=True, stop=False),
                reads=["ONES1", "BVROW"], writes=["PD0"], tag=f"{tag}vseed{h}")
            p.emit("te", lambda e, tt=tt, woff=woff: e.matmul(
                pd[:, 0:128], v["CKV"][0:32, tt * 128:(tt + 1) * 128], v["W"][0:32, woff + 256:woff + 384],
                start=False, stop=True),
                reads=[wreg, "mlaCKV"], writes=["PD0"], tag=f"{tag}vmm{h}")
            p.emit("dve", lambda e, tt=tt: e.tensor_copy(v["VB"][:, tt * 128:(tt + 1) * 128], pd[:, 0:128]),
                   reads=["PD0"], writes=["mlaVB"], tag=f"{tag}vev{h}")
        for g in range(NQG):
            gs = slice(g * 512, (g + 1) * 512)
            sb2 = 0
            p.emit("act", lambda e, gs=gs, sb2=sb2: e.activation(
                out=v["SQN"][:, sb2:sb2 + 512], in_=v["KB"][:, gs], func=AF.Square),
                reads=["mlaKB"], writes=[f"mlasq{sb2}", "mlaW0"], tag=f"{tag}k2{h}")
            p.emit("te", lambda e, sb2=sb2: e.matmul(
                pd[0:1, 0:512], onesc[:, :], v["SQN"][:, sb2:sb2 + 512], start=True, stop=False),
                reads=[f"mlasq{sb2}", "ONESC"], writes=["PD0"], tag=f"{tag}k2mm{h}")
            p.emit("act", lambda e, gs=gs, sb2=sb2: e.activation(
                out=v["SQN"][0:32, sb2:sb2 + 512], in_=v["KRB"][0:32, gs], func=AF.Square),
                reads=["mlaKRB"], writes=[f"mlasq{sb2}", "mlaW0"], tag=f"{tag}kr2{h}")
            p.emit("te", lambda e, sb2=sb2: e.matmul(
                pd[0:1, 0:512], onesc[0:32, :], v["SQN"][0:32, sb2:sb2 + 512], start=False, stop=True),
                reads=[f"mlasq{sb2}", "ONESC"], writes=["PD0"], tag=f"{tag}kr2mm{h}")
            p.emit("dve", lambda e, g=g: e.tensor_reduce(
                miscb[0:1, g:g + 1], pd[0:1, 0:512], AX.X, ALU.max),
                reads=["PD0"], writes=["misc"], tag=f"{tag}k2max{h}")
        p.emit("dve", lambda e: e.tensor_reduce(miscb[0:1, 8:9], miscb[0:1, 0:8], AX.X, ALU.max),
               reads=["misc"], writes=["misc"], tag=f"{tag}mk2{h}")
        p.emit("act", lambda e: e.activation(out=miscb[0:1, 9:10], in_=miscb[0:1, 8:9], func=AF.Sqrt),
               reads=["misc"], writes=["misc"], tag=f"{tag}mk{h}")
        p.emit("dve", lambda e: e.tensor_scalar(miscb[0:1, 10:11], miscb[0:1, 9:10], -1.0, None, ALU.mult),
               reads=["misc"], writes=["misc"], tag=f"{tag}mkn{h}")
        for g in range(NQG):
            gs = slice(g * 512, (g + 1) * 512)
            sb2 = 0
            p.emit("act", lambda e, gs=gs, sb2=sb2: e.activation(
                out=v["SQN"][:, sb2:sb2 + 512], in_=v["QB"][:, gs], func=AF.Square),
                reads=["mlaQB"], writes=[f"mlasq{sb2}", "mlaW0"], tag=f"{tag}q2{h}")
            p.emit("te", lambda e, sb2=sb2: e.matmul(
                pd[0:1, 0:512], onesc[:, :], v["SQN"][:, sb2:sb2 + 512], start=True, stop=False),
                reads=[f"mlasq{sb2}", "ONESC"], writes=["PD0"], tag=f"{tag}q2mm{h}")
            p.emit("act", lambda e, gs=gs, sb2=sb2: e.activation(
                out=v["SQN"][0:32, sb2:sb2 + 512], in_=v["QRB"][0:32, gs], func=AF.Square),
                reads=["mlaQRB"], writes=[f"mlasq{sb2}", "mlaW0"], tag=f"{tag}qr2{h}")
            p.emit("te", lambda e, sb2=sb2: e.matmul(
                pd[0:1, 0:512], onesc[0:32, :], v["SQN"][0:32, sb2:sb2 + 512], start=False, stop=True),
                reads=[f"mlasq{sb2}", "ONESC"], writes=["PD0"], tag=f"{tag}qr2mm{h}")
            p.emit("act", lambda e, gs=gs: e.activation(
                out=rowc[0:1, gs], in_=pd[0:1, 0:512], func=AF.Sqrt),
                reads=["PD0"], writes=["rowc"], tag=f"{tag}qn{h}")
        p.emit("dve", lambda e: e.tensor_scalar(
            v["QRB"][32:33, :], rowc[0:1, :], miscb[0:1, 10:11], NEG_MARGIN, ALU.mult, ALU.add),
            reads=["rowc", "misc"], writes=["mla_negc"], tag=f"{tag}negc{h}")
        for g in range(NQG):
            gs = slice(g * 512, (g + 1) * 512)
            oband = g % 2
            pc_reg = f"PC{oband}"
            def m_smm(kt):
                bank = kt % 2
                pa_reg = f"PA{bank}"
                p.emit("te", lambda e, kt=kt, gs=gs, bank=bank: e.matmul(
                    pa[:, bank * 512:bank * 512 + 512], v["KB"][:, kt * 128:(kt + 1) * 128],
                    v["QB"][:, gs], start=True, stop=False),
                    reads=["mlaKB", "mlaQB"], writes=[pa_reg], tag=f"{tag}smm{h}")
                p.emit("te", lambda e, kt=kt, gs=gs, bank=bank: e.matmul(
                    pa[:, bank * 512:bank * 512 + 512], v["KRB"][0:33, kt * 128:(kt + 1) * 128],
                    v["QRB"][0:33, gs], start=False, stop=True),
                    reads=["mlaKRB", "mla_krones", "mlaQRB", "mla_negc"], writes=[pa_reg],
                    tag=f"{tag}smm2{h}")
                p.emit("act", lambda e, bank=bank: e.activation(
                    out=v["ATB"][:, bank * 512:bank * 512 + 512],
                    in_=pa[:, bank * 512:bank * 512 + 512], func=AF.Exp),
                    reads=[pa_reg], writes=[f"mlaatb{bank}"], tag=f"{tag}exp{h}")

            def m_av(kt):
                bank = kt % 2
                p.emit("te", lambda e, kt=kt, bank=bank, oband=oband, st=(kt == 0), sp=(kt == 31): e.matmul(
                    pc[:, oband * 512:oband * 512 + 512], v["VB"][:, kt * 128:(kt + 1) * 128],
                    v["ATB"][:, bank * 512:bank * 512 + 512], start=st, stop=sp),
                    reads=["mlaVB", f"mlaatb{bank}"], writes=[pc_reg], tag=f"{tag}av{h}")
                p.emit("te", lambda e, kt=kt, bank=bank, st=(kt == 0), sp=(kt == 31): e.matmul(
                    pd[0:1, 0:512], onesc[:, 0:1], v["ATB"][:, bank * 512:bank * 512 + 512],
                    start=st, stop=sp),
                    reads=["ONESC", f"mlaatb{bank}"], writes=["PD0"], tag=f"{tag}rsum{h}")

            for kt in range(32):
                m_smm(kt)
                if kt > 0:
                    m_av(kt - 1)
            m_av(31)
            p.emit("dve", lambda e: e.reciprocal(xf[0:1, 0:512], pd[0:1, 0:512]),
                   reads=["PD0"], writes=["xf0"], tag=f"{tag}rrec{h}")
            p.emit("te", lambda e: e.matmul(
                pb[:, 0:512], onesf[0:1, 0:128], xf[0:1, 0:512], start=True, stop=True),
                reads=["xf0", "ONESF"], writes=["PB0"], tag=f"{tag}rbc{h}")
            p.emit("dve", lambda e: e.tensor_copy(v["RB"][:, 0:512], pb[:, 0:512]),
                reads=["PB0"], writes=["mlarb"], tag=f"{tag}rbev{h}")
            p.emit("dve", lambda e, gs=gs, oband=oband: e.tensor_mul(
                v["OTB"][:, gs], pc[:, oband * 512:oband * 512 + 512], v["RB"][:, 0:512]),
                reads=[pc_reg, "mlarb"], writes=["mlaOTB"], tag=f"{tag}oev{h}")
        p.dma("sp", "ot", lambda e, h=h: e.dma_start(out=ot[h], in_=v["OTB"][:, :]),
              reads=["mlaOTB"], writes=[f"M:ot{h}"], tag=f"{tag}otst{h}")

    ar = mk.sb["AR"]
    wob = ar[:, 0:8192]
    for hq in range(NH):
        p.dma("pool", "wob", lambda e, hq=hq: e.dma_start(
            out=wob[:, hq * 1024:(hq + 1) * 1024],
            in_=hbm["L0_attn_a_W_O"][hq * 128:(hq + 1) * 128, :]),
            reads=["M:L0_attn_a_W_O"], writes=["mlaWOB", "mlaQB", "mlaKB"],
            tag=f"{tag}ldwob{hq}")
    for g in range(NQG):
        gs = slice(g * 512, (g + 1) * 512)
        for hq in range(NH):
            p.dma("sp", "ots", lambda e, hq=hq, g=g: e.dma_start(
                out=v["OTB"][:, hq * 512:(hq + 1) * 512], in_=ot[hq, :, g * 512:(g + 1) * 512]),
                reads=[f"M:ot{k}" for k in range(NH)], writes=[f"ots{hq}"],
                tag=f"{tag}ldots{g}_{hq}")
        for oc in range(DC):
            bank = oc % 2
            pa_reg = f"PA{bank}"
            slot = (g * DC + oc) % 4
            xf_reg = f"xf{slot}"
            p.dma("sp", "xf", lambda e, oc=oc, gs=gs, slot=slot: e.dma_start(
                out=xf[:, slot * 512:(slot + 1) * 512],
                in_=hbm[master_in][oc * 128:(oc + 1) * 128, gs.start:gs.stop]),
                reads=[f"M:{master_in}"], writes=[xf_reg], tag=f"{tag}ldxf")
            p.emit("te", lambda e, oc=oc, bank=bank: e.matmul(
                pa[:, bank * 512:bank * 512 + 512], brow[0:1, oc * 128:(oc + 1) * 128],
                ones1[0:1, 0:512], start=True, stop=False),
                reads=["BROW", "ONES1"], writes=[pa_reg], tag=f"{tag}boseed{oc}")
            for hq in range(NH):
                p.emit("te", lambda e, hq=hq, oc=oc, bank=bank, sp=(hq == NH - 1): e.matmul(
                    pa[:, bank * 512:bank * 512 + 512],
                    wob[:, hq * 1024 + oc * 128: hq * 1024 + (oc + 1) * 128],
                    v["OTB"][:, hq * 512:(hq + 1) * 512], start=False, stop=sp),
                    reads=["mlaWOB", f"ots{hq}"], writes=[pa_reg], tag=f"{tag}womm{oc}")
            p.emit("dve", lambda e, slot=slot, bank=bank: e.tensor_add(
                xf[:, slot * 512:(slot + 1) * 512], xf[:, slot * 512:(slot + 1) * 512],
                pa[:, bank * 512:bank * 512 + 512]),
                reads=[pa_reg, xf_reg], writes=[xf_reg], tag=f"{tag}radd")
            p.dma("sp", "xfo", lambda e, oc=oc, gs=gs, slot=slot: e.dma_start(
                out=hbm[master_out][oc * 128:(oc + 1) * 128, gs.start:gs.stop],
                in_=xf[:, slot * 512:(slot + 1) * 512]),
                reads=[xf_reg], writes=[f"M:{master_out}:{oc}_{g}"], ckey="xfo", tag=f"{tag}stxf")
            p.emit("dve", lambda e, oc=oc, gs=gs, slot=slot: e.tensor_copy(
                xbf[:, oc * S + gs.start: oc * S + gs.stop], xf[:, slot * 512:(slot + 1) * 512]),
                reads=[xf_reg], writes=[f"xbf{oc}"], tag=f"{tag}shad")



def _l0f_views(mk):
    ar = mk.sb["AR"]
    v = {}
    v["TBL"] = ar[:, 0:16384]
    v["QT"] = ar[:, 16384:20480]
    v["KT"] = ar[:, 20480:24576]
    v["VB"] = ar[:, 24576:26656]
    v["WQ"] = ar[:, 26656:28192]
    v["ATB"] = ar[:, 28192:29216]
    v["OTB"] = ar[:, 29216:33312]
    v["SQN"] = ar[:, 33312:33824]
    v["RB"] = ar[:, 33824:34848].bitcast(F32)
    v["SH"] = ar[:, 34848:35360]
    v["TMPA"] = ar[:, 35360:35872]
    v["H4G"] = ar[:, 0:11264]
    v["W1SL"] = ar[:, 11264:15360]
    v["W2SL"] = ar[:, 16384:19200]
    v["GB"] = ar[:, 19200:19712]
    v["AB"] = ar[:, 19712:20224]
    v["GELB"] = ar[:, 20224:20736]
    v["LGC"] = ar[:, 20736:20800].bitcast(F32)
    v["LBC"] = ar[:, 20800:20816]
    v["B1R"] = ar[0:1, 20864:26496]
    v["CS1"] = ar[0:1, 26496:26752]
    v["QSC"] = ar[:, 35872:35904].bitcast(F32)
    v["NC1"] = ar[0:1, 27712:27968]
    v["B2C"] = ar[:, 27968:28032].bitcast(F32)
    v["SDR"] = ar[0:1, 28064:32160]
    return v


def _l0f_shift_rhs(xbf, c, lo, hi):
    if c < 4:
        return xbf[:, c * S + lo: c * S + hi], 0
    if lo == 0:
        return xbf[:, c * S + 0: c * S + hi - 1], 1
    return xbf[:, c * S + lo - 1: c * S + hi - 1], 0


def l0ffn(mk, master_in, master_out, tag):
    p = mk.p
    hbm = mk.hbm
    xbf, xf, brs, sq = mk.sb["XBF"], mk.sb["XF"], mk.sb["BRS"], mk.sb["SQ"]
    row0, rowc, rowp, miscb = mk.sb["ROW0"], mk.sb["ROWC"], mk.sb["ROWP"], mk.sb["MISC"]
    onesc, ones1, ident, cmask = mk.sb["ONESC"], mk.sb["ONES1"], mk.sb["IDENT"], mk.sb["CMASK"]
    onesf = mk.sb["ONESF"]
    rmask = mk.sb["RMASK"]
    pa, pb, pc, pd, pt = mk.ps["PA"], mk.ps["PB"], mk.ps["PC"], mk.ps["PD"], mk.ps["PT"]
    v = _l0f_views(mk)
    NQG = S // 512
    NH, HD = 8, 64
    ot = hbm["ot_l0f"]
    zm = hbm["zm"]

    ln_stats(mk, tag + "zi")
    bmu = v["TBL"][:, 0:4096]
    for g in range(NQG):
        bank = g % 2
        p.emit("te", lambda e, g=g, bank=bank: e.matmul(
            pc[:, bank * 512:bank * 512 + 512], ones1[0:1, 0:128], row0[0:1, g * 512:(g + 1) * 512],
            start=True, stop=True),
            reads=["row0", "ONES1"], writes=[f"PC{bank}"], tag=f"{tag}bmumm")
        p.emit("dve", lambda e, g=g, bank=bank: e.tensor_copy(
            bmu[:, g * 512:(g + 1) * 512], pc[:, bank * 512:bank * 512 + 512]),
            reads=[f"PC{bank}"], writes=["l0tbl"], tag=f"{tag}bmuev")
    for oc in range(DC):
        for g in range(NQG):
            gs = slice(g * 512, (g + 1) * 512)
            slot = (oc * NQG + g) % 4
            xf_reg = f"xf{slot}"
            p.dma("sp", "xf", lambda e, oc=oc, gs=gs, slot=slot: e.dma_start(
                out=xf[:, slot * 512:(slot + 1) * 512],
                in_=hbm[master_in][oc * 128:(oc + 1) * 128, gs.start:gs.stop]),
                reads=[f"M:{master_in}"], writes=[xf_reg], tag=f"{tag}zl")
            p.emit("dve", lambda e, gs=gs, slot=slot: e.tensor_sub(
                xf[:, slot * 512:(slot + 1) * 512], xf[:, slot * 512:(slot + 1) * 512], bmu[:, gs]),
                reads=["l0tbl", xf_reg], writes=[xf_reg], tag=f"{tag}zsub")
            p.emit("dve", lambda e, gs=gs, slot=slot: e.tensor_mul(
                xf[:, slot * 512:(slot + 1) * 512], xf[:, slot * 512:(slot + 1) * 512], brs[:, gs]),
                reads=["BRS", xf_reg], writes=[xf_reg], tag=f"{tag}zmul")
            p.dma("sp", "zm0", lambda e, oc=oc, gs=gs, slot=slot: e.dma_start(
                out=zm[oc * 128:(oc + 1) * 128, gs.start:gs.stop],
                in_=xf[:, slot * 512:(slot + 1) * 512]),
                reads=[xf_reg], writes=[f"M:zm:{oc}_{g}"], ckey="zm0", tag=f"{tag}zs")
            p.emit("dve", lambda e, oc=oc, gs=gs, slot=slot: e.tensor_copy(
                xbf[:, oc * S + gs.start: oc * S + gs.stop], xf[:, slot * 512:(slot + 1) * 512]),
                reads=[xf_reg], writes=[f"xbf{oc}"], tag=f"{tag}zshad")

    for li in range(2):
        lt = f"{tag}l{li}_"
        for ti, nm in enumerate(("c_l0f_cos", "c_l0f_ssin", "c_l0f_xs", "c_l0f_xsi")):
            p.dma("pool", "tbl", lambda e, ti=ti, nm=nm: e.dma_start(
                out=v["TBL"][0:64, ti * 4096:(ti + 1) * 4096], in_=hbm[nm][:, :]),
                reads=[f"M:{nm}"], writes=["l0tbl"], tag=f"{lt}tbl")
        cosl = v["TBL"][:, 0:4096]
        sinl = v["TBL"][:, 4096:8192]
        xst = v["TBL"][:, 8192:12288]
        xsi = v["TBL"][:, 12288:16384]
        p.dma("pool", "qsc", lambda e, li=li: e.dma_start(
            out=v["QSC"][0:64, 0:1], in_=hbm["L0_ffn_q_scale"][li].rearrange("(a b) -> a b", b=1)),
            reads=["M:qs"], writes=["l0qsc"], tag=f"{lt}qs")
        p.dma("pool", "qsc", lambda e, li=li: e.dma_start(
            out=v["QSC"][0:64, 1:2], in_=hbm["L0_ffn_k_scale"][li].rearrange("(a b) -> a b", b=1)),
            reads=["M:ks"], writes=["l0qsc"], tag=f"{lt}ks")
        p.emit("dve", lambda e: e.tensor_scalar(v["QSC"][0:64, 2:3], v["QSC"][0:64, 0:1],
                                                8.0, None, ALU.mult),
               reads=["l0qsc"], writes=["l0qsc"], tag=f"{lt}qs8")

        for h in range(NH):
            for c in range(DC):
                p.dma("pool", "wq0", lambda e, li=li, h=h, c=c: e.dma_start(
                    out=v["WQ"][:, c * 192:(c + 1) * 192], in_=hbm["w_l0f_qkv"][li, h, c * 128:(c + 1) * 128, :]),
                    reads=["M:w_l0f_qkv"], writes=[f"l0wq{c}"], ckey="wq0", tag=f"{lt}ldw{h}")
            for g in range(NQG):
                gs = slice(g * 512, (g + 1) * 512)
                bank = g % 2
                for c in range(DC):
                    rhs, ofs = _l0f_shift_rhs(xbf, c, g * 512, (g + 1) * 512)
                    for which, psum_t, col0 in (("q", pa, 0), ("k", pb, 64), ("v", pc, 128)):
                        p.emit("te", lambda e, rhs=rhs, ofs=ofs, c=c, bank=bank, psum_t=psum_t, col0=col0: e.matmul(
                            psum_t[0:64, bank * 512 + ofs: bank * 512 + 512],
                            v["WQ"][:, c * 192 + col0: c * 192 + col0 + 64], rhs,
                            start=(c == 0), stop=(c == DC - 1)),
                            reads=[f"l0wq{c}", f"xbf{c}"],
                            writes=[f"{'PA' if psum_t is pa else ('PB' if psum_t is pb else 'PC')}{bank}"],
                            tag=f"{lt}{which}mm{h}")
                p.emit("dve", lambda e, gs=gs, bank=bank: e.tensor_copy(
                    v["QT"][0:64, gs], pa[0:64, bank * 512:bank * 512 + 512]),
                    reads=[f"PA{bank}"], writes=["l0QT"], tag=f"{lt}qev{h}")
                p.emit("dve", lambda e, gs=gs, bank=bank: e.tensor_copy(
                    v["KT"][0:64, gs], pb[0:64, bank * 512:bank * 512 + 512]),
                    reads=[f"PB{bank}"], writes=["l0KT"], tag=f"{lt}kev{h}")
                p.emit("dve", lambda e, bank=bank: e.tensor_copy(
                    v["SQN"][0:64, 0:512], pc[0:64, bank * 512:bank * 512 + 512]),
                    reads=[f"PC{bank}"], writes=["l0sq"], tag=f"{lt}vev{h}")
                for b in range(4):
                    p.emit("te", lambda e, b=b: e.transpose(
                        pt[:, 0:64], v["SQN"][0:64, b * 128:(b + 1) * 128],
                        ident[0:64, 0:64]),
                        reads=["l0sq", "IDENT"], writes=["PT"], tag=f"{lt}vT{h}")
                    p.emit("dve", lambda e, g=g, b=b: e.tensor_copy(
                        v["VB"][:, (g * 4 + b) * 65: (g * 4 + b) * 65 + 64], pt[:, 0:64]),
                        reads=["PT"], writes=["l0VB"], tag=f"{lt}vTe{h}")
            p.emit("dve", lambda e: e.memset(
                v["VB"][:].rearrange("p (t c) -> p t c", c=65)[:, :, 64:65], 1.0),
                writes=["l0VB"], tag=f"{lt}vones{h}")
            for (buf, sc_col, nrm) in ((v["QT"], 2, "q"), (v["KT"], 1, "k")):
                breg = "l0QT" if nrm == "q" else "l0KT"
                for g in range(NQG):
                    gs = slice(g * 512, (g + 1) * 512)
                    p.emit("act", lambda e, buf=buf, gs=gs: e.activation(
                        out=v["SQN"][0:64, 0:512], in_=buf[0:64, gs], func=AF.Square),
                        reads=[breg], writes=["l0sq"], tag=f"{lt}n2{nrm}{h}")
                    p.emit("te", lambda e: e.matmul(
                        pd[0:1, 0:512], onesc[0:64, :], v["SQN"][0:64, 0:512], start=True, stop=True),
                        reads=["l0sq", "ONESC"], writes=["PD0"], tag=f"{lt}n2mm{nrm}{h}")
                    p.emit("act", lambda e: e.activation(
                        out=xf[0:1, 0:512], in_=pd[0:1, 0:512], func=AF.Sqrt),
                        reads=["PD0"], writes=["xf0"], tag=f"{lt}nrm{nrm}{h}")
                    p.emit("dve", lambda e: e.tensor_scalar(xf[0:1, 0:512], xf[0:1, 0:512],
                                                            1e-12, None, ALU.max),
                           reads=["xf0"], writes=["xf0"], tag=f"{lt}clip{nrm}{h}")
                    p.emit("dve", lambda e: e.reciprocal(xf[0:1, 0:512], xf[0:1, 0:512]),
                           reads=["xf0"], writes=["xf0"], tag=f"{lt}rq{nrm}{h}")
                    p.emit("te", lambda e: e.matmul(
                        pd[0:64, 0:512], onesf[0:1, 0:64], xf[0:1, 0:512], start=True, stop=True),
                        reads=["xf0", "ONESF"], writes=["PD0"], tag=f"{lt}rqbc{nrm}{h}")
                    p.emit("dve", lambda e, buf=buf, gs=gs: e.tensor_mul(
                        buf[0:64, gs], buf[0:64, gs], pd[0:64, 0:512]),
                        reads=[breg, "PD0"], writes=[breg], tag=f"{lt}nmul{nrm}{h}")
                    p.emit("dve", lambda e, buf=buf, gs=gs, sc_col=sc_col: e.tensor_scalar(
                        buf[0:64, gs], buf[0:64, gs], v["QSC"][0:64, sc_col:sc_col + 1], None, ALU.mult),
                        reads=[breg, "l0qsc"], writes=[breg], tag=f"{lt}smul{nrm}{h}")
            for g in range(NQG):
                gs = slice(g * 512, (g + 1) * 512)
                p.emit("act", lambda e, gs=gs: e.activation(
                    out=v["SQN"][0:64, 0:512], in_=v["QT"][0:64, gs], func=AF.Square),
                    reads=["l0QT"], writes=["l0sq"], tag=f"{lt}u2{h}")
                p.emit("te", lambda e: e.matmul(
                    pd[0:1, 0:512], onesc[0:64, :], v["SQN"][0:64, 0:512], start=True, stop=True),
                    reads=["l0sq", "ONESC"], writes=["PD0"], tag=f"{lt}u2mm{h}")
                p.emit("act", lambda e, gs=gs: e.activation(
                    out=rowc[0:1, gs], in_=pd[0:1, 0:512], func=AF.Sqrt),
                    reads=["PD0"], writes=["rowc"], tag=f"{lt}un{h}")
                p.emit("act", lambda e, gs=gs: e.activation(
                    out=v["SQN"][0:64, 0:512], in_=v["KT"][0:64, gs], func=AF.Square),
                    reads=["l0KT"], writes=["l0sq"], tag=f"{lt}ku2{h}")
                p.emit("te", lambda e: e.matmul(
                    pb[0:1, 0:512], onesc[0:64, :], v["SQN"][0:64, 0:512], start=True, stop=True),
                    reads=["l0sq", "ONESC"], writes=["PB0"], tag=f"{lt}ku2mm{h}")
                p.emit("dve", lambda e, g=g: e.tensor_reduce(
                    miscb[0:1, g:g + 1], pb[0:1, 0:512], AX.X, ALU.max),
                    reads=["PB0"], writes=["misc"], tag=f"{lt}kumax{h}")
            for w in range(NQG):
                lo = max(0, w - 1)
                p.emit("dve", lambda e, w=w, lo=lo: e.tensor_reduce(
                    miscb[0:1, 8 + w:9 + w], miscb[0:1, lo:w + 1], AX.X, ALU.max),
                    reads=["misc"], writes=["misc"], tag=f"{lt}mkw{h}")
            p.emit("act", lambda e: e.activation(out=miscb[0:1, 8:16], in_=miscb[0:1, 8:16], func=AF.Sqrt),
                   reads=["misc"], writes=["misc"], tag=f"{lt}mks{h}")
            p.emit("dve", lambda e: e.tensor_scalar(miscb[0:1, 8:16], miscb[0:1, 8:16], -1.0, None, ALU.mult),
                   reads=["misc"], writes=["misc"], tag=f"{lt}mkn{h}")
            for w in range(NQG):
                gs = slice(w * 512, (w + 1) * 512)
                p.emit("dve", lambda e, gs=gs, w=w: e.tensor_scalar(
                    v["QT"][64:65, gs], rowc[0:1, gs], miscb[0:1, 8 + w:9 + w], NEG_MARGIN,
                    ALU.mult, ALU.add),
                    reads=["rowc", "misc"], writes=["l0negc"], tag=f"{lt}negc{h}")
            p.emit("dve", lambda e: e.memset(v["KT"][64:65, :], 1.0), writes=["l0kones"], tag=f"{lt}ko{h}")
            for (buf, breg, xtab) in ((v["QT"], "l0QT", xst), (v["KT"], "l0KT", xsi)):
                for g in range(NQG):
                    gs = slice(g * 512, (g + 1) * 512)
                    p.emit("dve", lambda e, buf=buf, gs=gs: e.tensor_copy(
                        v["SH"][0:32, :], buf[32:64, gs]),
                        reads=[breg], writes=["l0sh"], tag=f"{lt}r1{h}")
                    p.emit("dve", lambda e, buf=buf, gs=gs: e.tensor_copy(
                        v["SH"][32:64, :], buf[0:32, gs]),
                        reads=[breg], writes=["l0sh"], tag=f"{lt}r2{h}")
                    p.emit("dve", lambda e, buf=buf, gs=gs: e.tensor_mul(
                        v["TMPA"][0:64, :], buf[0:64, gs], cosl[0:64, gs]),
                        reads=[breg, "l0tbl"], writes=["l0tmpa"], tag=f"{lt}r3{h}")
                    p.emit("dve", lambda e, gs=gs: e.tensor_mul(
                        v["SH"][0:64, :], v["SH"][0:64, :], sinl[0:64, gs]),
                        reads=["l0sh", "l0tbl"], writes=["l0sh"], tag=f"{lt}r4{h}")
                    p.emit("dve", lambda e, buf=buf, gs=gs: e.tensor_add(
                        buf[0:64, gs], v["TMPA"][0:64, :], v["SH"][0:64, :]),
                        reads=["l0tmpa", "l0sh"], writes=[breg], tag=f"{lt}r5{h}")
                    p.emit("dve", lambda e, buf=buf, gs=gs, xtab=xtab: e.tensor_mul(
                        buf[0:64, gs], buf[0:64, gs], xtab[0:64, gs]),
                        reads=[breg, "l0tbl"], writes=[breg], tag=f"{lt}r6{h}")
            for w in range(NQG):
                gs = slice(w * 512, (w + 1) * 512)
                oband = w % 2
                pc_reg = f"PC{oband}"
                kts = []
                if w > 0:
                    kts += [("p", j, (w - 1) * 4 + j) for j in range(4)]
                kts += [("c", j, w * 4 + j) for j in range(4)]
                def f_smm(i, pk, j, kt):
                    bank = i % 2
                    pa_reg = f"PA{bank}"
                    p.emit("te", lambda e, kt=kt, gs=gs, bank=bank: e.matmul(
                        pa[:, bank * 512:bank * 512 + 512], v["KT"][0:65, kt * 128:(kt + 1) * 128],
                        v["QT"][0:65, gs], start=True, stop=True),
                        reads=["l0KT", "l0kones", "l0QT", "l0negc"], writes=[pa_reg], tag=f"{lt}smm{h}")
                    p.emit("act", lambda e, bank=bank: e.activation(
                        out=v["ATB"][:, bank * 512:bank * 512 + 512],
                        in_=pa[:, bank * 512:bank * 512 + 512], func=AF.Exp),
                        reads=[pa_reg], writes=[f"l0atb{bank}"], tag=f"{lt}exp{h}")
                    msk = rmask if pk == "p" else cmask
                    p.emit("dve", lambda e, bank=bank, j=j, msk=msk: e.tensor_mul(
                        v["ATB"][:, bank * 512:bank * 512 + 512],
                        v["ATB"][:, bank * 512:bank * 512 + 512],
                        msk[:, 384 - 128 * j: 896 - 128 * j]),
                        reads=[f"l0atb{bank}", "CMASK", "RMASK"], writes=[f"l0atb{bank}"],
                        tag=f"{lt}msk{h}")

                def f_av(i, kt):
                    bank = i % 2
                    p.emit("te", lambda e, kt=kt, bank=bank, oband=oband, st=(i == 0), sp=(i == len(kts) - 1): e.matmul(
                        pc[0:65, oband * 512:oband * 512 + 512], v["VB"][:, kt * 65:kt * 65 + 65],
                        v["ATB"][:, bank * 512:bank * 512 + 512], start=st, stop=sp),
                        reads=["l0VB", f"l0atb{bank}"], writes=[pc_reg], tag=f"{lt}av{h}")

                for i, (pk, j, kt) in enumerate(kts):
                    f_smm(i, pk, j, kt)
                    if i > 0:
                        f_av(i - 1, kts[i - 1][2])
                f_av(len(kts) - 1, kts[-1][2])
                p.emit("dve", lambda e, oband=oband: e.reciprocal(
                    xf[0:1, 0:512], pc[64:65, oband * 512:oband * 512 + 512]),
                    reads=[pc_reg], writes=["xf0"], tag=f"{lt}rrec{h}")
                p.emit("te", lambda e: e.matmul(
                    pd[0:64, 0:512], onesf[0:1, 0:64], xf[0:1, 0:512], start=True, stop=True),
                    reads=["xf0", "ONESF"], writes=["PD0"], tag=f"{lt}rbc{h}")
                p.emit("dve", lambda e: e.tensor_copy(
                    v["RB"][0:64, 0:512], pd[0:64, 0:512]),
                    reads=["PD0"], writes=["l0rb"], tag=f"{lt}rbev{h}")
                p.emit("dve", lambda e, gs=gs, oband=oband: e.tensor_mul(
                    v["OTB"][0:64, gs], pc[0:64, oband * 512:oband * 512 + 512],
                    v["RB"][0:64, 0:512]),
                    reads=[pc_reg, "l0rb"], writes=["l0OTB"], tag=f"{lt}oev{h}")
            p.dma("sp", "ot", lambda e, h=h: e.dma_start(out=ot[h], in_=v["OTB"][0:64, :]),
                  reads=["l0OTB"], writes=[f"M:l0ot{h}"], tag=f"{lt}otst{h}")

        wof = v["TBL"][:, 0:4096]
        for pr in range(4):
            p.dma("pool", "wof", lambda e, li=li, pr=pr: e.dma_start(
                out=wof[:, pr * 1024:(pr + 1) * 1024],
                in_=hbm["L0_ffn_wo"][li, pr * 128:(pr + 1) * 128, :]),
                reads=["M:L0_ffn_wo"], writes=["l0wof", "l0tbl"], tag=f"{lt}ldwof{pr}")
        for g in range(NQG):
            gs = slice(g * 512, (g + 1) * 512)
            for hh in range(4):
                p.dma("sp", "ots", lambda e, hh=hh, g=g: e.dma_start(
                    out=v["OTB"][:, hh * 512:(hh + 1) * 512],
                    in_=ot[2 * hh:2 * hh + 2, :, g * 512:(g + 1) * 512]),
                    reads=[f"M:l0ot{k}" for k in range(NH)], writes=[f"l0ots{hh}"],
                    tag=f"{lt}ldots{g}_{hh}")
            for oc in range(DC):
                bank = oc % 2
                pa_reg = f"PA{bank}"
                slot = (g * DC + oc) % 4
                xf_reg = f"xf{slot}"
                p.dma("sp", "xf", lambda e, oc=oc, gs=gs, slot=slot: e.dma_start(
                    out=xf[:, slot * 512:(slot + 1) * 512],
                    in_=zm[oc * 128:(oc + 1) * 128, gs.start:gs.stop]),
                    reads=[f"M:zm:{oc}_{g}"], writes=[xf_reg], tag=f"{lt}ldxf")
                for hh in range(4):
                    p.emit("te", lambda e, hh=hh, oc=oc, bank=bank, st=(hh == 0), sp=(hh == 3): e.matmul(
                        pa[:, bank * 512:bank * 512 + 512],
                        wof[:, hh * 1024 + oc * 128: hh * 1024 + (oc + 1) * 128],
                        v["OTB"][:, hh * 512:(hh + 1) * 512], start=st, stop=sp),
                        reads=["l0wof", f"l0ots{hh}"], writes=[pa_reg], tag=f"{lt}womm{oc}")
                p.emit("dve", lambda e, slot=slot, bank=bank: e.tensor_add(
                    xf[:, slot * 512:(slot + 1) * 512], xf[:, slot * 512:(slot + 1) * 512],
                    pa[:, bank * 512:bank * 512 + 512]),
                    reads=[pa_reg, xf_reg], writes=[xf_reg], tag=f"{lt}radd")
                p.dma("sp", "zmo", lambda e, oc=oc, gs=gs, slot=slot: e.dma_start(
                    out=zm[oc * 128:(oc + 1) * 128, gs.start:gs.stop],
                    in_=xf[:, slot * 512:(slot + 1) * 512]),
                    reads=[xf_reg], writes=[f"M:zm:{oc}_{g}"], ckey=f"zmw{li}", tag=f"{lt}stzm")
                p.emit("dve", lambda e, oc=oc, gs=gs, slot=slot: e.tensor_copy(
                    xbf[:, oc * S + gs.start: oc * S + gs.stop], xf[:, slot * 512:(slot + 1) * 512]),
                    reads=[xf_reg], writes=[f"xbf{oc}"], tag=f"{lt}shad")

        if getattr(mk, "l0f_debug", False):
            dbga = mk.dram_out(f"dbga{li}", (D, S))
            for c in range(DC):
                p.dma("sp", f"dbga{li}", lambda e, c=c, dbga=dbga: e.dma_start(
                    out=dbga[c * 128:(c + 1) * 128, :], in_=zm[c * 128:(c + 1) * 128, :]),
                    reads=[f"M:zm:{c}_{gg}" for gg in range(8)], writes=[f"M:dbga{li}"], tag=f"dbga{li}")
        p.barrier()
        sdr = v["SDR"]
        ln_stats(mk, lt + "f", shifted=True, want_sd=sdr)
        if getattr(mk, "l0f_debug", False) and li == 0:
            dbs = mk.dram_out("dbgs", (2, 1, S))
            p.dma("pool", "dbgs", lambda e, dbs=dbs: e.dma_start(
                out=dbs[0], in_=mk.sb["ROW0"][0:1, :]),
                reads=["row0"], writes=["M:dbgs"], tag="dbgs_mu")
            dbx = mk.dram_out("dbgx", (128, 16), BF16)
            p.dma("sp", "dbgx", lambda e, dbx=dbx: e.dma_start(
                out=dbx[:, :], in_=mk.sb["XBF"][:, 4 * S + 504: 4 * S + 520]),
                reads=["xbf4"], writes=["M:dbgx"], tag="dbgx")
            dbsd = mk.dram_out("dbgsd", (1, S), BF16)
            p.dma("sp", "dbgsd", lambda e, dbsd=dbsd: e.dma_start(
                out=dbsd[0:1, :], in_=v["SDR"][0:1, :]),
                reads=["l0sdr"], writes=["M:dbgsd"], tag="dbgsd")
            dbb = mk.dram_out("dbgb", (128, 8))
            p.dma("sp", "dbgb", lambda e, dbb=dbb: e.dma_start(
                out=dbb[:, :], in_=mk.sb["BRS"][:, 509:517]),
                reads=["BRS"], writes=["M:dbgb"], tag="dbgb")
            p.dma("sp", "dbgs", lambda e, dbs=dbs: e.dma_start(
                out=dbs[1], in_=mk.sb["BRS"][0:1, :]),
                reads=["BRS"], writes=["M:dbgs"], tag="dbgs_rs")
        p.barrier()
        p.dma("sp", "lg", lambda e, li=li: e.dma_start(
            out=v["LGC"][:, 0:8], in_=hbm["lg_c"][li]),
            reads=["M:lg_c"], writes=["l0lg"], tag=f"{lt}lg")
        p.dma("pool", "lb", lambda e, li=li: e.dma_start(
            out=v["LBC"][:, 0:8], in_=hbm["lb_c"][li]),
            reads=["M:lb_c"], writes=["l0lb"], tag=f"{lt}lb")
        p.dma("pool", "b1r", lambda e, li=li: e.dma_start(
            out=v["B1R"][0:1, 0:5632], in_=hbm["b1p"][li].rearrange("(a b) -> a b", a=1)),
            reads=["M:b1p"], writes=["l0b1"], tag=f"{lt}b1")
        p.dma("sp", "b2c", lambda e, li=li: e.dma_start(
            out=v["B2C"][:, 0:8], in_=hbm["b2_c"][li]),
            reads=["M:b2_c"], writes=["l0b2c"], tag=f"{lt}b2")
        NJ = 22
        for tg in range(NQG):
            gs = slice(tg * 512, (tg + 1) * 512)
            for j in range(NJ):
                wbuf = j % 2
                wreg = f"l0w1_{wbuf}"
                for c in range(DC):
                    p.dma("sp", "w1", lambda e, li=li, j=j, c=c, wbuf=wbuf: e.dma_start(
                        out=v["W1SL"][:, wbuf * 2048 + c * 256: wbuf * 2048 + (c + 1) * 256],
                        in_=hbm["w1b"][li, j, c * 128:(c + 1) * 128, :]),
                        reads=["M:w1b"], writes=[f"{wreg}_{c}"], ckey=wreg, tag=f"{lt}ldw1")
                for c in range(DC):
                    p.emit("te", lambda e, c=c, wbuf=wbuf: e.matmul(
                        pd[0:1, 0:256], v["LBC"][:, c:c + 1],
                        v["W1SL"][:, wbuf * 2048 + c * 256: wbuf * 2048 + (c + 1) * 256],
                        start=(c == 0), stop=(c == DC - 1)),
                        reads=[f"{wreg}_{c}", "l0lb"], writes=["PD0"], tag=f"{lt}lbw1")
                p.emit("dve", lambda e, j=j: e.scalar_tensor_tensor(
                    v["CS1"][0:1, 0:256], pd[0:1, 0:256], 1.0, v["B1R"][0:1, j * 256:(j + 1) * 256],
                    ALU.mult, ALU.add),
                    reads=["PD0", "l0b1"], writes=["l0cs1"], tag=f"{lt}seedrow")
                for c in range(DC):
                    p.emit("dve", lambda e, c=c, wbuf=wbuf: e.tensor_scalar(
                        v["W1SL"][:, wbuf * 2048 + c * 256: wbuf * 2048 + (c + 1) * 256],
                        v["W1SL"][:, wbuf * 2048 + c * 256: wbuf * 2048 + (c + 1) * 256],
                        v["LGC"][:, c:c + 1], None, ALU.mult),
                        reads=[f"{wreg}_{c}", "l0lg"], writes=[f"{wreg}_{c}"], tag=f"{lt}w1lg")
                for c in range(DC):
                    p.emit("te", lambda e, c=c, wbuf=wbuf: e.matmul(
                        pd[0:1, 256:512], onesc[:, :],
                        v["W1SL"][:, wbuf * 2048 + c * 256: wbuf * 2048 + (c + 1) * 256],
                        start=(c == 0), stop=(c == DC - 1)),
                        reads=[f"{wreg}_{c}", "ONESC"], writes=["PD0"], tag=f"{lt}cs1")
                p.emit("act", lambda e: e.activation(out=v["NC1"][0:1, 0:256], in_=pd[0:1, 256:512],
                                                     func=AF.Identity, scale=-1.0),
                       reads=["PD0"], writes=["l0nc1"], tag=f"{lt}negcs")
                bank = j % 2
                for half, psum_t, preg in ((0, pa, f"PA{bank}"), (1, pb, f"PB{bank}")):
                    p.emit("te", lambda e, half=half, gs=gs, bank=bank, psum_t=psum_t: e.matmul(
                        psum_t[:, bank * 512:bank * 512 + 512],
                        v["NC1"][0:1, half * 128:(half + 1) * 128], row0[0:1, gs],
                        start=True, stop=False),
                        reads=["l0nc1", "row0"], writes=[preg], tag=f"{lt}seed1")
                    p.emit("te", lambda e, half=half, gs=gs, bank=bank, psum_t=psum_t: e.matmul(
                        psum_t[:, bank * 512:bank * 512 + 512],
                        v["CS1"][0:1, half * 128:(half + 1) * 128], sdr[0:1, gs],
                        start=False, stop=False),
                        reads=["l0cs1", "l0sdr"], writes=[preg], tag=f"{lt}seed2")
                    for c in range(DC):
                        rhs, ofs = _l0f_shift_rhs(xbf, c, gs.start, gs.stop)
                        p.emit("te", lambda e, rhs=rhs, ofs=ofs, c=c, half=half, bank=bank, psum_t=psum_t, wbuf=wbuf: e.matmul(
                            psum_t[:, bank * 512 + ofs: bank * 512 + 512],
                            v["W1SL"][:, wbuf * 2048 + c * 256 + half * 128: wbuf * 2048 + c * 256 + half * 128 + 128],
                            rhs, start=False, stop=(c == DC - 1)),
                            reads=[f"{wreg}_{c}", f"xbf{c}"], writes=[preg], tag=f"{lt}h3mm")
                if getattr(mk, "l0f_debug", False) and li == 0 and tg == 1 and j == 0:
                    dbp = mk.dram_out("dbgp", (128, 512))
                    p.emit("dve", lambda e, bank=bank: e.tensor_copy(
                        v["XNAT" if False else "GELB"][:, 0:512], pa[:, bank * 512:bank * 512 + 512]),
                        reads=[f"PA{bank}"], writes=["l0gelb"], tag="dbgp_ev")
                    p.dma("pool", "dbgp", lambda e, dbp=dbp: e.dma_start(
                        out=dbp[:, :], in_=v["GELB"][:, 0:512].bitcast(F32)[:, 0:256] if False else v["GELB"][:, 0:512]),
                        reads=["l0gelb"], writes=["M:dbgp"], tag="dbgp")
                p.emit("dve", lambda e, gs=gs, bank=bank: e.tensor_mul(
                    v["AB"][:, 0:512], pa[:, bank * 512:bank * 512 + 512], brs[:, gs]),
                    reads=[f"PA{bank}", "BRS"], writes=["l0ab"], tag=f"{lt}aev")
                p.emit("dve", lambda e, gs=gs, bank=bank: e.tensor_mul(
                    v["GB"][:, 0:512], pb[:, bank * 512:bank * 512 + 512], brs[:, gs]),
                    reads=[f"PB{bank}", "BRS"], writes=["l0gb"], tag=f"{lt}gev")
                p.emit("act", lambda e: e.activation(out=v["GELB"][:, 0:512], in_=v["GB"][:, 0:512],
                                                     func=AF.Gelu),
                       reads=["l0gb"], writes=["l0gelb"], tag=f"{lt}gelu")
                p.emit("dve", lambda e, j=j: e.tensor_mul(
                    v["H4G"][:, j * 512:(j + 1) * 512], v["AB"][:, 0:512], v["GELB"][:, 0:512]),
                    reads=["l0ab", "l0gelb"], writes=["l0h4g"], tag=f"{lt}h4")
                if getattr(mk, "l0f_debug", False) and li == 0 and tg == 1 and j == 0:
                    dbh = mk.dram_out("dbgh", (2, 128, 512), BF16)
                    p.dma("sp", "dbgh", lambda e, dbh=dbh: e.dma_start(
                        out=dbh[0], in_=v["AB"][:, 0:512]),
                        reads=["l0ab"], writes=["M:dbgh"], tag="dbgh_a")
                    p.dma("sp", "dbgh", lambda e, dbh=dbh: e.dma_start(
                        out=dbh[1], in_=v["GB"][:, 0:512]),
                        reads=["l0gb"], writes=["M:dbgh"], tag="dbgh_g")
            for oh in range(2):
                pbanks = [("PC", 0), ("PC", 1), ("PD", 0), ("PA", 0)]
                for j in range(NJ):
                    wbuf = j % 2
                    w2reg = f"l0w2_{wbuf}"
                    p.dma("sp", "w2", lambda e, li=li, j=j, oh=oh, wbuf=wbuf: e.dma_start(
                        out=v["W2SL"][:, wbuf * 512:(wbuf + 1) * 512],
                        in_=hbm["w2b"][li, j, :, oh * 512:(oh + 1) * 512]),
                        reads=["M:w2b"], writes=[w2reg], tag=f"{lt}ldw2")
                    for q4 in range(4):
                        pn, pk = pbanks[q4]
                        psum_t = mk.ps[pn]
                        p.emit("te", lambda e, j=j, q4=q4, wbuf=wbuf, psum_t=psum_t, pk=pk: e.matmul(
                            psum_t[:, pk * 512:pk * 512 + 512],
                            v["W2SL"][:, wbuf * 512 + q4 * 128: wbuf * 512 + (q4 + 1) * 128],
                            v["H4G"][:, j * 512:(j + 1) * 512],
                            start=(j == 0), stop=(j == NJ - 1)),
                            reads=[w2reg, "l0h4g"], writes=[f"{pn}{pk}"], tag=f"{lt}w2mm")
                for q4 in range(4):
                    oc = oh * 4 + q4
                    pn, pk = pbanks[q4]
                    psum_t = mk.ps[pn]
                    slot = q4
                    xf_reg = f"xf{slot}"
                    p.dma("sp", "xf", lambda e, oc=oc, gs=gs, slot=slot: e.dma_start(
                        out=xf[:, slot * 512:(slot + 1) * 512],
                        in_=zm[oc * 128:(oc + 1) * 128, gs.start:gs.stop]),
                        reads=[f"M:zm:{oc}_{tg}"], writes=[xf_reg], tag=f"{lt}f_ldxf")
                    p.emit("dve", lambda e, slot=slot, psum_t=psum_t, pk=pk: e.tensor_add(
                        xf[:, slot * 512:(slot + 1) * 512], xf[:, slot * 512:(slot + 1) * 512],
                        psum_t[:, pk * 512:pk * 512 + 512]),
                        reads=[f"{pn}{pk}", xf_reg], writes=[xf_reg], tag=f"{lt}f_radd")
                    p.emit("dve", lambda e, slot=slot, oc=oc: e.tensor_scalar(
                        xf[:, slot * 512:(slot + 1) * 512], xf[:, slot * 512:(slot + 1) * 512],
                        v["B2C"][:, oc:oc + 1], None, ALU.add),
                        reads=[xf_reg, "l0b2c"], writes=[xf_reg], tag=f"{lt}f_b2")
                    p.dma("sp", "zmo", lambda e, oc=oc, gs=gs, slot=slot: e.dma_start(
                        out=zm[oc * 128:(oc + 1) * 128, gs.start:gs.stop],
                        in_=xf[:, slot * 512:(slot + 1) * 512]),
                        reads=[xf_reg], writes=[f"M:zm:{oc}_{tg}"], ckey=f"zmf{li}", tag=f"{lt}f_stzm")
        for c in range(DC):
            p.dma("pool", "xbf", lambda e, c=c: e.dma_start(
                out=xbf[:, c * S:(c + 1) * S], in_=zm[c * 128:(c + 1) * 128, :]),
                reads=[f"M:zm:{c}_{gg}" for gg in range(8)], writes=[f"xbf{c}"], tag=f"{lt}f_shadr{c}")
        if getattr(mk, "l0f_debug", False):
            dbgf = mk.dram_out(f"dbgf{li}", (D, S))
            for c in range(DC):
                p.dma("sp", f"dbgf{li}", lambda e, c=c, dbgf=dbgf: e.dma_start(
                    out=dbgf[c * 128:(c + 1) * 128, :], in_=zm[c * 128:(c + 1) * 128, :]),
                    reads=[f"M:zm:{c}_{gg}" for gg in range(8)], writes=[f"M:dbgf{li}"], tag=f"dbgf{li}")
        p.barrier()

    for oc in range(DC):
        for g in range(NQG):
            gs = slice(g * 512, (g + 1) * 512)
            slot = (oc * NQG + g) % 2
            slot2 = 2 + slot
            xf_reg, xf2_reg = f"xf{slot}", f"xf{slot2}"
            p.dma("sp", "xf", lambda e, oc=oc, gs=gs, slot=slot: e.dma_start(
                out=xf[:, slot * 512:(slot + 1) * 512],
                in_=hbm[master_in][oc * 128:(oc + 1) * 128, gs.start:gs.stop]),
                reads=[f"M:{master_in}"], writes=[xf_reg], tag=f"{tag}fin_ldx")
            p.dma("sp", "xf2", lambda e, oc=oc, gs=gs, slot2=slot2: e.dma_start(
                out=xf[:, slot2 * 512:(slot2 + 1) * 512],
                in_=zm[oc * 128:(oc + 1) * 128, gs.start:gs.stop]),
                reads=[f"M:zm:{oc}_{g}"], writes=[xf2_reg], tag=f"{tag}fin_ldz")
            p.emit("dve", lambda e, slot=slot, slot2=slot2: e.tensor_add(
                xf[:, slot * 512:(slot + 1) * 512], xf[:, slot * 512:(slot + 1) * 512],
                xf[:, slot2 * 512:(slot2 + 1) * 512]),
                reads=[xf_reg, xf2_reg], writes=[xf_reg], tag=f"{tag}fin_add")
            p.dma("sp", "xfo", lambda e, oc=oc, gs=gs, slot=slot: e.dma_start(
                out=hbm[master_out][oc * 128:(oc + 1) * 128, gs.start:gs.stop],
                in_=xf[:, slot * 512:(slot + 1) * 512]),
                reads=[xf_reg], writes=[f"M:{master_out}:f{oc}_{g}"], ckey="xfo", tag=f"{tag}fin_st")
            p.emit("dve", lambda e, oc=oc, gs=gs, slot=slot: e.tensor_copy(
                xbf[:, oc * S + gs.start: oc * S + gs.stop], xf[:, slot * 512:(slot + 1) * 512]),
                reads=[xf_reg], writes=[f"xbf{oc}"], tag=f"{tag}fin_shad")


def l0f_tables():
    hd = 64
    half = np.arange(0, hd, 2, dtype=np.float64)
    inv = 1.0 / 10000.0 ** (half / hd)
    t = np.arange(S, dtype=np.float64)
    freqs = t[None, :] * inv[:, None]
    cos = np.cos(freqs)
    sin = np.sin(freqs)
    base = (half + 0.4 * hd) / (1.4 * hd)
    power = (t - S // 2) / (512 // 2)
    scale = base[:, None] ** power[None, :]
    cosl = np.concatenate([cos, cos], 0).astype(np.float32)
    sinls = np.concatenate([-sin, sin], 0).astype(np.float32)
    xs = np.concatenate([scale, scale], 0).astype(np.float32)
    xsi = np.concatenate([1.0 / scale, 1.0 / scale], 0).astype(np.float32)
    return {"c_l0f_cos": cosl, "c_l0f_ssin": sinls, "c_l0f_xs": xs, "c_l0f_xsi": xsi}


def l0f_precast(mk):
    p = mk.p
    hbm = mk.hbm
    xf = mk.sb["XF"]
    ar = mk.sb["AR"]
    stage = ar[:, 0:2048]
    w1p4 = hbm["w1p"].rearrange("l j (r p) o -> (l j r) p o", p=128)
    w1b4 = hbm["w1b"].rearrange("l j (r p) o -> (l j r) p o", p=128)
    for i in range(2 * 22 * 8):
        sl = i % 4
        p.dma("sp", "pc1", lambda e, i=i, sl=sl: e.dma_start(
            out=xf[:, sl * 512:sl * 512 + 256], in_=w1p4[i]),
            reads=["M:w1p"], writes=[f"xf{sl}"], tag="pc1l")
        p.emit("dve", lambda e, i=i, sl=sl: e.tensor_copy(
            stage[:, (i % 4) * 512: (i % 4) * 512 + 256], xf[:, sl * 512:sl * 512 + 256]),
            reads=[f"xf{sl}"], writes=[f"l0pc{i % 4}"], tag="pc1c")
        p.dma("sp", "pc1o", lambda e, i=i: e.dma_start(
            out=w1b4[i], in_=stage[:, (i % 4) * 512: (i % 4) * 512 + 256]),
            reads=[f"l0pc{i % 4}"], writes=[f"M:w1b:{i}"], ckey="pc1o", tag="pc1s")
    w2p4 = hbm["w2p"].rearrange("l j p o -> (l j) p o")
    w2b4 = hbm["w2b"].rearrange("l j p o -> (l j) p o")
    for i in range(44):
        sl = i % 2
        p.dma("sp", "pc2", lambda e, i=i, sl=sl: e.dma_start(
            out=xf[:, sl * 1024:(sl + 1) * 1024], in_=w2p4[i]),
            reads=["M:w2p"], writes=[f"xf{2 * sl}", f"xf{2 * sl + 1}"], tag="pc2l")
        p.emit("dve", lambda e, i=i, sl=sl: e.tensor_copy(
            stage[:, sl * 1024:(sl + 1) * 1024],
            xf[:, sl * 1024:(sl + 1) * 1024]),
            reads=[f"xf{2 * sl}", f"xf{2 * sl + 1}"], writes=[f"l0pc{sl}"], tag="pc2c")
        p.dma("sp", "pc2o", lambda e, i=i, sl=sl: e.dma_start(
            out=w2b4[i], in_=stage[:, sl * 1024:(sl + 1) * 1024]),
            reads=[f"l0pc{sl}"], writes=[f"M:w2b:{i}"], ckey="pc2o", tag="pc2s")
    p.barrier()
